# version 28
# baseline (speedup 1.0000x reference)
from contextlib import ExitStack

import numpy as np
import ml_dtypes
import concourse.bass as bass
import concourse.mybir as mybir
from concourse.bass_utils import run_bass_kernel_spmd

F32 = mybir.dt.float32
BF16 = mybir.dt.bfloat16
AF = mybir.ActivationFunctionType
ALU = mybir.AluOpType
AX = mybir.AxisListType

D = 1024
H = 16
DK = 64
DV = 128
C = 64
DFF = 4096
NE = 8
NCORE = 8
T = 16384

GAMMA = 1.0 - np.exp2(-5.0 - np.arange(H, dtype=np.float64))

DEBUG_OUTS = False
STAGE = 99


class CFG:
    def __init__(self, tok=T // NCORE, pref=T - T // NCORE, pgrp=8):
        assert tok % 512 == 0 and (pref % 128 == 0)
        self.tok = tok
        self.pref = pref
        self.ntau = tok // 128
        self.nptau = pref // 128
        self.nchunk = tok // C
        self.pgrp = min(pgrp, self.nptau) if self.nptau else 1
        assert self.nptau == 0 or self.nptau % self.pgrp == 0
        self.ngrp = (self.nptau // self.pgrp) if self.nptau else 0
        self.ntg = self.ntau // 4


class _Rec:
    __slots__ = ("eng", "kind", "fn", "idx", "tagged", "val", "slot")

    def __init__(self, eng, kind, fn, idx):
        self.eng, self.kind, self.fn, self.idx = eng, kind, fn, idx
        self.tagged = False
        self.val = 0
        self.slot = -1


class Buf:
    __slots__ = ("name", "wr", "rd")

    def __init__(self, name=""):
        self.name = name
        self.wr = None
        self.rd = []


NDMA = 6


class KB:
    ENGS = ("sync", "gpsimd", "tensor", "vector", "scalar")

    def __init__(self, nc, ctx):
        self.nc = nc
        self.stream = {e: [] for e in self.ENGS}
        self.nrec = {e: 0 for e in self.ENGS}
        self.waited = {e: {} for e in self.ENGS}
        self.sems = {}
        for e in self.ENGS:
            self.sems[(e, "c")] = ctx.enter_context(nc.semaphore(f"s_{e}_c"))
            for i in range(NDMA):
                self.sems[(e, "d", i)] = ctx.enter_context(nc.semaphore(f"s_{e}_d{i}"))
        self.dma_rr = {e: 0 for e in self.ENGS}
        self.dma_uses = {e: [0] * NDMA for e in self.ENGS}
        self.dma_last = {e: [None] * NDMA for e in self.ENGS}

    def _key(self, rec):
        return (rec.eng, "c") if rec.kind == "c" else (rec.eng, "d", rec.slot)

    def _wait(self, eng, dep: _Rec):
        k = self._key(dep)
        w = self.waited[eng]
        if w.get(k, -1) >= dep.idx:
            return
        w[k] = dep.idx
        dep.tagged = True
        self.stream[eng].append(("wait", dep))

    def emit(self, eng, fn, reads=(), writes=(), kind="c"):
        if kind == "d":
            slot = self.dma_rr[eng] % NDMA
            self.dma_rr[eng] += 1
            prev = self.dma_last[eng][slot]
            if prev is not None:
                self._wait(eng, prev)
        deps = {}
        def add(r):
            k = self._key(r)
            if k not in deps or deps[k].idx < r.idx:
                deps[k] = r
        for b in reads:
            if b.wr is not None:
                add(b.wr)
        for b in writes:
            if b.wr is not None:
                add(b.wr)
            for r in b.rd:
                add(r)
        for r in deps.values():
            self._wait(eng, r)
        rec = _Rec(eng, kind, fn, self.nrec[eng])
        if kind == "d":
            rec.tagged = True
            rec.slot = slot
            self.dma_uses[eng][slot] += 1
            rec.val = self.dma_uses[eng][slot]
            self.dma_last[eng][slot] = rec
        self.nrec[eng] += 1
        self.stream[eng].append(rec)
        for b in reads:
            b.rd.append(rec)
        for b in writes:
            b.wr = rec
            b.rd = []
        return rec

    def barrier(self):
        lasts = []
        for e in self.ENGS:
            seen_c = False
            seen_d = set()
            for item in reversed(self.stream[e]):
                if isinstance(item, _Rec):
                    if item.kind == "c" and not seen_c:
                        lasts.append(item); seen_c = True
                    elif item.kind == "d" and item.slot not in seen_d:
                        lasts.append(item); seen_d.add(item.slot)
                    if seen_c and len(seen_d) == NDMA:
                        break
        for e in self.ENGS:
            for rec in lasts:
                if rec.eng != e:
                    self._wait(e, rec)

    def finalize(self, block):
        for e in self.ENGS:
            cnt = 0
            for item in self.stream[e]:
                if isinstance(item, _Rec) and item.kind == "c" and item.tagged:
                    cnt += 1
                    item.val = cnt

        def mk(e):
            def run(eng_obj):
                for item in self.stream[e]:
                    if isinstance(item, _Rec):
                        ins = item.fn(eng_obj)
                        if item.tagged:
                            if item.kind == "d":
                                ins.then_inc(self.sems[(e, "d", item.slot)], 16)
                            else:
                                ins.then_inc(self.sems[(e, "c")], 1)
                    else:
                        _, dep = item
                        if dep.kind == "d":
                            eng_obj.wait_ge(self.sems[(dep.eng, "d", dep.slot)], dep.val * 16)
                        else:
                            eng_obj.wait_ge(self.sems[(dep.eng, "c")], dep.val)
            return run

        block.sync(mk("sync"))
        block.gpsimd(mk("gpsimd"))
        block.tensor(mk("tensor"))
        block.vector(mk("vector"))
        block.scalar(mk("scalar"))


def _host_constants():
    c = {}
    c["ident"] = np.eye(128, dtype=ml_dtypes.bfloat16)
    inv = 1.0 / 10000.0 ** (np.arange(0, DK, 2, dtype=np.float64) / DK)
    t_all = np.arange(T, dtype=np.float64)
    frq = t_all[:, None] * inv[None, :]
    c["_cos_all"] = np.cos(frq)
    c["_sin_all"] = np.sin(frq)

    i64 = np.arange(C, dtype=np.float64)
    ipos = np.concatenate([i64, i64])
    c["dq128"] = (GAMMA[None, :] ** (ipos[:, None] + 1.0) * (DK ** -0.5)).astype(np.float32)
    c["dkv128"] = (GAMMA[None, :] ** (C - 1.0 - ipos[:, None])).astype(np.float32)

    dm2 = np.zeros((128, H * 64), dtype=np.float32)
    for h in range(H):
        g = GAMMA[h]
        colj = g ** (-1.0 - i64)
        m = np.where(i64[None, :] >= i64[:, None], colj[:, None], 0.0)
        dm2[0:64, h * 64:(h + 1) * 64] = m.astype(np.float32)
        dm2[64:128, h * 64:(h + 1) * 64] = m.astype(np.float32)
    c["dmat2"] = dm2
    c["lamH"] = np.broadcast_to((GAMMA ** C).astype(np.float32)[None, :], (128, H)).copy()
    return c


_CONSTS = _host_constants()


def _per_core_inputs(core, cfg: CFG, full):
    tok, pref = cfg.tok, cfg.pref
    t0 = core * tok
    x = np.ascontiguousarray(np.asarray(full["L0_attn_x"], dtype=np.float32).reshape(-1, D))
    d = {}
    d["x_loc"] = np.ascontiguousarray(x[t0:t0 + tok])
    xp = np.zeros((pref, D), dtype=np.float32)
    nreal = min(t0, pref)
    if nreal > 0:
        xp[pref - nreal:] = x[t0 - nreal:t0]
    d["x_pref"] = xp

    def tab(a, n):
        return np.ascontiguousarray(
            a.reshape(n, 128, 32).transpose(1, 0, 2).reshape(128, n * 32).astype(np.float32))

    cos_all, sin_all = _CONSTS["_cos_all"], _CONSTS["_sin_all"]
    d["cos_loc"] = tab(cos_all[t0:t0 + tok], cfg.ntau)
    d["sin_loc"] = tab(sin_all[t0:t0 + tok], cfg.ntau)
    if cfg.nptau:
        cp = np.zeros((pref, DK // 2), dtype=np.float64)
        sp = np.zeros((pref, DK // 2), dtype=np.float64)
        if nreal > 0:
            cp[pref - nreal:] = cos_all[t0 - nreal:t0]
            sp[pref - nreal:] = sin_all[t0 - nreal:t0]
        d["cos_pref"] = tab(cp, cfg.nptau)
        d["sin_pref"] = tab(sp, cfg.nptau)
    else:
        d["cos_pref"] = np.zeros((128, 32), np.float32)
        d["sin_pref"] = np.zeros((128, 32), np.float32)

    gt = cfg.pgrp * 128
    u = np.arange(gt, dtype=np.float64)
    w = GAMMA[None, :] ** (gt - 1.0 - u[:, None])
    d["wpref"] = np.ascontiguousarray(
        w.reshape(cfg.pgrp, 128, H).transpose(1, 0, 2).reshape(128, cfg.pgrp * H).astype(np.float32))
    d["lamgH"] = np.broadcast_to((GAMMA ** gt).astype(np.float32)[None, :], (128, H)).copy()

    for k in ("ident", "dq128", "dkv128", "dmat2", "lamH"):
        d[k] = _CONSTS[k]
    d["dmat2"] = _CONSTS["dmat2"].astype(ml_dtypes.bfloat16)
    for k in ("L0_attn_Wq", "L0_attn_Wk", "L0_attn_Wv", "L0_attn_Wg", "L0_attn_Wo"):
        d[k] = np.ascontiguousarray(np.asarray(full[k], dtype=np.float32))
    gn = np.asarray(full["L0_attn_g_norm_w"], dtype=np.float32)
    d["gnwd"] = np.ascontiguousarray(np.broadcast_to(gn[None, :], (128, DV)).copy())
    d["gncol"] = np.ascontiguousarray(gn.reshape(128, 1))
    d["gate_wT"] = np.ascontiguousarray(np.asarray(full["L0_ffn_gate_w"], dtype=np.float32).T)
    d["gate_b"] = np.ascontiguousarray(np.asarray(full["L0_ffn_gate_b"], dtype=np.float32).reshape(1, NE))
    d["w1"] = np.ascontiguousarray(np.asarray(full["L0_ffn_w1"], dtype=np.float32))
    d["w2"] = np.ascontiguousarray(np.asarray(full["L0_ffn_w2"], dtype=np.float32))
    b1 = np.asarray(full["L0_ffn_b1"], dtype=np.float32)
    d["b1L"] = np.ascontiguousarray(b1.reshape(NE, 32, 128).transpose(2, 0, 1).reshape(128, NE * 32))
    d["b2"] = np.ascontiguousarray(np.asarray(full["L0_ffn_b2"], dtype=np.float32))
    d["onesb"] = np.ones((1, 128), dtype=ml_dtypes.bfloat16)
    return d


def build_nc(cfg: CFG):
    nc = bass.Bass("TRN2", target_bir_lowering=False)
    tok, pref = cfg.tok, cfg.pref

    ap = {}

    def din(name, shape, dt=F32):
        ap[name] = nc.dram_tensor(name, list(shape), dt, kind="ExternalInput").ap()

    din("x_loc", (tok, D)); din("x_pref", (pref, D))
    din("cos_loc", (128, cfg.ntau * 32)); din("sin_loc", (128, cfg.ntau * 32))
    din("cos_pref", (128, max(cfg.nptau, 1) * 32)); din("sin_pref", (128, max(cfg.nptau, 1) * 32))
    din("wpref", (128, cfg.pgrp * H)); din("lamgH", (128, H))
    din("ident", (128, 128), BF16)
    din("dq128", (128, H)); din("dkv128", (128, H)); din("dmat2", (128, H * 64), BF16)
    din("lamH", (128, H))
    din("L0_attn_Wq", (D, D)); din("L0_attn_Wk", (D, D))
    din("L0_attn_Wv", (D, 2 * D)); din("L0_attn_Wg", (D, 2 * D)); din("L0_attn_Wo", (2 * D, D))
    din("gnwd", (128, DV)); din("gncol", (128, 1))
    din("gate_wT", (D, NE)); din("gate_b", (1, NE))
    din("w1", (NE, D, DFF)); din("w2", (NE, DFF, D))
    din("b1L", (128, NE * 32)); din("b2", (NE, D))
    din("onesb", (1, 128), BF16)

    out = nc.dram_tensor("out", [tok, D], F32, kind="ExternalOutput").ap()
    okind = "ExternalOutput" if DEBUG_OUTS else "Internal"
    q_s = nc.dram_tensor("q_s", [tok, D], BF16, kind=okind).ap()
    k_s = nc.dram_tensor("k_s", [tok, D], BF16, kind=okind).ap()
    v_s = nc.dram_tensor("v_s", [tok, 2 * D], BF16, kind=okind).ap()
    g_s = nc.dram_tensor("g_s", [tok, 2 * D], F32, kind=okind).ap()
    s_dbg = nc.dram_tensor("s_dbg", [64, H * DV], F32, kind=okind).ap()
    acc_dbg = nc.dram_tensor("acc_dbg", [128, cfg.ntau * D], F32, kind=okind).ap()

    b_qs, b_ks, b_vs, b_gs = Buf("q_s"), Buf("k_s"), Buf("v_s"), Buf("g_s")

    with ExitStack() as ctx:
        e = ctx.enter_context
        kb = KB(nc, ctx)
        E = kb.emit

        def dma(eng, dst, src, bufs_w=(), bufs_r=()):
            return E(eng, lambda X, d=dst, s=src: X.dma_start(out=d, in_=s),
                     reads=list(bufs_r), writes=list(bufs_w), kind="d")

        ident = e(nc.sbuf_tensor("identS", [128, 128], BF16)); b_ident = Buf("ident")
        Sf = e(nc.sbuf_tensor("SfS", [128, H * DV], F32)); b_Sf = Buf("Sf")
        We = e(nc.sbuf_tensor("WeS", [128, cfg.ntau * NE], F32)); b_We = Buf()
        cst = e(nc.sbuf_tensor("cstS", [128, 4], F32)); b_cst = Buf("cst")
        dma("sync", ident[:], ap["ident"], [b_ident])
        E("gpsimd", lambda X: X.memset(cst[:, 0:1], 1e-5), writes=[b_cst])
        E("gpsimd", lambda X: X.memset(cst[:, 1:2], 1.0 / DV), writes=[b_cst])
        cEPS = cst[:, 0:1]

        with ExitStack() as p0:
            e0 = p0.enter_context
            cosL = e0(nc.sbuf_tensor("cosLS", [128, cfg.ntau * 32], F32)); b_cosL = Buf()
            sinL = e0(nc.sbuf_tensor("sinLS", [128, cfg.ntau * 32], F32)); b_sinL = Buf()
            wq = e0(nc.sbuf_tensor("wqS", [128, 8 * D], BF16)); b_wq = Buf()
            wk = e0(nc.sbuf_tensor("wkS", [128, 8 * D], BF16)); b_wk = Buf()
            wv = e0(nc.sbuf_tensor("wvS", [128, 8 * 2 * D], BF16)); b_wv = Buf()
            wg = e0(nc.sbuf_tensor("wgS", [128, 8 * 2 * D], BF16)); b_wg = Buf()
            cosP = [e0(nc.sbuf_tensor(f"cosPS{i}", [128, cfg.pgrp * 32], F32)) for i in range(2)]
            sinP = [e0(nc.sbuf_tensor(f"sinPS{i}", [128, cfg.pgrp * 32], F32)) for i in range(2)]
            b_cosP = [Buf() for _ in range(2)]
            b_sinP = [Buf() for _ in range(2)]
            wpref = e0(nc.sbuf_tensor("wprefS", [128, cfg.pgrp * H], F32)); b_wpref = Buf()
            lamgH = e0(nc.sbuf_tensor("lamgHS", [128, H], F32)); b_lamg = Buf()

            dma("sync", cosL[:], ap["cos_loc"], [b_cosL])
            dma("sync", sinL[:], ap["sin_loc"], [b_sinL])

            dma("sync", wpref[:], ap["wpref"], [b_wpref])
            dma("sync", lamgH[:], ap["lamgH"], [b_lamg])
            for dst, bf, nm in ((wk, b_wk, "L0_attn_Wk"), (wv, b_wv, "L0_attn_Wv"),
                                (wq, b_wq, "L0_attn_Wq"), (wg, b_wg, "L0_attn_Wg")):
                dma("gpsimd", dst[:].rearrange("p (a n) -> p a n", a=8),
                    ap[nm].rearrange("(a p) n -> p a n", p=128), [bf])

            NB = 2
            xt = [e0(nc.sbuf_tensor(f"xt{i}", [128, D], F32)) for i in range(NB)]
            b_xt = [Buf(f"xt{i}") for i in range(NB)]
            lnx = [e0(nc.sbuf_tensor(f"lnx{i}", [128, D], BF16)) for i in range(NB)]
            b_lnx = [Buf() for _ in range(NB)]
            lnxT = [e0(nc.sbuf_tensor(f"lnxT{i}", [128, D], BF16)) for i in range(NB)]
            b_lnxT = [Buf() for _ in range(NB)]

            xsw = [e0(nc.sbuf_tensor(f"xsw{i}", [128, D], F32)) for i in range(2)]
            b_xsw = [Buf() for _ in range(2)]
            kf_one = e0(nc.sbuf_tensor("kfS", [128, D], F32)); b_kf_one = Buf()
            kf = [kf_one, kf_one]; b_kf = [b_kf_one, b_kf_one]
            qf, b_qf = kf, b_kf
            qro = [e0(nc.sbuf_tensor(f"qro{i}", [128, D], BF16)) for i in range(NB)]
            b_qro = [Buf() for _ in range(NB)]
            kro = [e0(nc.sbuf_tensor(f"kro{i}", [128, D], BF16)) for i in range(NB)]
            b_kro = [Buf() for _ in range(NB)]
            rt1 = [e0(nc.sbuf_tensor(f"rt1S{i}", [128, D], F32)) for i in range(NB)]
            rt2 = [e0(nc.sbuf_tensor(f"rt2S{i}", [128, D], F32)) for i in range(NB)]
            b_rt1 = [Buf() for _ in range(NB)]
            b_rt2 = [Buf() for _ in range(NB)]
            vbf = [e0(nc.sbuf_tensor(f"vbf{i}", [128, 2 * D], BF16)) for i in range(NB)]
            b_vbf = [Buf() for _ in range(NB)]
            gf = [e0(nc.sbuf_tensor(f"gf{i}", [128, 2 * D], F32)) for i in range(NB)]
            b_gf = [Buf() for _ in range(NB)]
            kdw = [e0(nc.sbuf_tensor(f"kdw{i}", [128, D], BF16)) for i in range(NB)]
            b_kdw = [Buf() for _ in range(NB)]

            pT2 = [e0(nc.psum_tensor(f"pT{i}", [128, 1024], BF16)) for i in range(2)]
            b_pT2 = [Buf("pT0"), Buf("pT1")]
            pT_ctr = [0]
            pP = e0(nc.psum_tensor("pP", [128, 1024], F32))
            b_pP = [Buf("pP0"), Buf("pP1")]
            qbank = [0]

            def next_bank():
                qb = qbank[0] % 2
                qbank[0] += 1
                return qb

            def make_sweep(sched):
                NS = len(sched)
                lnsm = e0(nc.sbuf_tensor("lnsm", [128, NS], F32))
                lnsq = e0(nc.sbuf_tensor("lnsq", [128, NS], F32))
                lnmn = e0(nc.sbuf_tensor("lnmn", [128, NS], F32))
                lnmg = e0(nc.sbuf_tensor("lnmg", [128, NS], F32))
                lnvr = e0(nc.sbuf_tensor("lnvr", [128, NS], F32))
                lnrs = e0(nc.sbuf_tensor("lnrs", [128, NS], F32))
                lnnb = e0(nc.sbuf_tensor("lnnb", [128, NS], F32))
                ngrp16 = (NS + 15) // 16
                b_g16 = [Buf(f"lng{g}") for g in range(ngrp16)]

                def sweep(slot):
                    kind, tau = sched[slot]
                    b = slot % 2
                    src = ap["x_pref"] if kind == "P" else ap["x_loc"]
                    dma("sync", xsw[b][:], src[tau * 128:(tau + 1) * 128, :], [b_xsw[b]])
                    g = slot // 16
                    E("scalar", lambda X: X.activation(xsw[b][:], xsw[b][:], AF.Identity,
                                                       accum_out=lnsm[:, slot:slot + 1]),
                      reads=[b_xsw[b]], writes=[b_xsw[b], b_g16[g]])
                    E("scalar", lambda X: X.activation(xsw[b][:], xsw[b][:], AF.Square,
                                                       accum_out=lnsq[:, slot:slot + 1]),
                      reads=[b_xsw[b]], writes=[b_xsw[b], b_g16[g]])
                    if slot == min((g + 1) * 16, NS) - 1:
                        c = slice(g * 16, min((g + 1) * 16, NS))
                        E("vector", lambda X: (X.tensor_scalar_mul(lnmn[:, c], lnsm[:, c], 1.0 / D),
                                               X.tensor_scalar_mul(lnmg[:, c], lnsm[:, c], -1.0 / D),
                                               X.tensor_scalar_mul(lnvr[:, c], lnsq[:, c], 1.0 / D))[-1],
                          reads=[b_g16[g]], writes=[b_g16[g]])
                        E("vector", lambda X: X.tensor_mul(lnsq[:, c], lnmn[:, c], lnmg[:, c]),
                          reads=[b_g16[g]], writes=[b_g16[g]])
                        E("vector", lambda X: X.tensor_add(lnvr[:, c], lnvr[:, c], lnsq[:, c]),
                          reads=[b_g16[g]], writes=[b_g16[g]])
                        E("scalar", lambda X: X.activation(lnrs[:, c], lnvr[:, c], AF.Sqrt, bias=cEPS),
                          reads=[b_g16[g], b_cst], writes=[b_g16[g]])
                        E("vector", lambda X: X.reciprocal(lnrs[:, c], lnrs[:, c]),
                          reads=[b_g16[g]], writes=[b_g16[g]])
                        E("vector", lambda X: X.tensor_mul(lnnb[:, c], lnmg[:, c], lnrs[:, c]),
                          reads=[b_g16[g]], writes=[b_g16[g]])

                def apply_ln(slot, buf, b_src, src_ap_2d):
                    g = slot // 16
                    E("scalar", lambda X: X.activation(lnx[buf][:], src_ap_2d, AF.Identity,
                                                       bias=lnnb[:, slot:slot + 1],
                                                       scale=lnrs[:, slot:slot + 1]),
                      reads=[b_src, b_g16[g]], writes=[b_lnx[buf]])

                return sweep, apply_ln

            def transpose8(b_dst, dst_ap, src_sb, b_src):
                pb = pT_ctr[0] % 2
                pT_ctr[0] += 1
                pT, b_pT = pT2[pb], b_pT2[pb]
                E("tensor", lambda X, pT=pT: [X.transpose(pT[:, b * 128:(b + 1) * 128],
                                                   src_sb[:, b * 128:(b + 1) * 128], ident[:])
                                       for b in range(8)][-1],
                  reads=[b_src, b_ident], writes=[b_pT])
                E("scalar", lambda X, pT=pT: X.activation(dst_ap, pT[:, 0:1024], AF.Identity),
                  reads=[b_pT], writes=[b_dst])

            def projQ(lnxT_, b_lnxT_, w_sb, b_w, wcol0, wstride, qb):
                def f(X):
                    for kt in range(8):
                        ins = X.matmul(pP[:, qb * 512:(qb + 1) * 512],
                                       lnxT_[:, kt * 128:(kt + 1) * 128],
                                       w_sb[:, kt * wstride + wcol0: kt * wstride + wcol0 + 512],
                                       start=(kt == 0), stop=(kt == 7))
                    return ins
                E("tensor", f, reads=[b_lnxT_, b_w], writes=[b_pP[qb]])

            def rope(buf, dst, b_dst, srcf, b_src, tabc, b_tc, tabs, b_ts, col):
                q3 = srcf[:].rearrange("p (h d) -> p h d", h=H)
                o3 = dst[:].rearrange("p (h d) -> p h d", h=H)
                t3a = rt1[buf][:].rearrange("p (h d) -> p h d", h=H)
                t3b = rt2[buf][:].rearrange("p (h d) -> p h d", h=H)
                cs = tabc[:, col:col + 32].unsqueeze(1).broadcast_to((128, H, 32))
                sn = tabs[:, col:col + 32].unsqueeze(1).broadcast_to((128, H, 32))
                E("vector", lambda X: (X.tensor_tensor(t3a[:, :, 0:32], q3[:, :, 0:32], cs, ALU.mult),
                                       X.tensor_tensor(t3a[:, :, 32:64], q3[:, :, 32:64], sn, ALU.mult))[-1],
                  reads=[b_src, b_tc, b_ts], writes=[b_rt1[buf]])
                E("vector", lambda X: (X.tensor_tensor(t3b[:, :, 0:32], q3[:, :, 0:32], sn, ALU.mult),
                                       X.tensor_tensor(t3b[:, :, 32:64], q3[:, :, 32:64], cs, ALU.mult))[-1],
                  reads=[b_src, b_tc, b_ts], writes=[b_rt2[buf]])
                E("vector", lambda X: (X.tensor_sub(o3[:, :, 0:32], t3a[:, :, 0:32], t3a[:, :, 32:64]),
                                       X.tensor_add(o3[:, :, 32:64], t3b[:, :, 0:32], t3b[:, :, 32:64]))[-1],
                  reads=[b_rt1[buf], b_rt2[buf]], writes=[b_dst])

            E("gpsimd", lambda X: X.memset(Sf[:], 0.0), writes=[b_Sf])
            with ExitStack() as p0a:
                pKV = p0a.enter_context(nc.psum_tensor("pKV", [128, H * DV], F32))
                b_pKV = Buf("pKV")
                pending_kv = [None]

                def emit_pending_kv():
                    if pending_kv[0] is not None:
                        fkv, bufs = pending_kv[0]
                        E("tensor", fkv, reads=bufs, writes=[b_pKV])
                        pending_kv[0] = None

                sched = []
                if cfg.nptau and STAGE >= 0:
                    ratio = max(1, cfg.nptau // max(cfg.ntau, 1))
                    li = 0
                    for pi in range(cfg.nptau):
                        sched.append(("P", pi))
                        if STAGE >= 1 and (pi + 1) % ratio == 0 and li < cfg.ntau:
                            sched.append(("L", li))
                            li += 1
                    while STAGE >= 1 and li < cfg.ntau:
                        sched.append(("L", li))
                        li += 1
                elif STAGE >= 1:
                    sched = [("L", i) for i in range(cfg.ntau)]

                def ptab(g):
                    gb_ = g % 2
                    dma("sync", cosP[gb_][:],
                        ap["cos_pref"][:, g * cfg.pgrp * 32:(g + 1) * cfg.pgrp * 32],
                        [b_cosP[gb_]])
                    dma("sync", sinP[gb_][:],
                        ap["sin_pref"][:, g * cfg.pgrp * 32:(g + 1) * cfg.pgrp * 32],
                        [b_sinP[gb_]])

                if cfg.ngrp:
                    ptab(0)

                sweep, apply_ln = make_sweep(sched)
                LOOK = 24

                def stageA(pos):
                    kind, tau = sched[pos]
                    buf = pos % NB
                    src = ap["x_pref"] if kind == "P" else ap["x_loc"]
                    dma("sync", xt[buf][:], src[tau * 128:(tau + 1) * 128, :], [b_xt[buf]])
                    apply_ln(pos, buf, b_xt[buf], xt[buf][:])

                def stageT(pos):
                    buf = pos % NB
                    transpose8(b_lnxT[buf], lnxT[buf][:], lnx[buf], b_lnx[buf])

                def quarters(buf, w_sb, b_w, n, wstride, evac):
                    for qt in range(n):
                        qb = next_bank()
                        projQ(lnxT[buf], b_lnxT[buf], w_sb, b_w, qt * 512, wstride, qb)
                        evac(qt, qb)

                def ev_scalar(dst_sb, b_dst):
                    def f(qt, qb):
                        E("scalar", lambda X, qt=qt, qb=qb: X.activation(
                            dst_sb[:, qt * 512:(qt + 1) * 512],
                            pP[:, qb * 512:(qb + 1) * 512], AF.Identity),
                          reads=[b_pP[qb]], writes=[b_dst])
                    return f

                def ev_vector(dst_sb, b_dst):
                    def f(qt, qb):
                        E("vector", lambda X, qt=qt, qb=qb: X.tensor_copy(
                            dst_sb[:, qt * 512:(qt + 1) * 512],
                            pP[:, qb * 512:(qb + 1) * 512]),
                          reads=[b_pP[qb]], writes=[b_dst])
                    return f

                def ev_mixed(dst_sb, b_dst):
                    def f(qt, qb):
                        if qt % 2 == 0:
                            ev_vector(dst_sb, b_dst)(qt, qb)
                        else:
                            ev_scalar(dst_sb, b_dst)(qt, qb)
                    return f

                def stageP(pos):
                    kind, tau = sched[pos]
                    buf = pos % NB
                    if kind == "P":
                        g, ti = divmod(tau, cfg.pgrp)
                        gb_ = g % 2
                        quarters(buf, wk, b_wk, 2, D, ev_scalar(kf[buf], b_kf[buf]))
                        quarters(buf, wv, b_wv, 4, 2 * D, ev_mixed(vbf[buf], b_vbf[buf]))
                        emit_pending_kv()
                        rope(buf, kro[buf], b_kro[buf], kf[buf], b_kf[buf],
                             cosP[gb_], b_cosP[gb_], sinP[gb_], b_sinP[gb_], ti * 32)
                        def f(X, buf=buf, ti=ti):
                            w3 = wpref[:, ti * H:(ti + 1) * H].unsqueeze(2).broadcast_to((128, H, 64))
                            k3 = kro[buf][:].rearrange("p (h d) -> p h d", h=H)
                            o3 = kdw[buf][:].rearrange("p (h d) -> p h d", h=H)
                            return X.tensor_tensor(o3, k3, w3, ALU.mult)
                        E("vector", f, reads=[b_kro[buf], b_wpref], writes=[b_kdw[buf]])
                        def fkv(X, buf=buf, ti=ti):
                            for h in range(H):
                                ins = X.matmul(pKV[64:128, h * DV:(h + 1) * DV],
                                               kdw[buf][:, h * 64:(h + 1) * 64],
                                               vbf[buf][:, h * DV:(h + 1) * DV],
                                               start=(ti == 0 and h % 4 == 0),
                                               stop=(ti == cfg.pgrp - 1 and h % 4 == 3),
                                               tile_position=(0, 64))
                            return ins
                        pending_kv[0] = (fkv, [b_kdw[buf], b_vbf[buf]])
                        if ti == cfg.pgrp - 1:
                            emit_pending_kv()
                            def f(X):
                                S3 = Sf[64:128, :].rearrange("p (h v) -> p h v", h=H)
                                l3 = lamgH[64:128, :].unsqueeze(2).broadcast_to((64, H, DV))
                                return X.tensor_tensor(S3, S3, l3, ALU.mult)
                            E("vector", f, reads=[b_Sf, b_lamg], writes=[b_Sf])
                            E("vector", lambda X: X.tensor_add(Sf[64:128, :], Sf[64:128, :],
                                                               pKV[64:128, :]),
                              reads=[b_Sf, b_pKV], writes=[b_Sf])
                            if g + 1 < cfg.ngrp:
                                ptab(g + 1)
                    else:
                        emit_pending_kv()
                        quarters(buf, wq, b_wq, 2, D, ev_scalar(qf[buf], b_qf[buf]))
                        rope(buf, qro[buf], b_qro[buf], qf[buf], b_qf[buf],
                             cosL, b_cosL, sinL, b_sinL, tau * 32)
                        dma("sync", q_s[tau * 128:(tau + 1) * 128, :], qro[buf][:],
                            [b_qs], [b_qro[buf]])
                        quarters(buf, wk, b_wk, 2, D, ev_scalar(kf[buf], b_kf[buf]))
                        rope(buf, kro[buf], b_kro[buf], kf[buf], b_kf[buf],
                             cosL, b_cosL, sinL, b_sinL, tau * 32)
                        dma("sync", k_s[tau * 128:(tau + 1) * 128, :], kro[buf][:],
                            [b_ks], [b_kro[buf]])
                        quarters(buf, wv, b_wv, 4, 2 * D, ev_mixed(vbf[buf], b_vbf[buf]))
                        dma("sync", v_s[tau * 128:(tau + 1) * 128, :], vbf[buf][:],
                            [b_vs], [b_vbf[buf]])
                        quarters(buf, wg, b_wg, 4, 2 * D, ev_scalar(gf[buf], b_gf[buf]))
                        dma("sync", g_s[tau * 128:(tau + 1) * 128, :], gf[buf][:],
                            [b_gs], [b_gf[buf]])

                if sched:
                    for sl in range(min(LOOK, len(sched))):
                        sweep(sl)
                    stageA(0)
                    if len(sched) > 1:
                        stageA(1)
                    stageT(0)
                    for pos in range(len(sched)):
                        if pos + LOOK < len(sched):
                            sweep(pos + LOOK)
                        if pos + 2 < len(sched):
                            stageA(pos + 2)
                        if pos + 1 < len(sched):
                            stageT(pos + 1)
                        stageP(pos)
                emit_pending_kv()
                kb.barrier()

            if DEBUG_OUTS:
                dma("sync", s_dbg, Sf[64:128, :], [Buf()], [b_Sf])
            kb.barrier()

        accS = e(nc.sbuf_tensor("accS", [128, cfg.ntau * D], F32))
        b_accT = [Buf(f"acc{t}") for t in range(cfg.ntau)]
        ln2T = e(nc.sbuf_tensor("ln2TS", [128, 8 * cfg.ntau * 128], BF16))
        b_ln2T = [Buf(f"ln2T{g}") for g in range(cfg.ntg)]
        WeT = None
        b_WeTf = Buf("WeT")
        b_WeT_tau = [Buf(f"We{t}") for t in range(cfg.ntau)]
        punits = 0
        if STAGE >= 2:
            with ExitStack() as p1:
                punits = _retention(nc, p1.enter_context, kb, cfg, ap, q_s, b_qs, k_s, b_ks,
                                    v_s, b_vs, g_s, b_gs, ident, b_ident, Sf, b_Sf, accS, b_accT,
                                    cst, b_cst, ln2T, b_ln2T, We, b_WeT_tau, WeT, b_WeTf)
                kb.barrier()

        if STAGE >= 3:
            with ExitStack() as p23:
                _ffn(nc, p23.enter_context, kb, cfg, ap, accS, b_accT, ident, b_ident,
                     We, b_WeT_tau, WeT, b_WeTf, ln2T, b_ln2T, cst, b_cst, punits)
                kb.barrier()

        b_out = Buf("out")
        if STAGE >= 2:
            if DEBUG_OUTS:
                dma("sync", acc_dbg, accS[:], [Buf()], b_accT)
            dma("sync", out.rearrange("(tau p) d -> p tau d", p=128),
                accS[:].rearrange("p (tau d) -> p tau d", d=D), [b_out], b_accT)
        else:
            E("gpsimd", lambda X: X.memset(accS[:], 0.0), writes=b_accT)
            dma("sync", out.rearrange("(tau p) d -> p tau d", p=128),
                accS[:].rearrange("p (tau d) -> p tau d", d=D), [b_out], b_accT)
        kb._wait("sync", b_out.wr)

        block = e(nc.Block())
        kb.finalize(block)
    return nc



def _router(kb, env, tau, buf):
    E = kb.emit
    TB = env["TB"]
    accS, b_accT = env["accS"], env["b_accT"]
    ln2T, b_ln2T = env["ln2T"], env["b_ln2T"]
    We, b_WeT_tau = env["We"], env["b_WeT_tau"]
    ident, b_ident = env["ident"], env["b_ident"]
    cst, b_cst = env["cst"], env["b_cst"]
    st = env["st"]; b_st = env["b_st"][buf]
    ln2, b_ln2 = env["ln2"][buf], env["b_ln2"][buf]
    dmp, b_dmp = env["dmp"], env["b_dmp"]
    pT, b_pT = env["pT"], env["b_pT"]
    pG, b_pG = env["pG"], env["b_pG"]
    gts, b_gts = env["gts"], env["b_gts"]
    msk, b_msk = env["msk"], env["b_msk"]
    eqs, b_eqs = env["eqs"], env["b_eqs"]
    Webf, b_Webf = env["Webf"], env["b_Webf"]
    WeTs, b_WeTf = env["WeTs"], env["b_WeTf"]
    gwT, b_gwT = env["gwT"], env["b_gwT"]
    gb, b_gb = env["gb"], env["b_gb"]
    onesb, b_ones = env["onesb"], env["b_ones"]
    b2bf, b_b2 = env["b2bf"], env["b_b2"]
    b_m12 = env["b_m12"][buf]
    xa = accS[:, tau * D:(tau + 1) * D]
    sc = 16 + 32 * buf
    sm = st[:, sc:sc + 1]; sq = st[:, sc + 1:sc + 2]
    mean = st[:, sc + 2:sc + 3]; var = st[:, sc + 3:sc + 4]
    rstd = st[:, sc + 4:sc + 5]; nb = st[:, sc + 5:sc + 6]
    tg = tau // 4
    E("scalar", lambda X: (X.activation(dmp[0], xa, AF.Square, accum_out=sq),
                           X.activation(dmp[1], xa, AF.Identity, accum_out=sm))[-1],
      reads=[b_accT[tau]], writes=list(b_dmp) + [b_st])
    E("vector", lambda X: (X.tensor_scalar_mul(mean, sm, 1.0 / D),
                           X.tensor_scalar_mul(var, sq, 1.0 / D))[-1],
      reads=[b_st], writes=[b_st])
    E("vector", lambda X: X.tensor_mul(nb, mean, mean),
      reads=[b_st], writes=[b_st])
    E("vector", lambda X: X.tensor_sub(var, var, nb),
      reads=[b_st], writes=[b_st])
    E("scalar", lambda X: X.activation(rstd, var, AF.Sqrt, bias=cst[:, 0:1]),
      reads=[b_st, b_cst], writes=[b_st])
    E("vector", lambda X: X.reciprocal(rstd, rstd),
      reads=[b_st], writes=[b_st])
    E("vector", lambda X: X.tensor_mul(nb, mean, rstd),
      reads=[b_st], writes=[b_st])
    E("vector", lambda X: X.tensor_scalar_mul(nb, nb, -1.0),
      reads=[b_st], writes=[b_st])
    E("scalar", lambda X: X.activation(ln2[:], xa, AF.Identity, bias=nb, scale=rstd),
      reads=[b_accT[tau], b_st], writes=[b_ln2])
    E("tensor", lambda X: [X.transpose(pT[:, b * 128:(b + 1) * 128],
                                       ln2[:, b * 128:(b + 1) * 128], ident[:])
                           for b in range(8)][-1],
      reads=[b_ln2, b_ident], writes=[b_pT])
    def f(X, tau=tau):
        dst = ln2T[:].rearrange("p (kt t) -> p kt t", kt=8)
        src = pT[:, 0:1024].rearrange("p (kt t) -> p kt t", kt=8)
        return X.activation(dst[:, :, tau * 128:(tau + 1) * 128], src, AF.Identity)
    E("scalar", f, reads=[b_pT], writes=[b_ln2T[tg]])
    def f(X, tau=tau):
        for kt in range(8):
            X.matmul(pG[:, 0:NE], ln2T[:, kt * TB + tau * 128: kt * TB + (tau + 1) * 128],
                     gwT[:, kt * NE:(kt + 1) * NE], start=(kt == 0), stop=False)
        return X.matmul(pG[:, 0:NE], onesb[0:1, :], gb[0:1, :], start=False, stop=True)
    E("tensor", f, reads=[b_ln2T[tg], b_gwT, b_ones, b_gb], writes=[b_pG])
    mc = 22 + 32 * buf
    m1 = st[:, mc:mc + 1]; m2 = st[:, mc + 1:mc + 2]; p1 = st[:, mc + 2:mc + 3]
    p2v = st[:, mc + 3:mc + 4]; dd = st[:, mc + 4:mc + 5]
    eq1 = eqs[:, 0:NE]; eq2 = eqs[:, NE:2 * NE]
    E("vector", lambda X: X.tensor_copy(gts[:], pG[:, 0:NE]),
      reads=[b_pG], writes=[b_gts])
    E("vector", lambda X: X.reduce_max(m1, gts[:], AX.X),
      reads=[b_gts], writes=[b_m12])
    E("vector", lambda X: X.tensor_single_scalar(eq1, gts[:], m1, ALU.is_equal),
      reads=[b_gts, b_m12], writes=[b_eqs])
    E("vector", lambda X: X.scalar_tensor_tensor(msk[:], eq1, -1e30, gts[:],
                                                 op0=ALU.mult, op1=ALU.add),
      reads=[b_eqs, b_gts], writes=[b_msk])
    E("vector", lambda X: X.reduce_max(m2, msk[:], AX.X),
      reads=[b_msk], writes=[b_m12])
    E("vector", lambda X: X.tensor_single_scalar(eq2, msk[:], m2, ALU.is_equal),
      reads=[b_msk, b_m12], writes=[b_eqs])
    E("vector", lambda X: X.tensor_sub(dd, m1, m2),
      reads=[b_m12], writes=[b_m12])
    E("scalar", lambda X: X.activation(p1, dd, AF.Sigmoid),
      reads=[b_m12], writes=[b_m12])
    E("vector", lambda X: X.tensor_scalar(p2v, p1, -1.0, 1.0, op0=ALU.mult, op1=ALU.add),
      reads=[b_m12], writes=[b_m12])
    E("vector", lambda X, tau=tau: X.tensor_single_scalar(
        We[:, tau * NE:(tau + 1) * NE], eq1, p1, ALU.mult),
      reads=[b_eqs, b_m12], writes=[b_WeT_tau[tau]])
    E("vector", lambda X, tau=tau: X.scalar_tensor_tensor(
        We[:, tau * NE:(tau + 1) * NE], eq2, p2v, We[:, tau * NE:(tau + 1) * NE],
        op0=ALU.mult, op1=ALU.add),
      reads=[b_eqs, b_m12, b_WeT_tau[tau]], writes=[b_WeT_tau[tau]])
    E("vector", lambda X, tau=tau: X.tensor_copy(Webf[:], We[:, tau * NE:(tau + 1) * NE]),
      reads=[b_WeT_tau[tau]], writes=[b_Webf])
    E("tensor", lambda X: X.transpose(pT[0:NE, 0:128], Webf[:], ident[:]),
      reads=[b_Webf, b_ident], writes=[b_pT])
    E("scalar", lambda X: X.activation(WeTs[0:NE, :], pT[0:NE, 0:128], AF.Identity),
      reads=[b_pT], writes=[b_WeTf])
    def f(X):
        for nb_ in range(2):
            ins = X.matmul(pG[:, nb_ * 512:(nb_ + 1) * 512],
                           WeTs[0:NE, :],
                           b2bf[0:NE, nb_ * 512:(nb_ + 1) * 512],
                           start=True, stop=True)
        return ins
    E("tensor", f, reads=[b_WeTf, b_b2], writes=[b_pG])
    E("vector", lambda X, tau=tau: X.tensor_add(accS[:, tau * D:(tau + 1) * D],
                                                accS[:, tau * D:(tau + 1) * D], pG[:, :]),
      reads=[b_accT[tau], b_pG], writes=[b_pG, b_accT[tau]])


def _retention(nc, e, kb, cfg, ap, q_s, b_qs, k_s, b_ks, v_s, b_vs, g_s, b_gs,
               ident, b_ident, Sf, b_Sf, accS, b_accT, cst, b_cst,
               ln2T, b_ln2T, We, b_WeT_tau, WeT, b_WeTf):
    E = kb.emit
    NB = 2
    TB = cfg.ntau * 128
    wo = e(nc.sbuf_tensor("woS", [128, 16 * D], BF16)); b_wo = Buf()
    E("gpsimd", lambda X: X.dma_start(out=wo[:].rearrange("p (a n) -> p a n", a=16),
                                      in_=ap["L0_attn_Wo"].rearrange("(a p) n -> p a n", p=128)),
      writes=[b_wo], kind="d")
    dq128 = e(nc.sbuf_tensor("dq128S", [128, H], F32)); b_dq = Buf()
    dkv128 = e(nc.sbuf_tensor("dkv128S", [128, H], F32)); b_dkv = Buf()
    lamH = e(nc.sbuf_tensor("lamHS", [128, H], F32)); b_lam = Buf()
    dmat2 = e(nc.sbuf_tensor("dmat2S", [128, H * 64], BF16)); b_dm = Buf()
    gncol = e(nc.sbuf_tensor("gncolS", [128, 1], F32)); b_gnc = Buf()
    gwT = e(nc.sbuf_tensor("f_gwT", [128, 8 * NE], BF16)); b_gwT = Buf()
    gb = e(nc.sbuf_tensor("f_gb", [1, NE], BF16)); b_gb = Buf()
    onesb = e(nc.sbuf_tensor("f_ones", [1, 128], BF16)); b_ones = Buf()
    b2bf = e(nc.sbuf_tensor("f_b2", [8, D], BF16)); b_b2 = Buf()
    for dst, bf, nm in ((dq128, b_dq, "dq128"), (dkv128, b_dkv, "dkv128"),
                        (lamH, b_lam, "lamH"), (dmat2, b_dm, "dmat2"), (gncol, b_gnc, "gncol")):
        E("sync", lambda X, d=dst, n=nm: X.dma_start(out=d[:], in_=ap[n]),
          writes=[bf], kind="d")
    E("scalar", lambda X: X.activation(wo[:], wo[:], AF.Identity, scale=gncol[:]),
      reads=[b_wo, b_gnc], writes=[b_wo])
    E("gpsimd", lambda X: X.dma_start(out=gwT[:].rearrange("p (a n) -> p a n", a=8),
                                      in_=ap["gate_wT"].rearrange("(a p) n -> p a n", p=128)),
      writes=[b_gwT], kind="d")
    E("gpsimd", lambda X: X.dma_start(out=gb[:], in_=ap["gate_b"]), writes=[b_gb], kind="d")
    E("sync", lambda X: X.dma_start(out=onesb[:], in_=ap["onesb"]), writes=[b_ones], kind="d")
    E("gpsimd", lambda X: X.dma_start(out=b2bf[:], in_=ap["b2"]), writes=[b_b2], kind="d")

    qtk = [e(nc.sbuf_tensor(f"r_q{i}", [128, D], BF16)) for i in range(NB)]
    b_qtk = [Buf() for _ in range(NB)]
    qdq, b_qdq = qtk, b_qtk
    rln2A = e(nc.sbuf_tensor("r_ln2A", [128, D], BF16)); b_rln2A = Buf()
    ktk = [e(nc.sbuf_tensor(f"r_k{i}", [128, D], BF16)) for i in range(2)]
    b_ktk = [Buf() for _ in range(2)]
    kch = e(nc.sbuf_tensor("r_kc", [128, D], BF16)); b_kch = Buf()
    kdc, b_kdc = kch, b_kch
    grc = e(nc.sbuf_tensor("r_g", [128, 2 * D], F32)); b_grc = Buf()
    kTs = e(nc.sbuf_tensor("r_kT", [128, H * 128], BF16)); b_kTs = Buf()
    qTs = e(nc.sbuf_tensor("r_qT", [128, H * 128], BF16)); b_qTs = Buf()
    LT = [e(nc.sbuf_tensor(f"r_LT{i}", [128, H * 64], BF16)) for i in range(2)]
    b_LT = [Buf() for _ in range(2)]
    RT = [e(nc.sbuf_tensor(f"r_RT{i}", [128, H * DV], BF16)) for i in range(2)]
    b_RT = [Buf() for _ in range(2)]
    osb = e(nc.sbuf_tensor("r_o", [128, H * DV], F32)); b_osb = [Buf(), Buf()]
    osq = e(nc.sbuf_tensor("r_osq", [128, H * DV], F32)); b_osq = [Buf(), Buf()]
    o4_one = e(nc.sbuf_tensor("r_o4", [128, H * DV], BF16))
    o4 = [o4_one, o4_one]
    b_o4h = [Buf(), Buf()]
    o3T = e(nc.sbuf_tensor("r_o3T", [128, 16 * 128], BF16))
    b_o3T = [Buf() for _ in range(4)]
    rstt = e(nc.sbuf_tensor("r_rst", [128, 64], F32))
    b_rstt = [[Buf(), Buf()], [Buf(), Buf()]]
    ssq = e(nc.sbuf_tensor("r_ssq", [128, 2 * H], F32))
    ln2, b_ln2 = [rln2A, rln2A], [b_rln2A, b_rln2A]
    stats2 = rstt
    b_stats2 = [Buf() for _ in range(NB)]
    gts = e(nc.sbuf_tensor("f_gt", [128, NE], F32)); b_gts = Buf()
    msk = e(nc.sbuf_tensor("f_msk", [128, NE], F32)); b_msk = Buf()
    b_m12 = Buf()
    eqs = e(nc.sbuf_tensor("f_eq", [128, 2 * NE], F32)); b_eqs = Buf()
    Webf = e(nc.sbuf_tensor("f_Webf", [128, NE], BF16)); b_Webf = Buf()
    WeTs = e(nc.sbuf_tensor("f_WeTs", [8, 128], BF16))

    pTr = e(nc.psum_tensor("pTr", [128, 2048], BF16)); b_pTr = Buf("pTr")
    pA = e(nc.psum_tensor("pA", [128, 1024], F32)); b_pA = Buf("pA")
    pO = e(nc.psum_tensor("pO", [128, 8 * DV], F32))
    b_pOo = Buf("pOo")
    b_pOkv = Buf("pOkv")
    pO3T = e(nc.psum_tensor("pO3T", [128, 2048], BF16))
    b_pO3T = [Buf("pO3T0"), Buf("pO3T1")]

    PUNITS = 0 * min(64, max(0, 2 * (cfg.ntau - 5) - 1) // 2 * 2)
    if PUNITS:
        b1Li = e(nc.sbuf_tensor("i_b1L", [128, NE * 32], F32)); b_b1i = Buf()
        w18 = [e(nc.sbuf_tensor(f"i_w1{i}", [128, 8 * 512], BF16)) for i in range(2)]
        b_w18 = [Buf() for _ in range(2)]
        w28 = [e(nc.sbuf_tensor(f"i_w2{i}", [128, 4 * 1024], BF16)) for i in range(2)]
        b_w28 = [Buf() for _ in range(2)]
        hTi = e(nc.sbuf_tensor("i_hT", [128, 4 * 512], BF16)); b_hTi = Buf()
        pHi = e(nc.psum_tensor("pHi", [128, 512], F32)); b_pHi = Buf("pHi")
        pXi = e(nc.psum_tensor("pXi", [128, 512], F32)); b_pXi = Buf("pXi")
        E("sync", lambda X: X.dma_start(out=b1Li[:], in_=ap["b1L"]), writes=[b_b1i], kind="d")
    TBr = cfg.ntau * 128
    pst = {"load": 0, "comp": 0}

    def emit_load(u):
        wb = u % 2
        ex, h8 = u // 8, u % 8
        E("gpsimd", lambda X, ex=ex, h8=h8, wb=wb: X.dma_start(
            out=w18[wb][:].rearrange("p (a n) -> p a n", a=8),
            in_=ap["w1"][ex, :, h8 * 512:(h8 + 1) * 512]
            .rearrange("(a p) n -> p a n", p=128)),
          writes=[b_w18[wb]], kind="d")
        E("gpsimd", lambda X, ex=ex, h8=h8, wb=wb: X.dma_start(
            out=w28[wb][:].rearrange("p (a n) -> p a n", a=4),
            in_=ap["w2"][ex, h8 * 512:(h8 + 1) * 512, :]
            .rearrange("(a p) n -> p a n", p=128)),
          writes=[b_w28[wb]], kind="d")

    def emit_comp(u):
        wb = u % 2
        ex, h8 = u // 8, u % 8
        for ht in range(4):
            def f(X, wb=wb, ht=ht):
                for kt in range(8):
                    ins = X.matmul(pHi[:, :],
                                   w18[wb][:, kt * 512 + ht * 128: kt * 512 + (ht + 1) * 128],
                                   ln2T[:, kt * TBr: kt * TBr + 512],
                                   start=(kt == 0), stop=(kt == 7))
                return ins
            E("tensor", f, reads=[b_w18[wb], b_ln2T[0]], writes=[b_pHi])
            def f(X, ex=ex, h8=h8, ht=ht):
                bcol = ex * 32 + h8 * 4 + ht
                return X.activation(hTi[:, ht * 512:(ht + 1) * 512], pHi[:, :], AF.Gelu,
                                    bias=b1Li[:, bcol:bcol + 1])
            E("scalar", f, reads=[b_pHi, b_b1i], writes=[b_hTi])
        for m in range(4):
            for nb_ in range(2):
                def f(X, wb=wb, m=m, nb_=nb_):
                    for kt in range(4):
                        ins = X.matmul(pXi[:, :],
                                       hTi[:, kt * 512 + m * 128: kt * 512 + (m + 1) * 128],
                                       w28[wb][:, kt * 1024 + nb_ * 512: kt * 1024 + (nb_ + 1) * 512],
                                       start=(kt == 0), stop=(kt == 3))
                    return ins
                E("tensor", f, reads=[b_hTi, b_w28[wb]], writes=[b_pXi])
                sl = slice(m * D + nb_ * 512, m * D + (nb_ + 1) * 512)
                E("vector", lambda X, sl=sl, m=m, ex=ex: X.scalar_tensor_tensor(
                    accS[:, sl], pXi[:, :], We[:, m * NE + ex: m * NE + ex + 1],
                    accS[:, sl], op0=ALU.mult, op1=ALU.add),
                  reads=[b_pXi, b_WeT_tau[m], b_accT[m]], writes=[b_accT[m]])

    def pump():
        if pst["comp"] < pst["load"] and pst["comp"] < PUNITS:
            emit_comp(pst["comp"]); pst["comp"] += 1
        if pst["load"] < PUNITS and pst["load"] - pst["comp"] < 2:
            emit_load(pst["load"]); pst["load"] += 1


    def ld(dst, src, bw, br=()):
        E("sync", lambda X, d=dst, s=src: X.dma_start(out=d, in_=s),
          reads=list(br), writes=list(bw), kind="d")

    b_m12l = [b_m12, Buf()]
    envA = dict(TB=TB, accS=accS, b_accT=b_accT, ln2T=ln2T, b_ln2T=b_ln2T,
                We=We, b_WeT_tau=b_WeT_tau, ident=ident, b_ident=b_ident,
                cst=cst, b_cst=b_cst, st=rstt, b_st=b_stats2,
                ln2=ln2, b_ln2=b_ln2,
                dmp=[osq[:, 0:1024], osq[:, 1024:2048]], b_dmp=b_osq,
                pT=pTr, b_pT=b_pTr, pG=pA, b_pG=b_pA,
                gts=gts, b_gts=b_gts, msk=msk, b_msk=b_msk, eqs=eqs, b_eqs=b_eqs,
                Webf=Webf, b_Webf=b_Webf, WeTs=WeTs, b_WeTf=b_WeTf,
                gwT=gwT, b_gwT=b_gwT, gb=gb, b_gb=b_gb, onesb=onesb, b_ones=b_ones,
                b2bf=b2bf, b_b2=b_b2, b_m12=b_m12l)

    def p2_router(tau, buf):
        _router(kb, envA, tau, buf)

    def tau_loads(tau):
        buf = tau % NB
        ld(qtk[buf][:], q_s[tau * 128:(tau + 1) * 128, :], [b_qtk[buf]], [b_qs])
        ld(ktk[buf][:], k_s[tau * 128:(tau + 1) * 128, :], [b_ktk[buf]], [b_ks])
        ld(accS[:, tau * D:(tau + 1) * D].rearrange("p (o d) -> p o d", o=1),
           ap["x_loc"][tau * 128:(tau + 1) * 128, :].rearrange("(o t) d -> t o d", o=1),
           [b_accT[tau]])

        def f(X, buf=buf):
            d3 = dq128[:].unsqueeze(2).broadcast_to((128, H, 64))
            q3 = qtk[buf][:].rearrange("p (h d) -> p h d", h=H)
            return X.tensor_tensor(q3, q3, d3, ALU.mult)
        E("vector", f, reads=[b_qtk[buf], b_dq], writes=[b_qdq[buf]])

    tau_loads(0)
    for tau in range(cfg.ntau):
        buf = tau % NB
        if tau + 1 < cfg.ntau:
            tau_loads(tau + 1)

        def fk(X, buf=buf):
            for h in range(H):
                ins = X.transpose(pTr[0:64, h * 128:(h + 1) * 128],
                                  ktk[buf][:, h * 64:(h + 1) * 64], ident[:])
            return ins
        def fq(X, buf=buf):
            for h in range(H):
                ins = X.transpose(pTr[0:64, h * 128:(h + 1) * 128],
                                  qdq[buf][:, h * 64:(h + 1) * 64], ident[:])
            return ins
        def fqup(X, buf=buf):
            for h in range(H):
                ins = X.transpose(pTr[64:128, h * 128:(h + 1) * 128],
                                  qdq[buf][:, h * 64:(h + 1) * 64], ident[:],
                                  tile_position=(0, 64))
            return ins
        E("tensor", lambda X, buf=buf: (fk(X, buf), fqup(X, buf))[-1],
          reads=[b_ktk[buf], b_qdq[buf], b_ident], writes=[b_pTr])
        E("scalar", lambda X: X.activation(kTs[0:64, :], pTr[0:64, 0:H * 128], AF.Identity),
          reads=[b_pTr], writes=[b_kTs])
        for ci_ in range(2):
            def f(X, ci_=ci_):
                src = pTr[64:128, 0:H * 128].rearrange("p (h t) -> p h t", h=H)
                dst = LT[ci_][64:128, :].rearrange("p (h i) -> p h i", h=H)
                return X.activation(dst, src[:, :, ci_ * 64:(ci_ + 1) * 64], AF.Identity)
            E("scalar", f, reads=[b_pTr], writes=[b_LT[ci_]])
        E("tensor", fq, reads=[b_qdq[buf], b_ident], writes=[b_pTr])
        E("scalar", lambda X: X.activation(qTs[0:64, :], pTr[0:64, 0:H * 128], AF.Identity),
          reads=[b_pTr], writes=[b_qTs])

        for ci in range(2):
            ch = 2 * tau + ci
            ld(kch[0:64, :], k_s[ch * 64:(ch + 1) * 64, :], [b_kch], [b_ks])
            ld(RT[ci][0:64, :], v_s[ch * 64:(ch + 1) * 64, :], [b_RT[ci]], [b_vs])
            ld(grc[0:64, :], g_s[ch * 64:(ch + 1) * 64, :], [b_grc], [b_gs])
            E("scalar", lambda X: X.activation(grc[0:64, :], grc[0:64, :], AF.Silu),
              reads=[b_grc], writes=[b_grc])
            E("scalar", lambda X, ci=ci: X.activation(RT[ci][64:128, :], Sf[64:128, :], AF.Identity),
              reads=[b_Sf], writes=[b_RT[ci]])
            def f(X):
                S3 = Sf[64:128, :].rearrange("p (h v) -> p h v", h=H)
                l3 = lamH[64:128, :].unsqueeze(2).broadcast_to((64, H, DV))
                return X.tensor_tensor(S3, S3, l3, ALU.mult)
            E("vector", f, reads=[b_Sf, b_lam], writes=[b_Sf])
            def f(X, ci=ci):
                d3 = dkv128[0:64, :].unsqueeze(2).broadcast_to((64, H, 64))
                k3 = kch[0:64, :].rearrange("p (h d) -> p h d", h=H)
                return X.tensor_tensor(k3, k3, d3, ALU.mult)
            E("vector", f, reads=[b_kch, b_dkv], writes=[b_kdc])
            def f(X, ci=ci):
                for h in range(H):
                    ins = X.matmul(pA[0:64, h * 64:(h + 1) * 64],
                                   kTs[0:64, h * 128 + ci * 64: h * 128 + (ci + 1) * 64],
                                   qTs[0:64, h * 128 + ci * 64: h * 128 + (ci + 1) * 64],
                                   start=(h % 8 == 0), stop=(h % 8 == 7))
                return ins
            E("tensor", f, reads=[b_kTs, b_qTs], writes=[b_pA])
            E("vector", lambda X, ci=ci: X.tensor_tensor(LT[ci][0:64, :], pA[0:64, :],
                                                         dmat2[0:64, :], ALU.mult),
              reads=[b_pA, b_dm], writes=[b_LT[ci]])
            for hh in range(2):
                hs = range(hh * 8, hh * 8 + 8)
                def f(X, ci=ci, hs=hs):
                    for h in hs:
                        ins = X.matmul(pO[0:64, (h % 8) * DV:(h % 8 + 1) * DV],
                                       LT[ci][:, h * 64:(h + 1) * 64],
                                       RT[ci][:, h * DV:(h + 1) * DV],
                                       start=(h % 4 == 0), stop=(h % 4 == 3))
                    return ins
                E("tensor", f, reads=[b_LT[ci], b_RT[ci]], writes=[b_pOo])
                cols = slice(hh * 1024, (hh + 1) * 1024)
                E("scalar", lambda X, cols=cols: X.activation(osb[0:64, cols], pO[0:64, 0:1024],
                                                              AF.Identity),
                  reads=[b_pOo], writes=[b_osb[hh]])
                E("scalar", lambda X, cols=cols: X.activation(osq[0:64, cols], pO[0:64, 0:1024],
                                                              AF.Square),
                  reads=[b_pOo], writes=[b_osq[hh]])
                def f(X, ci=ci, hs=hs):
                    for h in hs:
                        ins = X.matmul(pO[64:128, (h % 8) * DV:(h % 8 + 1) * DV],
                                       kdc[0:64, h * 64:(h + 1) * 64],
                                       RT[ci][0:64, h * DV:(h + 1) * DV],
                                       start=(h % 4 == 0), stop=(h % 4 == 3),
                                       tile_position=(0, 64))
                    return ins
                E("tensor", f, reads=[b_kdc, b_RT[ci]], writes=[b_pOkv])
                E("vector", lambda X, cols=cols: X.tensor_add(Sf[64:128, cols],
                                                              Sf[64:128, cols], pO[64:128, 0:1024]),
                  reads=[b_Sf, b_pOkv], writes=[b_Sf])
            for hh in range(2):
                cols = slice(hh * 1024, (hh + 1) * 1024)
                sq = ssq[:, ci * H + hh * 8: ci * H + (hh + 1) * 8]
                rc = rstt[:, ci * 32 + hh * 8: ci * 32 + (hh + 1) * 8]
                E("vector", lambda X, sq=sq, cols=cols: X.tensor_reduce(
                    sq[0:64, :], osq[0:64, cols].rearrange("p (h v) -> p h v", h=8),
                    AX.X, ALU.add),
                  reads=[b_osq[hh]], writes=[b_rstt[ci][hh]])
                E("scalar", lambda X, sq=sq, rc=rc: X.activation(
                    rc[0:64, :], sq[0:64, :], AF.Sqrt, bias=cst[0:64, 0:1], scale=cst[0:64, 1:2]),
                  reads=[b_rstt[ci][hh], b_cst], writes=[b_rstt[ci][hh]])
                E("vector", lambda X, rc=rc: X.reciprocal(rc[0:64, :], rc[0:64, :]),
                  reads=[b_rstt[ci][hh]], writes=[b_rstt[ci][hh]])
                def f(X, rc=rc, cols=cols):
                    o3 = osb[0:64, cols].rearrange("p (h v) -> p h v", h=8)
                    r3 = rc[0:64, :].unsqueeze(2).broadcast_to((64, 8, DV))
                    return X.tensor_tensor(o3, o3, r3, ALU.mult)
                E("vector", f, reads=[b_osb[hh], b_rstt[ci][hh]], writes=[b_osb[hh]])
                E("vector", lambda X, ci=ci, cols=cols: X.tensor_tensor(
                    o4[ci][0:64, cols], osb[0:64, cols], grc[0:64, cols], ALU.mult),
                  reads=[b_osb[hh], b_grc], writes=[b_o4h[hh]])
                def f(X, ci=ci, hh=hh):
                    for b in range(8):
                        ins = X.transpose(pO3T[:, ci * 1024 + (hh * 8 + b) * 64:
                                               ci * 1024 + (hh * 8 + b + 1) * 64],
                                          o4[ci][0:64, hh * 1024 + b * 128:
                                                 hh * 1024 + (b + 1) * 128],
                                          ident[0:64, 0:64])
                    return ins
                E("tensor", f, reads=[b_o4h[hh], b_ident], writes=[b_pO3T[ci]])
                def f(X, ci=ci, hh=hh):
                    dst = o3T[:, :].rearrange("p (b t) -> p b t", b=16)
                    src = pO3T[:, ci * 1024 + hh * 512: ci * 1024 + (hh + 1) * 512]                         .rearrange("p (b t) -> p b t", b=8)
                    return X.activation(dst[:, hh * 8:(hh + 1) * 8, ci * 64:(ci + 1) * 64],
                                        src, AF.Identity)
                E("scalar", f, reads=[b_pO3T[ci]], writes=[b_o3T[ci * 2 + hh]])
            if tau >= 5 and ci == 0:
                pump()

        def f(X):
            for kt in range(16):
                for nb_ in range(2):
                    ins = X.matmul(pA[:, nb_ * 512:(nb_ + 1) * 512],
                                   o3T[:, kt * 128:(kt + 1) * 128],
                                   wo[:, kt * D + nb_ * 512: kt * D + (nb_ + 1) * 512],
                                   start=(kt == 0), stop=(kt == 15))
            return ins
        E("tensor", f, reads=b_o3T + [b_wo], writes=[b_pA])
        E("vector", lambda X, tau=tau: X.tensor_add(
            accS[:, tau * D:(tau + 1) * D], accS[:, tau * D:(tau + 1) * D], pA[:, :]),
          reads=[b_accT[tau], b_pA], writes=[b_accT[tau]])
        if tau >= 5:
            pump()
        if 1 <= tau <= 4:
            p2_router(tau - 1, (tau - 1) % NB)
    for r in range(min(4, cfg.ntau)):
        if r > cfg.ntau - 2:
            p2_router(r, r % NB)
    while pst["comp"] < PUNITS:
        pump()
    return PUNITS


def _ffn(nc, e, kb, cfg, ap, accS, b_accT, ident, b_ident, We, b_WeT_tau,
         WeT, b_WeTf, ln2T, b_ln2T, cst, b_cst, punits=0):
    E = kb.emit
    TB = cfg.ntau * 128
    b1L = e(nc.sbuf_tensor("f_b1L", [128, NE * 32], F32)); b_b1 = Buf()
    E("sync", lambda X: X.dma_start(out=b1L[:], in_=ap["b1L"]), writes=[b_b1], kind="d")

    w1q = [e(nc.sbuf_tensor(f"f_w1q{i}", [128, 8 * 1024], BF16)) for i in range(2)]
    b_w1q = [Buf() for _ in range(2)]
    w2q = [e(nc.sbuf_tensor(f"f_w2q{i}", [128, 8 * 1024], BF16)) for i in range(2)]
    b_w2q = [Buf() for _ in range(2)]
    hT = [e(nc.sbuf_tensor(f"f_hT{i}", [128, 8 * 512], BF16)) for i in range(2)]
    b_hT = [Buf() for _ in range(2)]
    pH = [e(nc.psum_tensor(f"f_pH{i}", [128, 512], F32)) for i in range(2)]
    b_pH = [Buf() for _ in range(2)]
    pX = [e(nc.psum_tensor(f"f_pX{i}", [128, 512], F32)) for i in range(2)]
    b_pX = [Buf() for _ in range(2)]

    nextr = min(4, cfg.ntau)
    envB = None
    if cfg.ntau > 4:
        rln2 = [e(nc.sbuf_tensor(f"rb_ln2{i}", [128, D], BF16)) for i in range(2)]
        rst = e(nc.sbuf_tensor("rb_st", [128, 64], F32))
        rdmp = e(nc.sbuf_tensor("rb_dmp", [128, 2048], F32))
        rgts = e(nc.sbuf_tensor("rb_gt", [128, NE], F32))
        rmsk = e(nc.sbuf_tensor("rb_msk", [128, NE], F32))
        reqs = e(nc.sbuf_tensor("rb_eq", [128, 2 * NE], F32))
        rWebf = e(nc.sbuf_tensor("rb_Webf", [128, NE], BF16))
        rWeTs = e(nc.sbuf_tensor("rb_WeTs", [8, 128], BF16))
        rgwT = e(nc.sbuf_tensor("rb_gwT", [128, 8 * NE], BF16)); b_rgwT = Buf()
        rgb = e(nc.sbuf_tensor("rb_gb", [1, NE], BF16)); b_rgb = Buf()
        rones = e(nc.sbuf_tensor("rb_ones", [1, 128], BF16)); b_rones = Buf()
        rb2bf = e(nc.sbuf_tensor("rb_b2", [8, D], BF16)); b_rb2 = Buf()
        pRT = e(nc.psum_tensor("pRT", [128, 1024], BF16))
        pG2 = e(nc.psum_tensor("pG2", [128, 1024], F32))
        E("gpsimd", lambda X: X.dma_start(out=rgwT[:].rearrange("p (a n) -> p a n", a=8),
                                          in_=ap["gate_wT"].rearrange("(a p) n -> p a n", p=128)),
          writes=[b_rgwT], kind="d")
        E("gpsimd", lambda X: X.dma_start(out=rgb[:], in_=ap["gate_b"]), writes=[b_rgb], kind="d")
        E("sync", lambda X: X.dma_start(out=rones[:], in_=ap["onesb"]), writes=[b_rones], kind="d")
        E("gpsimd", lambda X: X.dma_start(out=rb2bf[:], in_=ap["b2"]), writes=[b_rb2], kind="d")
        envB = dict(TB=TB, accS=accS, b_accT=b_accT, ln2T=ln2T, b_ln2T=b_ln2T,
                    We=We, b_WeT_tau=b_WeT_tau, ident=ident, b_ident=b_ident,
                    cst=cst, b_cst=b_cst, st=rst, b_st=[Buf(), Buf()],
                    ln2=rln2, b_ln2=[Buf(), Buf()],
                    dmp=[rdmp[:, 0:1024], rdmp[:, 1024:2048]], b_dmp=[Buf(), Buf()],
                    pT=pRT, b_pT=Buf("pRT"), pG=pG2, b_pG=Buf("pG2"),
                    gts=rgts, b_gts=Buf(), msk=rmsk, b_msk=Buf(), eqs=reqs, b_eqs=Buf(),
                    Webf=rWebf, b_Webf=Buf(), WeTs=rWeTs, b_WeTf=Buf(),
                    gwT=rgwT, b_gwT=b_rgwT, gb=rgb, b_gb=b_rgb, onesb=rones, b_ones=b_rones,
                    b2bf=rb2bf, b_b2=b_rb2, b_m12=[Buf(), Buf()])

    it = 0
    for ex in range(NE):
        for hq in range(4):
            wb = it % 2
            E("gpsimd", lambda X, ex=ex, hq=hq, wb=wb: X.dma_start(
                out=w1q[wb][:].rearrange("p (a n) -> p a n", a=8),
                in_=ap["w1"][ex, :, hq * 1024:(hq + 1) * 1024]
                .rearrange("(a p) n -> p a n", p=128)),
              writes=[b_w1q[wb]], kind="d")
            E("gpsimd", lambda X, ex=ex, hq=hq, wb=wb: X.dma_start(
                out=w2q[wb][:].rearrange("p (a n) -> p a n", a=8),
                in_=ap["w2"][ex, hq * 1024:(hq + 1) * 1024, :]
                .rearrange("(a p) n -> p a n", p=128)),
              writes=[b_w2q[wb]], kind="d")
            covered = 8 * ex + 2 * hq + 1 < punits
            for tg in range(cfg.ntg):
                while nextr < min(4 * (tg + 1), cfg.ntau):
                    _router(kb, envB, nextr, nextr % 2)
                    nextr += 1
                if tg == 0 and covered:
                    continue
                tgb = tg % 2
                for ht in range(8):
                    hb = ht % 2
                    def f(X, wb=wb, tg=tg, ht=ht, hb=hb):
                        for kt in range(8):
                            ins = X.matmul(pH[hb][:, :],
                                           w1q[wb][:, kt * 1024 + ht * 128: kt * 1024 + (ht + 1) * 128],
                                           ln2T[:, kt * TB + tg * 512: kt * TB + (tg + 1) * 512],
                                           start=(kt == 0), stop=(kt == 7))
                        return ins
                    E("tensor", f, reads=[b_w1q[wb], b_ln2T[tg]], writes=[b_pH[hb]])
                    def f(X, ex=ex, hq=hq, ht=ht, hb=hb, tgb=tgb):
                        bcol = ex * 32 + hq * 8 + ht
                        return X.activation(hT[tgb][:, ht * 512:(ht + 1) * 512],
                                            pH[hb][:, :], AF.Gelu,
                                            bias=b1L[:, bcol:bcol + 1])
                    E("scalar", f, reads=[b_pH[hb], b_b1], writes=[b_hT[tgb]])
                for m in range(4):
                    tau = tg * 4 + m
                    for nb_ in range(2):
                        def f(X, wb=wb, m=m, nb_=nb_, tgb=tgb):
                            for kt in range(8):
                                ins = X.matmul(pX[nb_][:, :],
                                               hT[tgb][:, kt * 512 + m * 128: kt * 512 + (m + 1) * 128],
                                               w2q[wb][:, kt * 1024 + nb_ * 512: kt * 1024 + (nb_ + 1) * 512],
                                               start=(kt == 0), stop=(kt == 7))
                            return ins
                        E("tensor", f, reads=[b_hT[tgb], b_w2q[wb]], writes=[b_pX[nb_]])
                        sl = slice(tau * D + nb_ * 512, tau * D + (nb_ + 1) * 512)
                        E("vector", lambda X, sl=sl, tau=tau, ex=ex, nb_=nb_: X.scalar_tensor_tensor(
                            accS[:, sl], pX[nb_][:, :],
                            We[:, tau * NE + ex: tau * NE + ex + 1],
                            accS[:, sl], op0=ALU.mult, op1=ALU.add),
                          reads=[b_pX[nb_], b_WeT_tau[tau], b_accT[tau]], writes=[b_accT[tau]])
            it += 1


def kernel(**inputs):
    cfg = CFG()
    nc = build_nc(cfg)
    in_maps = [_per_core_inputs(c, cfg, inputs) for c in range(NCORE)]
    res = run_bass_kernel_spmd(nc, in_maps, core_ids=list(range(NCORE)))
    outs = [np.asarray(res.results[c]["out"], dtype=np.float32) for c in range(NCORE)]
    return np.concatenate(outs, axis=0).reshape(1, T, D)


# version 29
# speedup vs baseline: 1.0772x; 1.0772x over previous
from contextlib import ExitStack

import numpy as np
import ml_dtypes
import concourse.bass as bass
import concourse.mybir as mybir
from concourse.bass_utils import run_bass_kernel_spmd

F32 = mybir.dt.float32
BF16 = mybir.dt.bfloat16
AF = mybir.ActivationFunctionType
ALU = mybir.AluOpType
AX = mybir.AxisListType

D = 1024
H = 16
DK = 64
DV = 128
C = 64
DFF = 4096
NE = 8
NCORE = 8
T = 16384

GAMMA = 1.0 - np.exp2(-5.0 - np.arange(H, dtype=np.float64))

DEBUG_OUTS = False
STAGE = 99


class CFG:
    def __init__(self, tok=T // NCORE, pref=T - T // NCORE, pgrp=8):
        assert tok % 512 == 0 and (pref % 128 == 0)
        self.tok = tok
        self.pref = pref
        self.ntau = tok // 128
        self.nptau = pref // 128
        self.nchunk = tok // C
        self.pgrp = min(pgrp, self.nptau) if self.nptau else 1
        assert self.nptau == 0 or self.nptau % self.pgrp == 0
        self.ngrp = (self.nptau // self.pgrp) if self.nptau else 0
        self.ntg = self.ntau // 4


class _Rec:
    __slots__ = ("eng", "kind", "fn", "idx", "tagged", "val", "slot")

    def __init__(self, eng, kind, fn, idx):
        self.eng, self.kind, self.fn, self.idx = eng, kind, fn, idx
        self.tagged = False
        self.val = 0
        self.slot = -1


class Buf:
    __slots__ = ("name", "wr", "rd")

    def __init__(self, name=""):
        self.name = name
        self.wr = None
        self.rd = []


NDMA = 6


class KB:
    ENGS = ("sync", "gpsimd", "tensor", "vector", "scalar")

    def __init__(self, nc, ctx):
        self.nc = nc
        self.stream = {e: [] for e in self.ENGS}
        self.nrec = {e: 0 for e in self.ENGS}
        self.waited = {e: {} for e in self.ENGS}
        self.sems = {}
        for e in self.ENGS:
            self.sems[(e, "c")] = ctx.enter_context(nc.semaphore(f"s_{e}_c"))
            for i in range(NDMA):
                self.sems[(e, "d", i)] = ctx.enter_context(nc.semaphore(f"s_{e}_d{i}"))
        self.dma_rr = {e: 0 for e in self.ENGS}
        self.dma_uses = {e: [0] * NDMA for e in self.ENGS}
        self.dma_last = {e: [None] * NDMA for e in self.ENGS}

    def _key(self, rec):
        return (rec.eng, "c") if rec.kind == "c" else (rec.eng, "d", rec.slot)

    def _wait(self, eng, dep: _Rec):
        k = self._key(dep)
        w = self.waited[eng]
        if w.get(k, -1) >= dep.idx:
            return
        w[k] = dep.idx
        dep.tagged = True
        self.stream[eng].append(("wait", dep))

    def emit(self, eng, fn, reads=(), writes=(), kind="c"):
        if kind == "d":
            slot = self.dma_rr[eng] % NDMA
            self.dma_rr[eng] += 1
            prev = self.dma_last[eng][slot]
            if prev is not None:
                self._wait(eng, prev)
        deps = {}
        def add(r):
            k = self._key(r)
            if k not in deps or deps[k].idx < r.idx:
                deps[k] = r
        for b in reads:
            if b.wr is not None:
                add(b.wr)
        for b in writes:
            if b.wr is not None:
                add(b.wr)
            for r in b.rd:
                add(r)
        for r in deps.values():
            self._wait(eng, r)
        rec = _Rec(eng, kind, fn, self.nrec[eng])
        if kind == "d":
            rec.tagged = True
            rec.slot = slot
            self.dma_uses[eng][slot] += 1
            rec.val = self.dma_uses[eng][slot]
            self.dma_last[eng][slot] = rec
        self.nrec[eng] += 1
        self.stream[eng].append(rec)
        for b in reads:
            b.rd.append(rec)
        for b in writes:
            b.wr = rec
            b.rd = []
        return rec

    def barrier(self):
        lasts = []
        for e in self.ENGS:
            seen_c = False
            seen_d = set()
            for item in reversed(self.stream[e]):
                if isinstance(item, _Rec):
                    if item.kind == "c" and not seen_c:
                        lasts.append(item); seen_c = True
                    elif item.kind == "d" and item.slot not in seen_d:
                        lasts.append(item); seen_d.add(item.slot)
                    if seen_c and len(seen_d) == NDMA:
                        break
        for e in self.ENGS:
            for rec in lasts:
                if rec.eng != e:
                    self._wait(e, rec)

    def finalize(self, block):
        for e in self.ENGS:
            cnt = 0
            for item in self.stream[e]:
                if isinstance(item, _Rec) and item.kind == "c" and item.tagged:
                    cnt += 1
                    item.val = cnt

        def mk(e):
            def run(eng_obj):
                for item in self.stream[e]:
                    if isinstance(item, _Rec):
                        ins = item.fn(eng_obj)
                        if item.tagged:
                            if item.kind == "d":
                                ins.then_inc(self.sems[(e, "d", item.slot)], 16)
                            else:
                                ins.then_inc(self.sems[(e, "c")], 1)
                    else:
                        _, dep = item
                        if dep.kind == "d":
                            eng_obj.wait_ge(self.sems[(dep.eng, "d", dep.slot)], dep.val * 16)
                        else:
                            eng_obj.wait_ge(self.sems[(dep.eng, "c")], dep.val)
            return run

        block.sync(mk("sync"))
        block.gpsimd(mk("gpsimd"))
        block.tensor(mk("tensor"))
        block.vector(mk("vector"))
        block.scalar(mk("scalar"))


def _host_constants():
    c = {}
    c["ident"] = np.eye(128, dtype=ml_dtypes.bfloat16)
    inv = 1.0 / 10000.0 ** (np.arange(0, DK, 2, dtype=np.float64) / DK)
    t_all = np.arange(T, dtype=np.float64)
    frq = t_all[:, None] * inv[None, :]
    c["_cos_all"] = np.cos(frq)
    c["_sin_all"] = np.sin(frq)

    i64 = np.arange(C, dtype=np.float64)
    ipos = np.concatenate([i64, i64])
    c["dq128"] = (GAMMA[None, :] ** (ipos[:, None] + 1.0) * (DK ** -0.5)).astype(np.float32)
    c["dkv128"] = (GAMMA[None, :] ** (C - 1.0 - ipos[:, None])).astype(np.float32)

    dm2 = np.zeros((128, H * 64), dtype=np.float32)
    for h in range(H):
        g = GAMMA[h]
        colj = g ** (-1.0 - i64)
        m = np.where(i64[None, :] >= i64[:, None], colj[:, None], 0.0)
        dm2[0:64, h * 64:(h + 1) * 64] = m.astype(np.float32)
        dm2[64:128, h * 64:(h + 1) * 64] = m.astype(np.float32)
    c["dmat2"] = dm2
    c["lamH"] = np.broadcast_to((GAMMA ** C).astype(np.float32)[None, :], (128, H)).copy()
    return c


_CONSTS = _host_constants()


def _per_core_inputs(core, cfg: CFG, full):
    tok, pref = cfg.tok, cfg.pref
    t0 = core * tok
    x = np.ascontiguousarray(np.asarray(full["L0_attn_x"], dtype=np.float32).reshape(-1, D))
    d = {}
    d["x_loc"] = np.ascontiguousarray(x[t0:t0 + tok])
    xp = np.zeros((pref, D), dtype=np.float32)
    nreal = min(t0, pref)
    if nreal > 0:
        xp[pref - nreal:] = x[t0 - nreal:t0]
    d["x_pref"] = xp

    def tab(a, n):
        return np.ascontiguousarray(
            a.reshape(n, 128, 32).transpose(1, 0, 2).reshape(128, n * 32).astype(np.float32))

    cos_all, sin_all = _CONSTS["_cos_all"], _CONSTS["_sin_all"]
    d["cos_loc"] = tab(cos_all[t0:t0 + tok], cfg.ntau)
    d["sin_loc"] = tab(sin_all[t0:t0 + tok], cfg.ntau)
    if cfg.nptau:
        cp = np.zeros((pref, DK // 2), dtype=np.float64)
        sp = np.zeros((pref, DK // 2), dtype=np.float64)
        if nreal > 0:
            cp[pref - nreal:] = cos_all[t0 - nreal:t0]
            sp[pref - nreal:] = sin_all[t0 - nreal:t0]
        d["cos_pref"] = tab(cp, cfg.nptau)
        d["sin_pref"] = tab(sp, cfg.nptau)
    else:
        d["cos_pref"] = np.zeros((128, 32), np.float32)
        d["sin_pref"] = np.zeros((128, 32), np.float32)

    gt = cfg.pgrp * 128
    u = np.arange(gt, dtype=np.float64)
    w = GAMMA[None, :] ** (gt - 1.0 - u[:, None])
    d["wpref"] = np.ascontiguousarray(
        w.reshape(cfg.pgrp, 128, H).transpose(1, 0, 2).reshape(128, cfg.pgrp * H).astype(np.float32))
    d["lamgH"] = np.broadcast_to((GAMMA ** gt).astype(np.float32)[None, :], (128, H)).copy()

    for k in ("ident", "dq128", "dkv128", "dmat2", "lamH"):
        d[k] = _CONSTS[k]
    d["dmat2"] = _CONSTS["dmat2"].astype(ml_dtypes.bfloat16)
    for k in ("L0_attn_Wq", "L0_attn_Wk", "L0_attn_Wv", "L0_attn_Wg", "L0_attn_Wo"):
        d[k] = np.ascontiguousarray(np.asarray(full[k], dtype=np.float32))
    gn = np.asarray(full["L0_attn_g_norm_w"], dtype=np.float32)
    d["gnwd"] = np.ascontiguousarray(np.broadcast_to(gn[None, :], (128, DV)).copy())
    d["gncol"] = np.ascontiguousarray(gn.reshape(128, 1))
    d["gate_wT"] = np.ascontiguousarray(np.asarray(full["L0_ffn_gate_w"], dtype=np.float32).T)
    d["gate_b"] = np.ascontiguousarray(np.asarray(full["L0_ffn_gate_b"], dtype=np.float32).reshape(1, NE))
    d["w1"] = np.ascontiguousarray(np.asarray(full["L0_ffn_w1"], dtype=np.float32))
    d["w2"] = np.ascontiguousarray(np.asarray(full["L0_ffn_w2"], dtype=np.float32))
    b1 = np.asarray(full["L0_ffn_b1"], dtype=np.float32)
    d["b1L"] = np.ascontiguousarray(b1.reshape(NE, 32, 128).transpose(2, 0, 1).reshape(128, NE * 32))
    d["b2"] = np.ascontiguousarray(np.asarray(full["L0_ffn_b2"], dtype=np.float32))
    d["onesb"] = np.ones((1, 128), dtype=ml_dtypes.bfloat16)
    return d


def build_nc(cfg: CFG):
    nc = bass.Bass("TRN2", target_bir_lowering=False)
    tok, pref = cfg.tok, cfg.pref

    ap = {}

    def din(name, shape, dt=F32):
        ap[name] = nc.dram_tensor(name, list(shape), dt, kind="ExternalInput").ap()

    din("x_loc", (tok, D)); din("x_pref", (pref, D))
    din("cos_loc", (128, cfg.ntau * 32)); din("sin_loc", (128, cfg.ntau * 32))
    din("cos_pref", (128, max(cfg.nptau, 1) * 32)); din("sin_pref", (128, max(cfg.nptau, 1) * 32))
    din("wpref", (128, cfg.pgrp * H)); din("lamgH", (128, H))
    din("ident", (128, 128), BF16)
    din("dq128", (128, H)); din("dkv128", (128, H)); din("dmat2", (128, H * 64), BF16)
    din("lamH", (128, H))
    din("L0_attn_Wq", (D, D)); din("L0_attn_Wk", (D, D))
    din("L0_attn_Wv", (D, 2 * D)); din("L0_attn_Wg", (D, 2 * D)); din("L0_attn_Wo", (2 * D, D))
    din("gnwd", (128, DV)); din("gncol", (128, 1))
    din("gate_wT", (D, NE)); din("gate_b", (1, NE))
    din("w1", (NE, D, DFF)); din("w2", (NE, DFF, D))
    din("b1L", (128, NE * 32)); din("b2", (NE, D))
    din("onesb", (1, 128), BF16)

    out = nc.dram_tensor("out", [tok, D], F32, kind="ExternalOutput").ap()
    okind = "ExternalOutput" if DEBUG_OUTS else "Internal"
    q_s = nc.dram_tensor("q_s", [tok, D], BF16, kind=okind).ap()
    k_s = nc.dram_tensor("k_s", [tok, D], BF16, kind=okind).ap()
    v_s = nc.dram_tensor("v_s", [tok, 2 * D], BF16, kind=okind).ap()
    g_s = nc.dram_tensor("g_s", [tok, 2 * D], F32, kind=okind).ap()
    s_dbg = nc.dram_tensor("s_dbg", [64, H * DV], F32, kind=okind).ap()
    acc_dbg = nc.dram_tensor("acc_dbg", [128, cfg.ntau * D], F32, kind=okind).ap()

    b_qs, b_ks, b_vs, b_gs = Buf("q_s"), Buf("k_s"), Buf("v_s"), Buf("g_s")

    with ExitStack() as ctx:
        e = ctx.enter_context
        kb = KB(nc, ctx)
        E = kb.emit

        def dma(eng, dst, src, bufs_w=(), bufs_r=()):
            return E(eng, lambda X, d=dst, s=src: X.dma_start(out=d, in_=s),
                     reads=list(bufs_r), writes=list(bufs_w), kind="d")

        ident = e(nc.sbuf_tensor("identS", [128, 128], BF16)); b_ident = Buf("ident")
        Sf = e(nc.sbuf_tensor("SfS", [128, H * DV], F32)); b_Sf = Buf("Sf")
        We = e(nc.sbuf_tensor("WeS", [128, cfg.ntau * NE], F32)); b_We = Buf()
        cst = e(nc.sbuf_tensor("cstS", [128, 4], F32)); b_cst = Buf("cst")
        dma("sync", ident[:], ap["ident"], [b_ident])
        E("gpsimd", lambda X: X.memset(cst[:, 0:1], 1e-5), writes=[b_cst])
        E("gpsimd", lambda X: X.memset(cst[:, 1:2], 1.0 / DV), writes=[b_cst])
        cEPS = cst[:, 0:1]

        with ExitStack() as p0:
            e0 = p0.enter_context
            cosL = e0(nc.sbuf_tensor("cosLS", [128, cfg.ntau * 32], F32)); b_cosL = Buf()
            sinL = e0(nc.sbuf_tensor("sinLS", [128, cfg.ntau * 32], F32)); b_sinL = Buf()
            wq = e0(nc.sbuf_tensor("wqS", [128, 8 * D], BF16)); b_wq = Buf()
            wk = e0(nc.sbuf_tensor("wkS", [128, 8 * D], BF16)); b_wk = Buf()
            wv = e0(nc.sbuf_tensor("wvS", [128, 8 * 2 * D], BF16)); b_wv = Buf()
            wg = e0(nc.sbuf_tensor("wgS", [128, 8 * 2 * D], BF16)); b_wg = Buf()
            cosP = [e0(nc.sbuf_tensor(f"cosPS{i}", [128, cfg.pgrp * 32], F32)) for i in range(2)]
            sinP = [e0(nc.sbuf_tensor(f"sinPS{i}", [128, cfg.pgrp * 32], F32)) for i in range(2)]
            b_cosP = [Buf() for _ in range(2)]
            b_sinP = [Buf() for _ in range(2)]
            wpref = e0(nc.sbuf_tensor("wprefS", [128, cfg.pgrp * H], F32)); b_wpref = Buf()
            lamgH = e0(nc.sbuf_tensor("lamgHS", [128, H], F32)); b_lamg = Buf()

            dma("sync", cosL[:], ap["cos_loc"], [b_cosL])
            dma("sync", sinL[:], ap["sin_loc"], [b_sinL])

            dma("sync", wpref[:], ap["wpref"], [b_wpref])
            dma("sync", lamgH[:], ap["lamgH"], [b_lamg])
            for dst, bf, nm in ((wk, b_wk, "L0_attn_Wk"), (wv, b_wv, "L0_attn_Wv"),
                                (wq, b_wq, "L0_attn_Wq"), (wg, b_wg, "L0_attn_Wg")):
                dma("gpsimd", dst[:].rearrange("p (a n) -> p a n", a=8),
                    ap[nm].rearrange("(a p) n -> p a n", p=128), [bf])

            NB = 2
            xt = [e0(nc.sbuf_tensor(f"xt{i}", [128, D], F32)) for i in range(NB)]
            b_xt = [Buf(f"xt{i}") for i in range(NB)]
            lnx = [e0(nc.sbuf_tensor(f"lnx{i}", [128, D], BF16)) for i in range(NB)]
            b_lnx = [Buf() for _ in range(NB)]
            lnxT = [e0(nc.sbuf_tensor(f"lnxT{i}", [128, D], BF16)) for i in range(NB)]
            b_lnxT = [Buf() for _ in range(NB)]

            xsw = [e0(nc.sbuf_tensor(f"xsw{i}", [128, D], F32)) for i in range(2)]
            b_xsw = [Buf() for _ in range(2)]
            kf_one = e0(nc.sbuf_tensor("kfS", [128, D], F32)); b_kf_one = Buf()
            kf = [kf_one, kf_one]; b_kf = [b_kf_one, b_kf_one]
            qf, b_qf = kf, b_kf
            qro = [e0(nc.sbuf_tensor(f"qro{i}", [128, D], BF16)) for i in range(NB)]
            b_qro = [Buf() for _ in range(NB)]
            kro = [e0(nc.sbuf_tensor(f"kro{i}", [128, D], BF16)) for i in range(NB)]
            b_kro = [Buf() for _ in range(NB)]
            rt1 = [e0(nc.sbuf_tensor(f"rt1S{i}", [128, D], F32)) for i in range(NB)]
            rt2 = [e0(nc.sbuf_tensor(f"rt2S{i}", [128, D], F32)) for i in range(NB)]
            b_rt1 = [Buf() for _ in range(NB)]
            b_rt2 = [Buf() for _ in range(NB)]
            vbf = [e0(nc.sbuf_tensor(f"vbf{i}", [128, 2 * D], BF16)) for i in range(NB)]
            b_vbf = [Buf() for _ in range(NB)]
            gf = [e0(nc.sbuf_tensor(f"gf{i}", [128, 2 * D], F32)) for i in range(NB)]
            b_gf = [Buf() for _ in range(NB)]
            kdw = [e0(nc.sbuf_tensor(f"kdw{i}", [128, D], BF16)) for i in range(NB)]
            b_kdw = [Buf() for _ in range(NB)]

            pT2 = [e0(nc.psum_tensor(f"pT{i}", [128, 1024], BF16)) for i in range(2)]
            b_pT2 = [Buf("pT0"), Buf("pT1")]
            pT_ctr = [0]
            pP = e0(nc.psum_tensor("pP", [128, 1024], F32))
            b_pP = [Buf("pP0"), Buf("pP1")]
            qbank = [0]

            def next_bank():
                qb = qbank[0] % 2
                qbank[0] += 1
                return qb

            def make_sweep(sched):
                NS = len(sched)
                lnsm = e0(nc.sbuf_tensor("lnsm", [128, NS], F32))
                lnsq = e0(nc.sbuf_tensor("lnsq", [128, NS], F32))
                lnmn = e0(nc.sbuf_tensor("lnmn", [128, NS], F32))
                lnmg = e0(nc.sbuf_tensor("lnmg", [128, NS], F32))
                lnvr = e0(nc.sbuf_tensor("lnvr", [128, NS], F32))
                lnrs = e0(nc.sbuf_tensor("lnrs", [128, NS], F32))
                lnnb = e0(nc.sbuf_tensor("lnnb", [128, NS], F32))
                ngrp16 = (NS + 15) // 16
                b_g16 = [Buf(f"lng{g}") for g in range(ngrp16)]

                def sweep(slot):
                    kind, tau = sched[slot]
                    b = slot % 2
                    src = ap["x_pref"] if kind == "P" else ap["x_loc"]
                    dma("sync", xsw[b][:], src[tau * 128:(tau + 1) * 128, :], [b_xsw[b]])
                    g = slot // 16
                    E("scalar", lambda X: X.activation(xsw[b][:], xsw[b][:], AF.Identity,
                                                       accum_out=lnsm[:, slot:slot + 1]),
                      reads=[b_xsw[b]], writes=[b_xsw[b], b_g16[g]])
                    E("scalar", lambda X: X.activation(xsw[b][:], xsw[b][:], AF.Square,
                                                       accum_out=lnsq[:, slot:slot + 1]),
                      reads=[b_xsw[b]], writes=[b_xsw[b], b_g16[g]])
                    if slot == min((g + 1) * 16, NS) - 1:
                        c = slice(g * 16, min((g + 1) * 16, NS))
                        E("vector", lambda X: (X.tensor_scalar_mul(lnmn[:, c], lnsm[:, c], 1.0 / D),
                                               X.tensor_scalar_mul(lnmg[:, c], lnsm[:, c], -1.0 / D),
                                               X.tensor_scalar_mul(lnvr[:, c], lnsq[:, c], 1.0 / D))[-1],
                          reads=[b_g16[g]], writes=[b_g16[g]])
                        E("vector", lambda X: X.tensor_mul(lnsq[:, c], lnmn[:, c], lnmg[:, c]),
                          reads=[b_g16[g]], writes=[b_g16[g]])
                        E("vector", lambda X: X.tensor_add(lnvr[:, c], lnvr[:, c], lnsq[:, c]),
                          reads=[b_g16[g]], writes=[b_g16[g]])
                        E("scalar", lambda X: X.activation(lnrs[:, c], lnvr[:, c], AF.Sqrt, bias=cEPS),
                          reads=[b_g16[g], b_cst], writes=[b_g16[g]])
                        E("vector", lambda X: X.reciprocal(lnrs[:, c], lnrs[:, c]),
                          reads=[b_g16[g]], writes=[b_g16[g]])
                        E("vector", lambda X: X.tensor_mul(lnnb[:, c], lnmg[:, c], lnrs[:, c]),
                          reads=[b_g16[g]], writes=[b_g16[g]])

                def apply_ln(slot, buf, b_src, src_ap_2d):
                    g = slot // 16
                    E("scalar", lambda X: X.activation(lnx[buf][:], src_ap_2d, AF.Identity,
                                                       bias=lnnb[:, slot:slot + 1],
                                                       scale=lnrs[:, slot:slot + 1]),
                      reads=[b_src, b_g16[g]], writes=[b_lnx[buf]])

                return sweep, apply_ln

            def transpose8(b_dst, dst_ap, src_sb, b_src):
                pb = pT_ctr[0] % 2
                pT_ctr[0] += 1
                pT, b_pT = pT2[pb], b_pT2[pb]
                E("tensor", lambda X, pT=pT: [X.transpose(pT[:, b * 128:(b + 1) * 128],
                                                   src_sb[:, b * 128:(b + 1) * 128], ident[:])
                                       for b in range(8)][-1],
                  reads=[b_src, b_ident], writes=[b_pT])
                E("scalar", lambda X, pT=pT: X.activation(dst_ap, pT[:, 0:1024], AF.Identity),
                  reads=[b_pT], writes=[b_dst])

            def projQ(lnxT_, b_lnxT_, w_sb, b_w, wcol0, wstride, qb):
                def f(X):
                    for kt in range(8):
                        ins = X.matmul(pP[:, qb * 512:(qb + 1) * 512],
                                       lnxT_[:, kt * 128:(kt + 1) * 128],
                                       w_sb[:, kt * wstride + wcol0: kt * wstride + wcol0 + 512],
                                       start=(kt == 0), stop=(kt == 7))
                    return ins
                E("tensor", f, reads=[b_lnxT_, b_w], writes=[b_pP[qb]])

            def rope(buf, dst, b_dst, srcf, b_src, tabc, b_tc, tabs, b_ts, col):
                q3 = srcf[:].rearrange("p (h d) -> p h d", h=H)
                o3 = dst[:].rearrange("p (h d) -> p h d", h=H)
                t3a = rt1[buf][:].rearrange("p (h d) -> p h d", h=H)
                t3b = rt2[buf][:].rearrange("p (h d) -> p h d", h=H)
                cs = tabc[:, col:col + 32].unsqueeze(1).broadcast_to((128, H, 32))
                sn = tabs[:, col:col + 32].unsqueeze(1).broadcast_to((128, H, 32))
                E("vector", lambda X: (X.tensor_tensor(t3a[:, :, 0:32], q3[:, :, 0:32], cs, ALU.mult),
                                       X.tensor_tensor(t3a[:, :, 32:64], q3[:, :, 32:64], sn, ALU.mult))[-1],
                  reads=[b_src, b_tc, b_ts], writes=[b_rt1[buf]])
                E("vector", lambda X: (X.tensor_tensor(t3b[:, :, 0:32], q3[:, :, 0:32], sn, ALU.mult),
                                       X.tensor_tensor(t3b[:, :, 32:64], q3[:, :, 32:64], cs, ALU.mult))[-1],
                  reads=[b_src, b_tc, b_ts], writes=[b_rt2[buf]])
                E("vector", lambda X: (X.tensor_sub(o3[:, :, 0:32], t3a[:, :, 0:32], t3a[:, :, 32:64]),
                                       X.tensor_add(o3[:, :, 32:64], t3b[:, :, 0:32], t3b[:, :, 32:64]))[-1],
                  reads=[b_rt1[buf], b_rt2[buf]], writes=[b_dst])

            E("gpsimd", lambda X: X.memset(Sf[:], 0.0), writes=[b_Sf])
            with ExitStack() as p0a:
                pKV = p0a.enter_context(nc.psum_tensor("pKV", [128, H * DV], F32))
                b_pKV = Buf("pKV")
                pending_kv = [None]

                def emit_pending_kv():
                    if pending_kv[0] is not None:
                        fkv, bufs = pending_kv[0]
                        E("tensor", fkv, reads=bufs, writes=[b_pKV])
                        pending_kv[0] = None

                sched = []
                if cfg.nptau and STAGE >= 0:
                    ratio = max(1, cfg.nptau // max(cfg.ntau, 1))
                    li = 0
                    for pi in range(cfg.nptau):
                        sched.append(("P", pi))
                        if STAGE >= 1 and (pi + 1) % ratio == 0 and li < cfg.ntau:
                            sched.append(("L", li))
                            li += 1
                    while STAGE >= 1 and li < cfg.ntau:
                        sched.append(("L", li))
                        li += 1
                elif STAGE >= 1:
                    sched = [("L", i) for i in range(cfg.ntau)]

                def ptab(g):
                    gb_ = g % 2
                    dma("sync", cosP[gb_][:],
                        ap["cos_pref"][:, g * cfg.pgrp * 32:(g + 1) * cfg.pgrp * 32],
                        [b_cosP[gb_]])
                    dma("sync", sinP[gb_][:],
                        ap["sin_pref"][:, g * cfg.pgrp * 32:(g + 1) * cfg.pgrp * 32],
                        [b_sinP[gb_]])

                if cfg.ngrp:
                    ptab(0)

                sweep, apply_ln = make_sweep(sched)
                LOOK = 24

                def stageA(pos):
                    kind, tau = sched[pos]
                    buf = pos % NB
                    src = ap["x_pref"] if kind == "P" else ap["x_loc"]
                    dma("sync", xt[buf][:], src[tau * 128:(tau + 1) * 128, :], [b_xt[buf]])
                    apply_ln(pos, buf, b_xt[buf], xt[buf][:])

                def stageT(pos):
                    buf = pos % NB
                    transpose8(b_lnxT[buf], lnxT[buf][:], lnx[buf], b_lnx[buf])

                def quarters(buf, w_sb, b_w, n, wstride, evac):
                    for qt in range(n):
                        qb = next_bank()
                        projQ(lnxT[buf], b_lnxT[buf], w_sb, b_w, qt * 512, wstride, qb)
                        evac(qt, qb)

                def ev_scalar(dst_sb, b_dst):
                    def f(qt, qb):
                        E("scalar", lambda X, qt=qt, qb=qb: X.activation(
                            dst_sb[:, qt * 512:(qt + 1) * 512],
                            pP[:, qb * 512:(qb + 1) * 512], AF.Identity),
                          reads=[b_pP[qb]], writes=[b_dst])
                    return f

                def ev_vector(dst_sb, b_dst):
                    def f(qt, qb):
                        E("vector", lambda X, qt=qt, qb=qb: X.tensor_copy(
                            dst_sb[:, qt * 512:(qt + 1) * 512],
                            pP[:, qb * 512:(qb + 1) * 512]),
                          reads=[b_pP[qb]], writes=[b_dst])
                    return f

                def ev_mixed(dst_sb, b_dst):
                    def f(qt, qb):
                        if qt % 2 == 0:
                            ev_vector(dst_sb, b_dst)(qt, qb)
                        else:
                            ev_scalar(dst_sb, b_dst)(qt, qb)
                    return f

                def stageP(pos):
                    kind, tau = sched[pos]
                    buf = pos % NB
                    if kind == "P":
                        g, ti = divmod(tau, cfg.pgrp)
                        gb_ = g % 2
                        quarters(buf, wk, b_wk, 2, D, ev_scalar(kf[buf], b_kf[buf]))
                        quarters(buf, wv, b_wv, 4, 2 * D, ev_mixed(vbf[buf], b_vbf[buf]))
                        emit_pending_kv()
                        rope(buf, kro[buf], b_kro[buf], kf[buf], b_kf[buf],
                             cosP[gb_], b_cosP[gb_], sinP[gb_], b_sinP[gb_], ti * 32)
                        def f(X, buf=buf, ti=ti):
                            w3 = wpref[:, ti * H:(ti + 1) * H].unsqueeze(2).broadcast_to((128, H, 64))
                            k3 = kro[buf][:].rearrange("p (h d) -> p h d", h=H)
                            o3 = kdw[buf][:].rearrange("p (h d) -> p h d", h=H)
                            return X.tensor_tensor(o3, k3, w3, ALU.mult)
                        E("vector", f, reads=[b_kro[buf], b_wpref], writes=[b_kdw[buf]])
                        def fkv(X, buf=buf, ti=ti):
                            for h in range(H):
                                ins = X.matmul(pKV[64:128, h * DV:(h + 1) * DV],
                                               kdw[buf][:, h * 64:(h + 1) * 64],
                                               vbf[buf][:, h * DV:(h + 1) * DV],
                                               start=(ti == 0 and h % 4 == 0),
                                               stop=(ti == cfg.pgrp - 1 and h % 4 == 3),
                                               tile_position=(0, 64))
                            return ins
                        pending_kv[0] = (fkv, [b_kdw[buf], b_vbf[buf]])
                        if ti == cfg.pgrp - 1:
                            emit_pending_kv()
                            def f(X):
                                S3 = Sf[64:128, :].rearrange("p (h v) -> p h v", h=H)
                                l3 = lamgH[64:128, :].unsqueeze(2).broadcast_to((64, H, DV))
                                return X.tensor_tensor(S3, S3, l3, ALU.mult)
                            E("vector", f, reads=[b_Sf, b_lamg], writes=[b_Sf])
                            E("vector", lambda X: X.tensor_add(Sf[64:128, :], Sf[64:128, :],
                                                               pKV[64:128, :]),
                              reads=[b_Sf, b_pKV], writes=[b_Sf])
                            if g + 1 < cfg.ngrp:
                                ptab(g + 1)
                    else:
                        emit_pending_kv()
                        quarters(buf, wq, b_wq, 2, D, ev_scalar(qf[buf], b_qf[buf]))
                        rope(buf, qro[buf], b_qro[buf], qf[buf], b_qf[buf],
                             cosL, b_cosL, sinL, b_sinL, tau * 32)
                        dma("sync", q_s[tau * 128:(tau + 1) * 128, :], qro[buf][:],
                            [b_qs], [b_qro[buf]])
                        quarters(buf, wk, b_wk, 2, D, ev_scalar(kf[buf], b_kf[buf]))
                        rope(buf, kro[buf], b_kro[buf], kf[buf], b_kf[buf],
                             cosL, b_cosL, sinL, b_sinL, tau * 32)
                        dma("sync", k_s[tau * 128:(tau + 1) * 128, :], kro[buf][:],
                            [b_ks], [b_kro[buf]])
                        quarters(buf, wv, b_wv, 4, 2 * D, ev_mixed(vbf[buf], b_vbf[buf]))
                        dma("sync", v_s[tau * 128:(tau + 1) * 128, :], vbf[buf][:],
                            [b_vs], [b_vbf[buf]])
                        quarters(buf, wg, b_wg, 4, 2 * D, ev_scalar(gf[buf], b_gf[buf]))
                        dma("sync", g_s[tau * 128:(tau + 1) * 128, :], gf[buf][:],
                            [b_gs], [b_gf[buf]])

                if sched:
                    for sl in range(min(LOOK, len(sched))):
                        sweep(sl)
                    stageA(0)
                    if len(sched) > 1:
                        stageA(1)
                    stageT(0)
                    for pos in range(len(sched)):
                        if pos + LOOK < len(sched):
                            sweep(pos + LOOK)
                        if pos + 2 < len(sched):
                            stageA(pos + 2)
                        if pos + 1 < len(sched):
                            stageT(pos + 1)
                        stageP(pos)
                emit_pending_kv()
                kb.barrier()

            if DEBUG_OUTS:
                dma("sync", s_dbg, Sf[64:128, :], [Buf()], [b_Sf])
            kb.barrier()

        accS = e(nc.sbuf_tensor("accS", [128, cfg.ntau * D], F32))
        b_accT = [Buf(f"acc{t}") for t in range(cfg.ntau)]
        ln2T = e(nc.sbuf_tensor("ln2TS", [128, 8 * cfg.ntau * 128], BF16))
        b_ln2T = [Buf(f"ln2T{g}") for g in range(cfg.ntg)]
        WeT = None
        b_WeTf = Buf("WeT")
        b_WeT_tau = [Buf(f"We{t}") for t in range(cfg.ntau)]
        punits = 0
        if STAGE >= 2:
            with ExitStack() as p1:
                punits = _retention(nc, p1.enter_context, kb, cfg, ap, q_s, b_qs, k_s, b_ks,
                                    v_s, b_vs, g_s, b_gs, ident, b_ident, Sf, b_Sf, accS, b_accT,
                                    cst, b_cst, ln2T, b_ln2T, We, b_WeT_tau, WeT, b_WeTf)
                kb.barrier()

        if STAGE >= 3:
            with ExitStack() as p23:
                _ffn(nc, p23.enter_context, kb, cfg, ap, accS, b_accT, ident, b_ident,
                     We, b_WeT_tau, WeT, b_WeTf, ln2T, b_ln2T, cst, b_cst, punits)
                kb.barrier()

        b_out = Buf("out")
        if STAGE >= 2:
            if DEBUG_OUTS:
                dma("sync", acc_dbg, accS[:], [Buf()], b_accT)
            dma("sync", out.rearrange("(tau p) d -> p tau d", p=128),
                accS[:].rearrange("p (tau d) -> p tau d", d=D), [b_out], b_accT)
        else:
            E("gpsimd", lambda X: X.memset(accS[:], 0.0), writes=b_accT)
            dma("sync", out.rearrange("(tau p) d -> p tau d", p=128),
                accS[:].rearrange("p (tau d) -> p tau d", d=D), [b_out], b_accT)
        kb._wait("sync", b_out.wr)

        block = e(nc.Block())
        kb.finalize(block)
    return nc



def _router(kb, env, tau, buf):
    E = kb.emit
    TB = env["TB"]
    accS, b_accT = env["accS"], env["b_accT"]
    ln2T, b_ln2T = env["ln2T"], env["b_ln2T"]
    We, b_WeT_tau = env["We"], env["b_WeT_tau"]
    ident, b_ident = env["ident"], env["b_ident"]
    cst, b_cst = env["cst"], env["b_cst"]
    st = env["st"]; b_st = env["b_st"][buf]
    ln2, b_ln2 = env["ln2"][buf], env["b_ln2"][buf]
    dmp, b_dmp = env["dmp"], env["b_dmp"]
    pT, b_pT = env["pT"], env["b_pT"]
    pG, b_pG = env["pG"], env["b_pG"]
    gts, b_gts = env["gts"], env["b_gts"]
    msk, b_msk = env["msk"], env["b_msk"]
    eqs, b_eqs = env["eqs"], env["b_eqs"]
    Webf, b_Webf = env["Webf"], env["b_Webf"]
    WeTs, b_WeTf = env["WeTs"], env["b_WeTf"]
    gwT, b_gwT = env["gwT"], env["b_gwT"]
    gb, b_gb = env["gb"], env["b_gb"]
    onesb, b_ones = env["onesb"], env["b_ones"]
    b2bf, b_b2 = env["b2bf"], env["b_b2"]
    b_m12 = env["b_m12"][buf]
    xa = accS[:, tau * D:(tau + 1) * D]
    sc = 16 + 32 * buf
    sm = st[:, sc:sc + 1]; sq = st[:, sc + 1:sc + 2]
    mean = st[:, sc + 2:sc + 3]; var = st[:, sc + 3:sc + 4]
    rstd = st[:, sc + 4:sc + 5]; nb = st[:, sc + 5:sc + 6]
    tg = tau // 4
    E("scalar", lambda X: (X.activation(dmp[0], xa, AF.Square, accum_out=sq),
                           X.activation(dmp[1], xa, AF.Identity, accum_out=sm))[-1],
      reads=[b_accT[tau]], writes=list(b_dmp) + [b_st])
    E("vector", lambda X: (X.tensor_scalar_mul(mean, sm, 1.0 / D),
                           X.tensor_scalar_mul(var, sq, 1.0 / D))[-1],
      reads=[b_st], writes=[b_st])
    E("vector", lambda X: X.tensor_mul(nb, mean, mean),
      reads=[b_st], writes=[b_st])
    E("vector", lambda X: X.tensor_sub(var, var, nb),
      reads=[b_st], writes=[b_st])
    E("scalar", lambda X: X.activation(rstd, var, AF.Sqrt, bias=cst[:, 0:1]),
      reads=[b_st, b_cst], writes=[b_st])
    E("vector", lambda X: X.reciprocal(rstd, rstd),
      reads=[b_st], writes=[b_st])
    E("vector", lambda X: X.tensor_mul(nb, mean, rstd),
      reads=[b_st], writes=[b_st])
    E("vector", lambda X: X.tensor_scalar_mul(nb, nb, -1.0),
      reads=[b_st], writes=[b_st])
    E("scalar", lambda X: X.activation(ln2[:], xa, AF.Identity, bias=nb, scale=rstd),
      reads=[b_accT[tau], b_st], writes=[b_ln2])
    E("tensor", lambda X: [X.transpose(pT[:, b * 128:(b + 1) * 128],
                                       ln2[:, b * 128:(b + 1) * 128], ident[:])
                           for b in range(8)][-1],
      reads=[b_ln2, b_ident], writes=[b_pT])
    def f(X, tau=tau):
        dst = ln2T[:].rearrange("p (kt t) -> p kt t", kt=8)
        src = pT[:, 0:1024].rearrange("p (kt t) -> p kt t", kt=8)
        return X.activation(dst[:, :, tau * 128:(tau + 1) * 128], src, AF.Identity)
    E("scalar", f, reads=[b_pT], writes=[b_ln2T[tg]])
    def f(X, tau=tau):
        for kt in range(8):
            X.matmul(pG[:, 0:NE], ln2T[:, kt * TB + tau * 128: kt * TB + (tau + 1) * 128],
                     gwT[:, kt * NE:(kt + 1) * NE], start=(kt == 0), stop=False)
        return X.matmul(pG[:, 0:NE], onesb[0:1, :], gb[0:1, :], start=False, stop=True)
    E("tensor", f, reads=[b_ln2T[tg], b_gwT, b_ones, b_gb], writes=[b_pG])
    mc = 22 + 32 * buf
    m1 = st[:, mc:mc + 1]; m2 = st[:, mc + 1:mc + 2]; p1 = st[:, mc + 2:mc + 3]
    p2v = st[:, mc + 3:mc + 4]; dd = st[:, mc + 4:mc + 5]
    eq1 = eqs[:, 0:NE]; eq2 = eqs[:, NE:2 * NE]
    E("vector", lambda X: X.tensor_copy(gts[:], pG[:, 0:NE]),
      reads=[b_pG], writes=[b_gts])
    E("vector", lambda X: X.reduce_max(m1, gts[:], AX.X),
      reads=[b_gts], writes=[b_m12])
    E("vector", lambda X: X.tensor_single_scalar(eq1, gts[:], m1, ALU.is_equal),
      reads=[b_gts, b_m12], writes=[b_eqs])
    E("vector", lambda X: X.scalar_tensor_tensor(msk[:], eq1, -1e30, gts[:],
                                                 op0=ALU.mult, op1=ALU.add),
      reads=[b_eqs, b_gts], writes=[b_msk])
    E("vector", lambda X: X.reduce_max(m2, msk[:], AX.X),
      reads=[b_msk], writes=[b_m12])
    E("vector", lambda X: X.tensor_single_scalar(eq2, msk[:], m2, ALU.is_equal),
      reads=[b_msk, b_m12], writes=[b_eqs])
    E("vector", lambda X: X.tensor_sub(dd, m1, m2),
      reads=[b_m12], writes=[b_m12])
    E("scalar", lambda X: X.activation(p1, dd, AF.Sigmoid),
      reads=[b_m12], writes=[b_m12])
    E("vector", lambda X: X.tensor_scalar(p2v, p1, -1.0, 1.0, op0=ALU.mult, op1=ALU.add),
      reads=[b_m12], writes=[b_m12])
    E("vector", lambda X, tau=tau: X.tensor_single_scalar(
        We[:, tau * NE:(tau + 1) * NE], eq1, p1, ALU.mult),
      reads=[b_eqs, b_m12], writes=[b_WeT_tau[tau]])
    E("vector", lambda X, tau=tau: X.scalar_tensor_tensor(
        We[:, tau * NE:(tau + 1) * NE], eq2, p2v, We[:, tau * NE:(tau + 1) * NE],
        op0=ALU.mult, op1=ALU.add),
      reads=[b_eqs, b_m12, b_WeT_tau[tau]], writes=[b_WeT_tau[tau]])
    E("vector", lambda X, tau=tau: X.tensor_copy(Webf[:], We[:, tau * NE:(tau + 1) * NE]),
      reads=[b_WeT_tau[tau]], writes=[b_Webf])
    E("tensor", lambda X: X.transpose(pT[0:NE, 0:128], Webf[:], ident[:]),
      reads=[b_Webf, b_ident], writes=[b_pT])
    E("scalar", lambda X: X.activation(WeTs[0:NE, :], pT[0:NE, 0:128], AF.Identity),
      reads=[b_pT], writes=[b_WeTf])
    def f(X):
        for nb_ in range(2):
            ins = X.matmul(pG[:, nb_ * 512:(nb_ + 1) * 512],
                           WeTs[0:NE, :],
                           b2bf[0:NE, nb_ * 512:(nb_ + 1) * 512],
                           start=True, stop=True)
        return ins
    E("tensor", f, reads=[b_WeTf, b_b2], writes=[b_pG])
    E("vector", lambda X, tau=tau: X.tensor_add(accS[:, tau * D:(tau + 1) * D],
                                                accS[:, tau * D:(tau + 1) * D], pG[:, :]),
      reads=[b_accT[tau], b_pG], writes=[b_pG, b_accT[tau]])


def _retention(nc, e, kb, cfg, ap, q_s, b_qs, k_s, b_ks, v_s, b_vs, g_s, b_gs,
               ident, b_ident, Sf, b_Sf, accS, b_accT, cst, b_cst,
               ln2T, b_ln2T, We, b_WeT_tau, WeT, b_WeTf):
    E = kb.emit
    NB = 2
    TB = cfg.ntau * 128
    wo = e(nc.sbuf_tensor("woS", [128, 16 * D], BF16)); b_wo = Buf()
    E("gpsimd", lambda X: X.dma_start(out=wo[:].rearrange("p (a n) -> p a n", a=16),
                                      in_=ap["L0_attn_Wo"].rearrange("(a p) n -> p a n", p=128)),
      writes=[b_wo], kind="d")
    dq128 = e(nc.sbuf_tensor("dq128S", [128, H], F32)); b_dq = Buf()
    dkv128 = e(nc.sbuf_tensor("dkv128S", [128, H], F32)); b_dkv = Buf()
    lamH = e(nc.sbuf_tensor("lamHS", [128, H], F32)); b_lam = Buf()
    dmat2 = e(nc.sbuf_tensor("dmat2S", [128, H * 64], BF16)); b_dm = Buf()
    gncol = e(nc.sbuf_tensor("gncolS", [128, 1], F32)); b_gnc = Buf()
    gwT = e(nc.sbuf_tensor("f_gwT", [128, 8 * NE], BF16)); b_gwT = Buf()
    gb = e(nc.sbuf_tensor("f_gb", [1, NE], BF16)); b_gb = Buf()
    onesb = e(nc.sbuf_tensor("f_ones", [1, 128], BF16)); b_ones = Buf()
    b2bf = e(nc.sbuf_tensor("f_b2", [8, D], BF16)); b_b2 = Buf()
    for dst, bf, nm in ((dq128, b_dq, "dq128"), (dkv128, b_dkv, "dkv128"),
                        (lamH, b_lam, "lamH"), (dmat2, b_dm, "dmat2"), (gncol, b_gnc, "gncol")):
        E("sync", lambda X, d=dst, n=nm: X.dma_start(out=d[:], in_=ap[n]),
          writes=[bf], kind="d")
    E("scalar", lambda X: X.activation(wo[:], wo[:], AF.Identity, scale=gncol[:]),
      reads=[b_wo, b_gnc], writes=[b_wo])
    E("gpsimd", lambda X: X.dma_start(out=gwT[:].rearrange("p (a n) -> p a n", a=8),
                                      in_=ap["gate_wT"].rearrange("(a p) n -> p a n", p=128)),
      writes=[b_gwT], kind="d")
    E("gpsimd", lambda X: X.dma_start(out=gb[:], in_=ap["gate_b"]), writes=[b_gb], kind="d")
    E("sync", lambda X: X.dma_start(out=onesb[:], in_=ap["onesb"]), writes=[b_ones], kind="d")
    E("gpsimd", lambda X: X.dma_start(out=b2bf[:], in_=ap["b2"]), writes=[b_b2], kind="d")

    qtk = [e(nc.sbuf_tensor(f"r_q{i}", [128, D], BF16)) for i in range(NB)]
    b_qtk = [Buf() for _ in range(NB)]
    qdq, b_qdq = qtk, b_qtk
    rln2A = e(nc.sbuf_tensor("r_ln2A", [128, D], BF16)); b_rln2A = Buf()
    ktk = [e(nc.sbuf_tensor(f"r_k{i}", [128, D], BF16)) for i in range(2)]
    b_ktk = [Buf() for _ in range(2)]
    kch = e(nc.sbuf_tensor("r_kc", [128, D], BF16)); b_kch = Buf()
    kdc, b_kdc = kch, b_kch
    grc = e(nc.sbuf_tensor("r_g", [128, 2 * D], F32)); b_grc = Buf()
    kTs = e(nc.sbuf_tensor("r_kT", [128, H * 128], BF16)); b_kTs = Buf()
    qTs = e(nc.sbuf_tensor("r_qT", [128, H * 128], BF16)); b_qTs = Buf()
    LT = [e(nc.sbuf_tensor(f"r_LT{i}", [128, H * 64], BF16)) for i in range(2)]
    b_LT = [Buf() for _ in range(2)]
    RT = [e(nc.sbuf_tensor(f"r_RT{i}", [128, H * DV], BF16)) for i in range(2)]
    b_RT = [Buf() for _ in range(2)]
    osb = e(nc.sbuf_tensor("r_o", [128, H * DV], F32)); b_osb = [Buf(), Buf()]
    osq = e(nc.sbuf_tensor("r_osq", [128, H * DV], F32)); b_osq = [Buf(), Buf()]
    o4_one = e(nc.sbuf_tensor("r_o4", [128, H * DV], BF16))
    o4 = [o4_one, o4_one]
    b_o4h = [Buf(), Buf()]
    o3T = e(nc.sbuf_tensor("r_o3T", [128, 16 * 128], BF16))
    b_o3T = [Buf() for _ in range(4)]
    rstt = e(nc.sbuf_tensor("r_rst", [128, 64], F32))
    b_rstt = [[Buf(), Buf()], [Buf(), Buf()]]
    ssq = e(nc.sbuf_tensor("r_ssq", [128, 2 * H], F32))
    ln2, b_ln2 = [rln2A, rln2A], [b_rln2A, b_rln2A]
    stats2 = rstt
    b_stats2 = [Buf() for _ in range(NB)]
    gts = e(nc.sbuf_tensor("f_gt", [128, NE], F32)); b_gts = Buf()
    msk = e(nc.sbuf_tensor("f_msk", [128, NE], F32)); b_msk = Buf()
    b_m12 = Buf()
    eqs = e(nc.sbuf_tensor("f_eq", [128, 2 * NE], F32)); b_eqs = Buf()
    Webf = e(nc.sbuf_tensor("f_Webf", [128, NE], BF16)); b_Webf = Buf()
    WeTs = e(nc.sbuf_tensor("f_WeTs", [8, 128], BF16))

    pTr = e(nc.psum_tensor("pTr", [128, 2048], BF16)); b_pTr = Buf("pTr")
    pA = e(nc.psum_tensor("pA", [128, 1024], F32)); b_pA = Buf("pA")
    pO = e(nc.psum_tensor("pO", [128, 8 * DV], F32))
    b_pOo = Buf("pOo")
    b_pOkv = Buf("pOkv")
    pO3T = e(nc.psum_tensor("pO3T", [128, 2048], BF16))
    b_pO3T = [Buf("pO3T0"), Buf("pO3T1")]

    PUNITS = 0 * min(64, max(0, 2 * (cfg.ntau - 5) - 1) // 2 * 2)
    if PUNITS:
        b1Li = e(nc.sbuf_tensor("i_b1L", [128, NE * 32], F32)); b_b1i = Buf()
        w18 = [e(nc.sbuf_tensor(f"i_w1{i}", [128, 8 * 512], BF16)) for i in range(2)]
        b_w18 = [Buf() for _ in range(2)]
        w28 = [e(nc.sbuf_tensor(f"i_w2{i}", [128, 4 * 1024], BF16)) for i in range(2)]
        b_w28 = [Buf() for _ in range(2)]
        hTi = e(nc.sbuf_tensor("i_hT", [128, 4 * 512], BF16)); b_hTi = Buf()
        pHi = e(nc.psum_tensor("pHi", [128, 512], F32)); b_pHi = Buf("pHi")
        pXi = e(nc.psum_tensor("pXi", [128, 512], F32)); b_pXi = Buf("pXi")
        E("sync", lambda X: X.dma_start(out=b1Li[:], in_=ap["b1L"]), writes=[b_b1i], kind="d")
    TBr = cfg.ntau * 128
    pst = {"load": 0, "comp": 0}

    def emit_load(u):
        wb = u % 2
        ex, h8 = u // 8, u % 8
        E("gpsimd", lambda X, ex=ex, h8=h8, wb=wb: X.dma_start(
            out=w18[wb][:].rearrange("p (a n) -> p a n", a=8),
            in_=ap["w1"][ex, :, h8 * 512:(h8 + 1) * 512]
            .rearrange("(a p) n -> p a n", p=128)),
          writes=[b_w18[wb]], kind="d")
        E("gpsimd", lambda X, ex=ex, h8=h8, wb=wb: X.dma_start(
            out=w28[wb][:].rearrange("p (a n) -> p a n", a=4),
            in_=ap["w2"][ex, h8 * 512:(h8 + 1) * 512, :]
            .rearrange("(a p) n -> p a n", p=128)),
          writes=[b_w28[wb]], kind="d")

    def emit_comp(u):
        wb = u % 2
        ex, h8 = u // 8, u % 8
        for ht in range(4):
            def f(X, wb=wb, ht=ht):
                for kt in range(8):
                    ins = X.matmul(pHi[:, :],
                                   w18[wb][:, kt * 512 + ht * 128: kt * 512 + (ht + 1) * 128],
                                   ln2T[:, kt * TBr: kt * TBr + 512],
                                   start=(kt == 0), stop=(kt == 7))
                return ins
            E("tensor", f, reads=[b_w18[wb], b_ln2T[0]], writes=[b_pHi])
            def f(X, ex=ex, h8=h8, ht=ht):
                bcol = ex * 32 + h8 * 4 + ht
                return X.activation(hTi[:, ht * 512:(ht + 1) * 512], pHi[:, :], AF.Gelu,
                                    bias=b1Li[:, bcol:bcol + 1])
            E("scalar", f, reads=[b_pHi, b_b1i], writes=[b_hTi])
        for m in range(4):
            for nb_ in range(2):
                def f(X, wb=wb, m=m, nb_=nb_):
                    for kt in range(4):
                        ins = X.matmul(pXi[:, :],
                                       hTi[:, kt * 512 + m * 128: kt * 512 + (m + 1) * 128],
                                       w28[wb][:, kt * 1024 + nb_ * 512: kt * 1024 + (nb_ + 1) * 512],
                                       start=(kt == 0), stop=(kt == 3))
                    return ins
                E("tensor", f, reads=[b_hTi, b_w28[wb]], writes=[b_pXi])
                sl = slice(m * D + nb_ * 512, m * D + (nb_ + 1) * 512)
                E("vector", lambda X, sl=sl, m=m, ex=ex: X.scalar_tensor_tensor(
                    accS[:, sl], pXi[:, :], We[:, m * NE + ex: m * NE + ex + 1],
                    accS[:, sl], op0=ALU.mult, op1=ALU.add),
                  reads=[b_pXi, b_WeT_tau[m], b_accT[m]], writes=[b_accT[m]])

    def pump():
        if pst["comp"] < pst["load"] and pst["comp"] < PUNITS:
            emit_comp(pst["comp"]); pst["comp"] += 1
        if pst["load"] < PUNITS and pst["load"] - pst["comp"] < 2:
            emit_load(pst["load"]); pst["load"] += 1


    def ld(dst, src, bw, br=()):
        E("sync", lambda X, d=dst, s=src: X.dma_start(out=d, in_=s),
          reads=list(br), writes=list(bw), kind="d")

    b_m12l = [b_m12, Buf()]
    envA = dict(TB=TB, accS=accS, b_accT=b_accT, ln2T=ln2T, b_ln2T=b_ln2T,
                We=We, b_WeT_tau=b_WeT_tau, ident=ident, b_ident=b_ident,
                cst=cst, b_cst=b_cst, st=rstt, b_st=b_stats2,
                ln2=ln2, b_ln2=b_ln2,
                dmp=[osq[:, 0:1024], osq[:, 1024:2048]], b_dmp=b_osq,
                pT=pTr, b_pT=b_pTr, pG=pA, b_pG=b_pA,
                gts=gts, b_gts=b_gts, msk=msk, b_msk=b_msk, eqs=eqs, b_eqs=b_eqs,
                Webf=Webf, b_Webf=b_Webf, WeTs=WeTs, b_WeTf=b_WeTf,
                gwT=gwT, b_gwT=b_gwT, gb=gb, b_gb=b_gb, onesb=onesb, b_ones=b_ones,
                b2bf=b2bf, b_b2=b_b2, b_m12=b_m12l)

    def p2_router(tau, buf):
        _router(kb, envA, tau, buf)

    def tau_loads(tau):
        buf = tau % NB
        ld(qtk[buf][:], q_s[tau * 128:(tau + 1) * 128, :], [b_qtk[buf]], [b_qs])
        ld(ktk[buf][:], k_s[tau * 128:(tau + 1) * 128, :], [b_ktk[buf]], [b_ks])
        ld(accS[:, tau * D:(tau + 1) * D].rearrange("p (o d) -> p o d", o=1),
           ap["x_loc"][tau * 128:(tau + 1) * 128, :].rearrange("(o t) d -> t o d", o=1),
           [b_accT[tau]])

        def f(X, buf=buf):
            d3 = dq128[:].unsqueeze(2).broadcast_to((128, H, 64))
            q3 = qtk[buf][:].rearrange("p (h d) -> p h d", h=H)
            return X.tensor_tensor(q3, q3, d3, ALU.mult)
        E("vector", f, reads=[b_qtk[buf], b_dq], writes=[b_qdq[buf]])

    tau_loads(0)
    for tau in range(cfg.ntau):
        buf = tau % NB
        if tau + 1 < cfg.ntau:
            tau_loads(tau + 1)

        def fk(X, buf=buf):
            for h in range(H):
                ins = X.transpose(pTr[0:64, h * 128:(h + 1) * 128],
                                  ktk[buf][:, h * 64:(h + 1) * 64], ident[:])
            return ins
        def fq(X, buf=buf):
            for h in range(H):
                ins = X.transpose(pTr[0:64, h * 128:(h + 1) * 128],
                                  qdq[buf][:, h * 64:(h + 1) * 64], ident[:])
            return ins
        def fqup(X, buf=buf):
            for h in range(H):
                ins = X.transpose(pTr[64:128, h * 128:(h + 1) * 128],
                                  qdq[buf][:, h * 64:(h + 1) * 64], ident[:],
                                  tile_position=(0, 64))
            return ins
        E("tensor", lambda X, buf=buf: (fk(X, buf), fqup(X, buf))[-1],
          reads=[b_ktk[buf], b_qdq[buf], b_ident], writes=[b_pTr])
        E("scalar", lambda X: X.activation(kTs[0:64, :], pTr[0:64, 0:H * 128], AF.Identity),
          reads=[b_pTr], writes=[b_kTs])
        for ci_ in range(2):
            def f(X, ci_=ci_):
                src = pTr[64:128, 0:H * 128].rearrange("p (h t) -> p h t", h=H)
                dst = LT[ci_][64:128, :].rearrange("p (h i) -> p h i", h=H)
                return X.activation(dst, src[:, :, ci_ * 64:(ci_ + 1) * 64], AF.Identity)
            E("scalar", f, reads=[b_pTr], writes=[b_LT[ci_]])
        E("tensor", fq, reads=[b_qdq[buf], b_ident], writes=[b_pTr])
        E("scalar", lambda X: X.activation(qTs[0:64, :], pTr[0:64, 0:H * 128], AF.Identity),
          reads=[b_pTr], writes=[b_qTs])

        for ci in range(2):
            ch = 2 * tau + ci
            ld(kch[0:64, :], k_s[ch * 64:(ch + 1) * 64, :], [b_kch], [b_ks])
            ld(RT[ci][0:64, :], v_s[ch * 64:(ch + 1) * 64, :], [b_RT[ci]], [b_vs])
            ld(grc[0:64, :], g_s[ch * 64:(ch + 1) * 64, :], [b_grc], [b_gs])
            E("scalar", lambda X: X.activation(grc[0:64, :], grc[0:64, :], AF.Silu),
              reads=[b_grc], writes=[b_grc])
            E("scalar", lambda X, ci=ci: X.activation(RT[ci][64:128, :], Sf[64:128, :], AF.Identity),
              reads=[b_Sf], writes=[b_RT[ci]])
            def f(X):
                S3 = Sf[64:128, :].rearrange("p (h v) -> p h v", h=H)
                l3 = lamH[64:128, :].unsqueeze(2).broadcast_to((64, H, DV))
                return X.tensor_tensor(S3, S3, l3, ALU.mult)
            E("vector", f, reads=[b_Sf, b_lam], writes=[b_Sf])
            def f(X, ci=ci):
                d3 = dkv128[0:64, :].unsqueeze(2).broadcast_to((64, H, 64))
                k3 = kch[0:64, :].rearrange("p (h d) -> p h d", h=H)
                return X.tensor_tensor(k3, k3, d3, ALU.mult)
            E("vector", f, reads=[b_kch, b_dkv], writes=[b_kdc])
            def f(X, ci=ci):
                for h in range(H):
                    ins = X.matmul(pA[0:64, h * 64:(h + 1) * 64],
                                   kTs[0:64, h * 128 + ci * 64: h * 128 + (ci + 1) * 64],
                                   qTs[0:64, h * 128 + ci * 64: h * 128 + (ci + 1) * 64],
                                   start=(h % 8 == 0), stop=(h % 8 == 7))
                return ins
            E("tensor", f, reads=[b_kTs, b_qTs], writes=[b_pA])
            E("vector", lambda X, ci=ci: X.tensor_tensor(LT[ci][0:64, :], pA[0:64, :],
                                                         dmat2[0:64, :], ALU.mult),
              reads=[b_pA, b_dm], writes=[b_LT[ci]])
            for hh in range(2):
                hs = range(hh * 8, hh * 8 + 8)
                def f(X, ci=ci, hs=hs):
                    for h in hs:
                        ins = X.matmul(pO[0:64, (h % 8) * DV:(h % 8 + 1) * DV],
                                       LT[ci][:, h * 64:(h + 1) * 64],
                                       RT[ci][:, h * DV:(h + 1) * DV],
                                       start=(h % 4 == 0), stop=(h % 4 == 3))
                    return ins
                E("tensor", f, reads=[b_LT[ci], b_RT[ci]], writes=[b_pOo])
                cols = slice(hh * 1024, (hh + 1) * 1024)
                E("scalar", lambda X, cols=cols: X.activation(osb[0:64, cols], pO[0:64, 0:1024],
                                                              AF.Identity),
                  reads=[b_pOo], writes=[b_osb[hh]])
                E("scalar", lambda X, cols=cols: X.activation(osq[0:64, cols], pO[0:64, 0:1024],
                                                              AF.Square),
                  reads=[b_pOo], writes=[b_osq[hh]])
                def f(X, ci=ci, hs=hs):
                    for h in hs:
                        ins = X.matmul(pO[64:128, (h % 8) * DV:(h % 8 + 1) * DV],
                                       kdc[0:64, h * 64:(h + 1) * 64],
                                       RT[ci][0:64, h * DV:(h + 1) * DV],
                                       start=(h % 4 == 0), stop=(h % 4 == 3),
                                       tile_position=(0, 64))
                    return ins
                E("tensor", f, reads=[b_kdc, b_RT[ci]], writes=[b_pOkv])
                E("vector", lambda X, cols=cols: X.tensor_add(Sf[64:128, cols],
                                                              Sf[64:128, cols], pO[64:128, 0:1024]),
                  reads=[b_Sf, b_pOkv], writes=[b_Sf])
            for hh in range(2):
                cols = slice(hh * 1024, (hh + 1) * 1024)
                sq = ssq[:, ci * H + hh * 8: ci * H + (hh + 1) * 8]
                rc = rstt[:, ci * 32 + hh * 8: ci * 32 + (hh + 1) * 8]
                E("vector", lambda X, sq=sq, cols=cols: X.tensor_reduce(
                    sq[0:64, :], osq[0:64, cols].rearrange("p (h v) -> p h v", h=8),
                    AX.X, ALU.add),
                  reads=[b_osq[hh]], writes=[b_rstt[ci][hh]])
                E("scalar", lambda X, sq=sq, rc=rc: X.activation(
                    rc[0:64, :], sq[0:64, :], AF.Sqrt, bias=cst[0:64, 0:1], scale=cst[0:64, 1:2]),
                  reads=[b_rstt[ci][hh], b_cst], writes=[b_rstt[ci][hh]])
                E("vector", lambda X, rc=rc: X.reciprocal(rc[0:64, :], rc[0:64, :]),
                  reads=[b_rstt[ci][hh]], writes=[b_rstt[ci][hh]])
                def f(X, rc=rc, cols=cols):
                    o3 = osb[0:64, cols].rearrange("p (h v) -> p h v", h=8)
                    r3 = rc[0:64, :].unsqueeze(2).broadcast_to((64, 8, DV))
                    return X.tensor_tensor(o3, o3, r3, ALU.mult)
                E("vector", f, reads=[b_osb[hh], b_rstt[ci][hh]], writes=[b_osb[hh]])
                E("vector", lambda X, ci=ci, cols=cols: X.tensor_tensor(
                    o4[ci][0:64, cols], osb[0:64, cols], grc[0:64, cols], ALU.mult),
                  reads=[b_osb[hh], b_grc], writes=[b_o4h[hh]])
                def f(X, ci=ci, hh=hh):
                    for b in range(8):
                        ins = X.transpose(pO3T[:, ci * 1024 + (hh * 8 + b) * 64:
                                               ci * 1024 + (hh * 8 + b + 1) * 64],
                                          o4[ci][0:64, hh * 1024 + b * 128:
                                                 hh * 1024 + (b + 1) * 128],
                                          ident[0:64, 0:64])
                    return ins
                E("tensor", f, reads=[b_o4h[hh], b_ident], writes=[b_pO3T[ci]])
                def f(X, ci=ci, hh=hh):
                    dst = o3T[:, :].rearrange("p (b t) -> p b t", b=16)
                    src = pO3T[:, ci * 1024 + hh * 512: ci * 1024 + (hh + 1) * 512]                         .rearrange("p (b t) -> p b t", b=8)
                    return X.activation(dst[:, hh * 8:(hh + 1) * 8, ci * 64:(ci + 1) * 64],
                                        src, AF.Identity)
                E("scalar", f, reads=[b_pO3T[ci]], writes=[b_o3T[ci * 2 + hh]])
            if tau >= 5 and ci == 0:
                pump()

        def f(X):
            for kt in range(16):
                for nb_ in range(2):
                    ins = X.matmul(pA[:, nb_ * 512:(nb_ + 1) * 512],
                                   o3T[:, kt * 128:(kt + 1) * 128],
                                   wo[:, kt * D + nb_ * 512: kt * D + (nb_ + 1) * 512],
                                   start=(kt == 0), stop=(kt == 15))
            return ins
        E("tensor", f, reads=b_o3T + [b_wo], writes=[b_pA])
        E("vector", lambda X, tau=tau: X.tensor_add(
            accS[:, tau * D:(tau + 1) * D], accS[:, tau * D:(tau + 1) * D], pA[:, :]),
          reads=[b_accT[tau], b_pA], writes=[b_accT[tau]])
        if tau >= 5:
            pump()
        if tau >= 1:
            p2_router(tau - 1, (tau - 1) % NB)
    p2_router(cfg.ntau - 1, (cfg.ntau - 1) % NB)
    while pst["comp"] < PUNITS:
        pump()
    return PUNITS


def _ffn(nc, e, kb, cfg, ap, accS, b_accT, ident, b_ident, We, b_WeT_tau,
         WeT, b_WeTf, ln2T, b_ln2T, cst, b_cst, punits=0):
    E = kb.emit
    TB = cfg.ntau * 128
    b1L = e(nc.sbuf_tensor("f_b1L", [128, NE * 32], F32)); b_b1 = Buf()
    E("sync", lambda X: X.dma_start(out=b1L[:], in_=ap["b1L"]), writes=[b_b1], kind="d")

    w1q = [e(nc.sbuf_tensor(f"f_w1q{i}", [128, 8 * 1024], BF16)) for i in range(2)]
    b_w1q = [Buf() for _ in range(2)]
    w2q = [e(nc.sbuf_tensor(f"f_w2q{i}", [128, 8 * 1024], BF16)) for i in range(2)]
    b_w2q = [Buf() for _ in range(2)]
    hT = [e(nc.sbuf_tensor(f"f_hT{i}", [128, 8 * 512], BF16)) for i in range(2)]
    b_hT = [Buf() for _ in range(2)]
    pH = [e(nc.psum_tensor(f"f_pH{i}", [128, 512], F32)) for i in range(2)]
    b_pH = [Buf() for _ in range(2)]
    pX = [e(nc.psum_tensor(f"f_pX{i}", [128, 512], F32)) for i in range(2)]
    b_pX = [Buf() for _ in range(2)]

    nextr = cfg.ntau
    envB = None
    if False:
        rln2 = [e(nc.sbuf_tensor(f"rb_ln2{i}", [128, D], BF16)) for i in range(2)]
        rst = e(nc.sbuf_tensor("rb_st", [128, 64], F32))
        rdmp = e(nc.sbuf_tensor("rb_dmp", [128, 2048], F32))
        rgts = e(nc.sbuf_tensor("rb_gt", [128, NE], F32))
        rmsk = e(nc.sbuf_tensor("rb_msk", [128, NE], F32))
        reqs = e(nc.sbuf_tensor("rb_eq", [128, 2 * NE], F32))
        rWebf = e(nc.sbuf_tensor("rb_Webf", [128, NE], BF16))
        rWeTs = e(nc.sbuf_tensor("rb_WeTs", [8, 128], BF16))
        rgwT = e(nc.sbuf_tensor("rb_gwT", [128, 8 * NE], BF16)); b_rgwT = Buf()
        rgb = e(nc.sbuf_tensor("rb_gb", [1, NE], BF16)); b_rgb = Buf()
        rones = e(nc.sbuf_tensor("rb_ones", [1, 128], BF16)); b_rones = Buf()
        rb2bf = e(nc.sbuf_tensor("rb_b2", [8, D], BF16)); b_rb2 = Buf()
        pRT = e(nc.psum_tensor("pRT", [128, 1024], BF16))
        pG2 = e(nc.psum_tensor("pG2", [128, 1024], F32))
        E("gpsimd", lambda X: X.dma_start(out=rgwT[:].rearrange("p (a n) -> p a n", a=8),
                                          in_=ap["gate_wT"].rearrange("(a p) n -> p a n", p=128)),
          writes=[b_rgwT], kind="d")
        E("gpsimd", lambda X: X.dma_start(out=rgb[:], in_=ap["gate_b"]), writes=[b_rgb], kind="d")
        E("sync", lambda X: X.dma_start(out=rones[:], in_=ap["onesb"]), writes=[b_rones], kind="d")
        E("gpsimd", lambda X: X.dma_start(out=rb2bf[:], in_=ap["b2"]), writes=[b_rb2], kind="d")
        envB = dict(TB=TB, accS=accS, b_accT=b_accT, ln2T=ln2T, b_ln2T=b_ln2T,
                    We=We, b_WeT_tau=b_WeT_tau, ident=ident, b_ident=b_ident,
                    cst=cst, b_cst=b_cst, st=rst, b_st=[Buf(), Buf()],
                    ln2=rln2, b_ln2=[Buf(), Buf()],
                    dmp=[rdmp[:, 0:1024], rdmp[:, 1024:2048]], b_dmp=[Buf(), Buf()],
                    pT=pRT, b_pT=Buf("pRT"), pG=pG2, b_pG=Buf("pG2"),
                    gts=rgts, b_gts=Buf(), msk=rmsk, b_msk=Buf(), eqs=reqs, b_eqs=Buf(),
                    Webf=rWebf, b_Webf=Buf(), WeTs=rWeTs, b_WeTf=Buf(),
                    gwT=rgwT, b_gwT=b_rgwT, gb=rgb, b_gb=b_rgb, onesb=rones, b_ones=b_rones,
                    b2bf=rb2bf, b_b2=b_rb2, b_m12=[Buf(), Buf()])

    it = 0
    for ex in range(NE):
        for hq in range(4):
            wb = it % 2
            E("gpsimd", lambda X, ex=ex, hq=hq, wb=wb: X.dma_start(
                out=w1q[wb][:].rearrange("p (a n) -> p a n", a=8),
                in_=ap["w1"][ex, :, hq * 1024:(hq + 1) * 1024]
                .rearrange("(a p) n -> p a n", p=128)),
              writes=[b_w1q[wb]], kind="d")
            E("gpsimd", lambda X, ex=ex, hq=hq, wb=wb: X.dma_start(
                out=w2q[wb][:].rearrange("p (a n) -> p a n", a=8),
                in_=ap["w2"][ex, hq * 1024:(hq + 1) * 1024, :]
                .rearrange("(a p) n -> p a n", p=128)),
              writes=[b_w2q[wb]], kind="d")
            covered = 8 * ex + 2 * hq + 1 < punits
            for tg in range(cfg.ntg):
                while nextr < min(4 * (tg + 1), cfg.ntau):
                    _router(kb, envB, nextr, nextr % 2)
                    nextr += 1
                if tg == 0 and covered:
                    continue
                tgb = tg % 2
                for ht in range(8):
                    hb = ht % 2
                    def f(X, wb=wb, tg=tg, ht=ht, hb=hb):
                        for kt in range(8):
                            ins = X.matmul(pH[hb][:, :],
                                           w1q[wb][:, kt * 1024 + ht * 128: kt * 1024 + (ht + 1) * 128],
                                           ln2T[:, kt * TB + tg * 512: kt * TB + (tg + 1) * 512],
                                           start=(kt == 0), stop=(kt == 7))
                        return ins
                    E("tensor", f, reads=[b_w1q[wb], b_ln2T[tg]], writes=[b_pH[hb]])
                    def f(X, ex=ex, hq=hq, ht=ht, hb=hb, tgb=tgb):
                        bcol = ex * 32 + hq * 8 + ht
                        return X.activation(hT[tgb][:, ht * 512:(ht + 1) * 512],
                                            pH[hb][:, :], AF.Gelu,
                                            bias=b1L[:, bcol:bcol + 1])
                    E("scalar", f, reads=[b_pH[hb], b_b1], writes=[b_hT[tgb]])
                for m in range(4):
                    tau = tg * 4 + m
                    for nb_ in range(2):
                        def f(X, wb=wb, m=m, nb_=nb_, tgb=tgb):
                            for kt in range(8):
                                ins = X.matmul(pX[nb_][:, :],
                                               hT[tgb][:, kt * 512 + m * 128: kt * 512 + (m + 1) * 128],
                                               w2q[wb][:, kt * 1024 + nb_ * 512: kt * 1024 + (nb_ + 1) * 512],
                                               start=(kt == 0), stop=(kt == 7))
                            return ins
                        E("tensor", f, reads=[b_hT[tgb], b_w2q[wb]], writes=[b_pX[nb_]])
                        sl = slice(tau * D + nb_ * 512, tau * D + (nb_ + 1) * 512)
                        E("vector", lambda X, sl=sl, tau=tau, ex=ex, nb_=nb_: X.scalar_tensor_tensor(
                            accS[:, sl], pX[nb_][:, :],
                            We[:, tau * NE + ex: tau * NE + ex + 1],
                            accS[:, sl], op0=ALU.mult, op1=ALU.add),
                          reads=[b_pX[nb_], b_WeT_tau[tau], b_accT[tau]], writes=[b_accT[tau]])
            it += 1


def kernel(**inputs):
    cfg = CFG()
    nc = build_nc(cfg)
    in_maps = [_per_core_inputs(c, cfg, inputs) for c in range(NCORE)]
    res = run_bass_kernel_spmd(nc, in_maps, core_ids=list(range(NCORE)))
    outs = [np.asarray(res.results[c]["out"], dtype=np.float32) for c in range(NCORE)]
    return np.concatenate(outs, axis=0).reshape(1, T, D)


# version 31 (final)
# speedup vs baseline: 1.1101x; 1.0306x over previous
from contextlib import ExitStack

import numpy as np
import ml_dtypes
import concourse.bass as bass
import concourse.mybir as mybir
from concourse.bass_utils import run_bass_kernel_spmd

F32 = mybir.dt.float32
BF16 = mybir.dt.bfloat16
AF = mybir.ActivationFunctionType
ALU = mybir.AluOpType
AX = mybir.AxisListType

D = 1024
H = 16
DK = 64
DV = 128
C = 64
DFF = 4096
NE = 8
NCORE = 8
T = 16384

GAMMA = 1.0 - np.exp2(-5.0 - np.arange(H, dtype=np.float64))

DEBUG_OUTS = False
STAGE = 99


class CFG:
    def __init__(self, tok=T // NCORE, pref=T - T // NCORE, pgrp=8):
        assert tok % 512 == 0 and (pref % 128 == 0)
        self.tok = tok
        self.pref = pref
        self.ntau = tok // 128
        self.nptau = pref // 128
        self.nchunk = tok // C
        self.pgrp = min(pgrp, self.nptau) if self.nptau else 1
        assert self.nptau == 0 or self.nptau % self.pgrp == 0
        self.ngrp = (self.nptau // self.pgrp) if self.nptau else 0
        self.ntg = self.ntau // 4


class _Rec:
    __slots__ = ("eng", "kind", "fn", "idx", "tagged", "val", "slot")

    def __init__(self, eng, kind, fn, idx):
        self.eng, self.kind, self.fn, self.idx = eng, kind, fn, idx
        self.tagged = False
        self.val = 0
        self.slot = -1


class Buf:
    __slots__ = ("name", "wr", "rd")

    def __init__(self, name=""):
        self.name = name
        self.wr = None
        self.rd = []


NDMA = 6


class KB:
    ENGS = ("sync", "gpsimd", "tensor", "vector", "scalar")

    def __init__(self, nc, ctx):
        self.nc = nc
        self.stream = {e: [] for e in self.ENGS}
        self.nrec = {e: 0 for e in self.ENGS}
        self.waited = {e: {} for e in self.ENGS}
        self.sems = {}
        for e in self.ENGS:
            self.sems[(e, "c")] = ctx.enter_context(nc.semaphore(f"s_{e}_c"))
            for i in range(NDMA):
                self.sems[(e, "d", i)] = ctx.enter_context(nc.semaphore(f"s_{e}_d{i}"))
        self.dma_rr = {e: 0 for e in self.ENGS}
        self.dma_uses = {e: [0] * NDMA for e in self.ENGS}
        self.dma_last = {e: [None] * NDMA for e in self.ENGS}

    def _key(self, rec):
        return (rec.eng, "c") if rec.kind == "c" else (rec.eng, "d", rec.slot)

    def _wait(self, eng, dep: _Rec):
        k = self._key(dep)
        w = self.waited[eng]
        if w.get(k, -1) >= dep.idx:
            return
        w[k] = dep.idx
        dep.tagged = True
        self.stream[eng].append(("wait", dep))

    def emit(self, eng, fn, reads=(), writes=(), kind="c"):
        if kind == "d":
            slot = self.dma_rr[eng] % NDMA
            self.dma_rr[eng] += 1
            prev = self.dma_last[eng][slot]
            if prev is not None:
                self._wait(eng, prev)
        deps = {}
        def add(r):
            k = self._key(r)
            if k not in deps or deps[k].idx < r.idx:
                deps[k] = r
        for b in reads:
            if b.wr is not None:
                add(b.wr)
        for b in writes:
            if b.wr is not None:
                add(b.wr)
            for r in b.rd:
                add(r)
        for r in deps.values():
            self._wait(eng, r)
        rec = _Rec(eng, kind, fn, self.nrec[eng])
        if kind == "d":
            rec.tagged = True
            rec.slot = slot
            self.dma_uses[eng][slot] += 1
            rec.val = self.dma_uses[eng][slot]
            self.dma_last[eng][slot] = rec
        self.nrec[eng] += 1
        self.stream[eng].append(rec)
        for b in reads:
            b.rd.append(rec)
        for b in writes:
            b.wr = rec
            b.rd = []
        return rec

    def barrier(self):
        lasts = []
        for e in self.ENGS:
            seen_c = False
            seen_d = set()
            for item in reversed(self.stream[e]):
                if isinstance(item, _Rec):
                    if item.kind == "c" and not seen_c:
                        lasts.append(item); seen_c = True
                    elif item.kind == "d" and item.slot not in seen_d:
                        lasts.append(item); seen_d.add(item.slot)
                    if seen_c and len(seen_d) == NDMA:
                        break
        for e in self.ENGS:
            for rec in lasts:
                if rec.eng != e:
                    self._wait(e, rec)

    def finalize(self, block):
        for e in self.ENGS:
            cnt = 0
            for item in self.stream[e]:
                if isinstance(item, _Rec) and item.kind == "c" and item.tagged:
                    cnt += 1
                    item.val = cnt

        def mk(e):
            def run(eng_obj):
                for item in self.stream[e]:
                    if isinstance(item, _Rec):
                        ins = item.fn(eng_obj)
                        if item.tagged:
                            if item.kind == "d":
                                ins.then_inc(self.sems[(e, "d", item.slot)], 16)
                            else:
                                ins.then_inc(self.sems[(e, "c")], 1)
                    else:
                        _, dep = item
                        if dep.kind == "d":
                            eng_obj.wait_ge(self.sems[(dep.eng, "d", dep.slot)], dep.val * 16)
                        else:
                            eng_obj.wait_ge(self.sems[(dep.eng, "c")], dep.val)
            return run

        block.sync(mk("sync"))
        block.gpsimd(mk("gpsimd"))
        block.tensor(mk("tensor"))
        block.vector(mk("vector"))
        block.scalar(mk("scalar"))


def _host_constants():
    c = {}
    c["ident"] = np.eye(128, dtype=ml_dtypes.bfloat16)
    inv = 1.0 / 10000.0 ** (np.arange(0, DK, 2, dtype=np.float64) / DK)
    t_all = np.arange(T, dtype=np.float64)
    frq = t_all[:, None] * inv[None, :]
    c["_cos_all"] = np.cos(frq)
    c["_sin_all"] = np.sin(frq)

    i64 = np.arange(C, dtype=np.float64)
    ipos = np.concatenate([i64, i64])
    c["dq128"] = (GAMMA[None, :] ** (ipos[:, None] + 1.0) * (DK ** -0.5)).astype(np.float32)
    c["dkv128"] = (GAMMA[None, :] ** (C - 1.0 - ipos[:, None])).astype(np.float32)

    dm2 = np.zeros((128, H * 64), dtype=np.float32)
    for h in range(H):
        g = GAMMA[h]
        colj = g ** (-1.0 - i64)
        m = np.where(i64[None, :] >= i64[:, None], colj[:, None], 0.0)
        dm2[0:64, h * 64:(h + 1) * 64] = m.astype(np.float32)
        dm2[64:128, h * 64:(h + 1) * 64] = m.astype(np.float32)
    c["dmat2"] = dm2
    c["lamH"] = np.broadcast_to((GAMMA ** C).astype(np.float32)[None, :], (128, H)).copy()
    return c


_CONSTS = _host_constants()


def _per_core_inputs(core, cfg: CFG, full):
    tok, pref = cfg.tok, cfg.pref
    t0 = core * tok
    x = np.ascontiguousarray(np.asarray(full["L0_attn_x"], dtype=np.float32).reshape(-1, D))
    d = {}
    d["x_loc"] = np.ascontiguousarray(x[t0:t0 + tok])
    xp = np.zeros((pref, D), dtype=np.float32)
    nreal = min(t0, pref)
    if nreal > 0:
        xp[pref - nreal:] = x[t0 - nreal:t0]
    d["x_pref"] = xp

    def tab(a, n):
        return np.ascontiguousarray(
            a.reshape(n, 128, 32).transpose(1, 0, 2).reshape(128, n * 32).astype(np.float32))

    cos_all, sin_all = _CONSTS["_cos_all"], _CONSTS["_sin_all"]
    d["cos_loc"] = tab(cos_all[t0:t0 + tok], cfg.ntau)
    d["sin_loc"] = tab(sin_all[t0:t0 + tok], cfg.ntau)
    if cfg.nptau:
        cp = np.zeros((pref, DK // 2), dtype=np.float64)
        sp = np.zeros((pref, DK // 2), dtype=np.float64)
        if nreal > 0:
            cp[pref - nreal:] = cos_all[t0 - nreal:t0]
            sp[pref - nreal:] = sin_all[t0 - nreal:t0]
        d["cos_pref"] = tab(cp, cfg.nptau)
        d["sin_pref"] = tab(sp, cfg.nptau)
    else:
        d["cos_pref"] = np.zeros((128, 32), np.float32)
        d["sin_pref"] = np.zeros((128, 32), np.float32)

    gt = cfg.pgrp * 128
    u = np.arange(gt, dtype=np.float64)
    w = GAMMA[None, :] ** (gt - 1.0 - u[:, None])
    d["wpref"] = np.ascontiguousarray(
        w.reshape(cfg.pgrp, 128, H).transpose(1, 0, 2).reshape(128, cfg.pgrp * H).astype(np.float32))
    d["lamgH"] = np.broadcast_to((GAMMA ** gt).astype(np.float32)[None, :], (128, H)).copy()

    for k in ("ident", "dq128", "dkv128", "dmat2", "lamH"):
        d[k] = _CONSTS[k]
    d["dmat2"] = _CONSTS["dmat2"].astype(ml_dtypes.bfloat16)
    for k in ("L0_attn_Wq", "L0_attn_Wk", "L0_attn_Wv", "L0_attn_Wg", "L0_attn_Wo"):
        d[k] = np.ascontiguousarray(np.asarray(full[k], dtype=np.float32))
    gn = np.asarray(full["L0_attn_g_norm_w"], dtype=np.float32)
    d["gnwd"] = np.ascontiguousarray(np.broadcast_to(gn[None, :], (128, DV)).copy())
    d["gncol"] = np.ascontiguousarray(gn.reshape(128, 1))
    d["gate_wT"] = np.ascontiguousarray(np.asarray(full["L0_ffn_gate_w"], dtype=np.float32).T)
    d["gate_b"] = np.ascontiguousarray(np.asarray(full["L0_ffn_gate_b"], dtype=np.float32).reshape(1, NE))
    d["w1"] = np.ascontiguousarray(np.asarray(full["L0_ffn_w1"], dtype=np.float32))
    d["w2"] = np.ascontiguousarray(np.asarray(full["L0_ffn_w2"], dtype=np.float32))
    b1 = np.asarray(full["L0_ffn_b1"], dtype=np.float32)
    d["b1L"] = np.ascontiguousarray(b1.reshape(NE, 32, 128).transpose(2, 0, 1).reshape(128, NE * 32))
    d["b2"] = np.ascontiguousarray(np.asarray(full["L0_ffn_b2"], dtype=np.float32))
    d["onesb"] = np.ones((1, 128), dtype=ml_dtypes.bfloat16)
    return d


def build_nc(cfg: CFG):
    nc = bass.Bass("TRN2", target_bir_lowering=False)
    tok, pref = cfg.tok, cfg.pref

    ap = {}

    def din(name, shape, dt=F32):
        ap[name] = nc.dram_tensor(name, list(shape), dt, kind="ExternalInput").ap()

    din("x_loc", (tok, D)); din("x_pref", (pref, D))
    din("cos_loc", (128, cfg.ntau * 32)); din("sin_loc", (128, cfg.ntau * 32))
    din("cos_pref", (128, max(cfg.nptau, 1) * 32)); din("sin_pref", (128, max(cfg.nptau, 1) * 32))
    din("wpref", (128, cfg.pgrp * H)); din("lamgH", (128, H))
    din("ident", (128, 128), BF16)
    din("dq128", (128, H)); din("dkv128", (128, H)); din("dmat2", (128, H * 64), BF16)
    din("lamH", (128, H))
    din("L0_attn_Wq", (D, D)); din("L0_attn_Wk", (D, D))
    din("L0_attn_Wv", (D, 2 * D)); din("L0_attn_Wg", (D, 2 * D)); din("L0_attn_Wo", (2 * D, D))
    din("gnwd", (128, DV)); din("gncol", (128, 1))
    din("gate_wT", (D, NE)); din("gate_b", (1, NE))
    din("w1", (NE, D, DFF)); din("w2", (NE, DFF, D))
    din("b1L", (128, NE * 32)); din("b2", (NE, D))
    din("onesb", (1, 128), BF16)

    out = nc.dram_tensor("out", [tok, D], F32, kind="ExternalOutput").ap()
    okind = "ExternalOutput" if DEBUG_OUTS else "Internal"
    q_s = nc.dram_tensor("q_s", [tok, D], BF16, kind=okind).ap()
    k_s = nc.dram_tensor("k_s", [tok, D], BF16, kind=okind).ap()
    v_s = nc.dram_tensor("v_s", [tok, 2 * D], BF16, kind=okind).ap()
    g_s = nc.dram_tensor("g_s", [tok, 2 * D], F32, kind=okind).ap()
    s_dbg = nc.dram_tensor("s_dbg", [64, H * DV], F32, kind=okind).ap()
    acc_dbg = nc.dram_tensor("acc_dbg", [128, cfg.ntau * D], F32, kind=okind).ap()

    b_qs, b_ks, b_vs, b_gs = Buf("q_s"), Buf("k_s"), Buf("v_s"), Buf("g_s")

    with ExitStack() as ctx:
        e = ctx.enter_context
        kb = KB(nc, ctx)
        E = kb.emit

        def dma(eng, dst, src, bufs_w=(), bufs_r=()):
            return E(eng, lambda X, d=dst, s=src: X.dma_start(out=d, in_=s),
                     reads=list(bufs_r), writes=list(bufs_w), kind="d")

        ident = e(nc.sbuf_tensor("identS", [128, 128], BF16)); b_ident = Buf("ident")
        Sf = e(nc.sbuf_tensor("SfS", [128, H * DV], F32)); b_Sf = Buf("Sf")
        We = e(nc.sbuf_tensor("WeS", [128, cfg.ntau * NE], F32)); b_We = Buf()
        cst = e(nc.sbuf_tensor("cstS", [128, 4], F32)); b_cst = Buf("cst")
        dma("sync", ident[:], ap["ident"], [b_ident])
        E("gpsimd", lambda X: X.memset(cst[:, 0:1], 1e-5), writes=[b_cst])
        E("gpsimd", lambda X: X.memset(cst[:, 1:2], 1.0 / DV), writes=[b_cst])
        cEPS = cst[:, 0:1]

        with ExitStack() as p0:
            e0 = p0.enter_context
            cosL = e0(nc.sbuf_tensor("cosLS", [128, cfg.ntau * 32], F32)); b_cosL = Buf()
            sinL = e0(nc.sbuf_tensor("sinLS", [128, cfg.ntau * 32], F32)); b_sinL = Buf()
            wq = e0(nc.sbuf_tensor("wqS", [128, 8 * D], BF16)); b_wq = Buf()
            wk = e0(nc.sbuf_tensor("wkS", [128, 8 * D], BF16)); b_wk = Buf()
            wv = e0(nc.sbuf_tensor("wvS", [128, 8 * 2 * D], BF16)); b_wv = Buf()
            wg = e0(nc.sbuf_tensor("wgS", [128, 8 * 2 * D], BF16)); b_wg = Buf()
            cosP = [e0(nc.sbuf_tensor(f"cosPS{i}", [128, cfg.pgrp * 32], F32)) for i in range(2)]
            sinP = [e0(nc.sbuf_tensor(f"sinPS{i}", [128, cfg.pgrp * 32], F32)) for i in range(2)]
            b_cosP = [Buf() for _ in range(2)]
            b_sinP = [Buf() for _ in range(2)]
            wpref = e0(nc.sbuf_tensor("wprefS", [128, cfg.pgrp * H], F32)); b_wpref = Buf()
            lamgH = e0(nc.sbuf_tensor("lamgHS", [128, H], F32)); b_lamg = Buf()

            dma("sync", cosL[:], ap["cos_loc"], [b_cosL])
            dma("sync", sinL[:], ap["sin_loc"], [b_sinL])

            dma("sync", wpref[:], ap["wpref"], [b_wpref])
            dma("sync", lamgH[:], ap["lamgH"], [b_lamg])
            for dst, bf, nm in ((wk, b_wk, "L0_attn_Wk"), (wv, b_wv, "L0_attn_Wv"),
                                (wq, b_wq, "L0_attn_Wq"), (wg, b_wg, "L0_attn_Wg")):
                dma("gpsimd", dst[:].rearrange("p (a n) -> p a n", a=8),
                    ap[nm].rearrange("(a p) n -> p a n", p=128), [bf])

            NB = 2
            xt = [e0(nc.sbuf_tensor(f"xt{i}", [128, D], F32)) for i in range(NB)]
            b_xt = [Buf(f"xt{i}") for i in range(NB)]
            lnx = [e0(nc.sbuf_tensor(f"lnx{i}", [128, D], BF16)) for i in range(NB)]
            b_lnx = [Buf() for _ in range(NB)]
            lnxT = [e0(nc.sbuf_tensor(f"lnxT{i}", [128, D], BF16)) for i in range(NB)]
            b_lnxT = [Buf() for _ in range(NB)]

            xsw = [e0(nc.sbuf_tensor(f"xsw{i}", [128, D], F32)) for i in range(2)]
            b_xsw = [Buf() for _ in range(2)]
            kf_one = e0(nc.sbuf_tensor("kfS", [128, D], F32)); b_kf_one = Buf()
            kf = [kf_one, kf_one]; b_kf = [b_kf_one, b_kf_one]
            qf, b_qf = kf, b_kf
            qro = [e0(nc.sbuf_tensor(f"qro{i}", [128, D], BF16)) for i in range(NB)]
            b_qro = [Buf() for _ in range(NB)]
            kro = [e0(nc.sbuf_tensor(f"kro{i}", [128, D], BF16)) for i in range(NB)]
            b_kro = [Buf() for _ in range(NB)]
            rt1 = [e0(nc.sbuf_tensor(f"rt1S{i}", [128, D], F32)) for i in range(NB)]
            rt2 = [e0(nc.sbuf_tensor(f"rt2S{i}", [128, D], F32)) for i in range(NB)]
            b_rt1 = [Buf() for _ in range(NB)]
            b_rt2 = [Buf() for _ in range(NB)]
            vbf = [e0(nc.sbuf_tensor(f"vbf{i}", [128, 2 * D], BF16)) for i in range(NB)]
            b_vbf = [Buf() for _ in range(NB)]
            gf = [e0(nc.sbuf_tensor(f"gf{i}", [128, 2 * D], F32)) for i in range(NB)]
            b_gf = [Buf() for _ in range(NB)]
            kdw = [e0(nc.sbuf_tensor(f"kdw{i}", [128, D], BF16)) for i in range(NB)]
            b_kdw = [Buf() for _ in range(NB)]

            pT2 = [e0(nc.psum_tensor(f"pT{i}", [128, 1024], BF16)) for i in range(2)]
            b_pT2 = [Buf("pT0"), Buf("pT1")]
            pT_ctr = [0]
            pP = e0(nc.psum_tensor("pP", [128, 1024], F32))
            b_pP = [Buf("pP0"), Buf("pP1")]
            qbank = [0]

            def next_bank():
                qb = qbank[0] % 2
                qbank[0] += 1
                return qb

            def make_sweep(sched):
                NS = len(sched)
                lnsm = e0(nc.sbuf_tensor("lnsm", [128, NS], F32))
                lnsq = e0(nc.sbuf_tensor("lnsq", [128, NS], F32))
                lnmn = e0(nc.sbuf_tensor("lnmn", [128, NS], F32))
                lnmg = e0(nc.sbuf_tensor("lnmg", [128, NS], F32))
                lnvr = e0(nc.sbuf_tensor("lnvr", [128, NS], F32))
                lnrs = e0(nc.sbuf_tensor("lnrs", [128, NS], F32))
                lnnb = e0(nc.sbuf_tensor("lnnb", [128, NS], F32))
                ngrp16 = (NS + 15) // 16
                b_g16 = [Buf(f"lng{g}") for g in range(ngrp16)]

                def sweep(slot):
                    kind, tau = sched[slot]
                    b = slot % 2
                    src = ap["x_pref"] if kind == "P" else ap["x_loc"]
                    dma("sync", xsw[b][:], src[tau * 128:(tau + 1) * 128, :], [b_xsw[b]])
                    g = slot // 16
                    E("scalar", lambda X: X.activation(xsw[b][:], xsw[b][:], AF.Identity,
                                                       accum_out=lnsm[:, slot:slot + 1]),
                      reads=[b_xsw[b]], writes=[b_xsw[b], b_g16[g]])
                    E("scalar", lambda X: X.activation(xsw[b][:], xsw[b][:], AF.Square,
                                                       accum_out=lnsq[:, slot:slot + 1]),
                      reads=[b_xsw[b]], writes=[b_xsw[b], b_g16[g]])
                    if slot == min((g + 1) * 16, NS) - 1:
                        c = slice(g * 16, min((g + 1) * 16, NS))
                        E("vector", lambda X: (X.tensor_scalar_mul(lnmn[:, c], lnsm[:, c], 1.0 / D),
                                               X.tensor_scalar_mul(lnmg[:, c], lnsm[:, c], -1.0 / D),
                                               X.tensor_scalar_mul(lnvr[:, c], lnsq[:, c], 1.0 / D))[-1],
                          reads=[b_g16[g]], writes=[b_g16[g]])
                        E("vector", lambda X: X.tensor_mul(lnsq[:, c], lnmn[:, c], lnmg[:, c]),
                          reads=[b_g16[g]], writes=[b_g16[g]])
                        E("vector", lambda X: X.tensor_add(lnvr[:, c], lnvr[:, c], lnsq[:, c]),
                          reads=[b_g16[g]], writes=[b_g16[g]])
                        E("scalar", lambda X: X.activation(lnrs[:, c], lnvr[:, c], AF.Sqrt, bias=cEPS),
                          reads=[b_g16[g], b_cst], writes=[b_g16[g]])
                        E("vector", lambda X: X.reciprocal(lnrs[:, c], lnrs[:, c]),
                          reads=[b_g16[g]], writes=[b_g16[g]])
                        E("vector", lambda X: X.tensor_mul(lnnb[:, c], lnmg[:, c], lnrs[:, c]),
                          reads=[b_g16[g]], writes=[b_g16[g]])

                def apply_ln(slot, buf, b_src, src_ap_2d):
                    g = slot // 16
                    E("scalar", lambda X: X.activation(lnx[buf][:], src_ap_2d, AF.Identity,
                                                       bias=lnnb[:, slot:slot + 1],
                                                       scale=lnrs[:, slot:slot + 1]),
                      reads=[b_src, b_g16[g]], writes=[b_lnx[buf]])

                return sweep, apply_ln

            def transpose8(b_dst, dst_ap, src_sb, b_src):
                pb = pT_ctr[0] % 2
                pT_ctr[0] += 1
                pT, b_pT = pT2[pb], b_pT2[pb]
                E("tensor", lambda X, pT=pT: [X.transpose(pT[:, b * 128:(b + 1) * 128],
                                                   src_sb[:, b * 128:(b + 1) * 128], ident[:])
                                       for b in range(8)][-1],
                  reads=[b_src, b_ident], writes=[b_pT])
                E("vector", lambda X, pT=pT: X.tensor_copy(dst_ap, pT[:, 0:1024]),
                  reads=[b_pT], writes=[b_dst])

            def projQ(lnxT_, b_lnxT_, w_sb, b_w, wcol0, wstride, qb):
                def f(X):
                    for kt in range(8):
                        ins = X.matmul(pP[:, qb * 512:(qb + 1) * 512],
                                       lnxT_[:, kt * 128:(kt + 1) * 128],
                                       w_sb[:, kt * wstride + wcol0: kt * wstride + wcol0 + 512],
                                       start=(kt == 0), stop=(kt == 7))
                    return ins
                E("tensor", f, reads=[b_lnxT_, b_w], writes=[b_pP[qb]])

            def rope(buf, dst, b_dst, srcf, b_src, tabc, b_tc, tabs, b_ts, col):
                q3 = srcf[:].rearrange("p (h d) -> p h d", h=H)
                o3 = dst[:].rearrange("p (h d) -> p h d", h=H)
                t3a = rt1[buf][:].rearrange("p (h d) -> p h d", h=H)
                t3b = rt2[buf][:].rearrange("p (h d) -> p h d", h=H)
                cs = tabc[:, col:col + 32].unsqueeze(1).broadcast_to((128, H, 32))
                sn = tabs[:, col:col + 32].unsqueeze(1).broadcast_to((128, H, 32))
                E("vector", lambda X: (X.tensor_tensor(t3a[:, :, 0:32], q3[:, :, 0:32], cs, ALU.mult),
                                       X.tensor_tensor(t3a[:, :, 32:64], q3[:, :, 32:64], sn, ALU.mult))[-1],
                  reads=[b_src, b_tc, b_ts], writes=[b_rt1[buf]])
                E("vector", lambda X: (X.tensor_tensor(t3b[:, :, 0:32], q3[:, :, 0:32], sn, ALU.mult),
                                       X.tensor_tensor(t3b[:, :, 32:64], q3[:, :, 32:64], cs, ALU.mult))[-1],
                  reads=[b_src, b_tc, b_ts], writes=[b_rt2[buf]])
                E("vector", lambda X: (X.tensor_sub(o3[:, :, 0:32], t3a[:, :, 0:32], t3a[:, :, 32:64]),
                                       X.tensor_add(o3[:, :, 32:64], t3b[:, :, 0:32], t3b[:, :, 32:64]))[-1],
                  reads=[b_rt1[buf], b_rt2[buf]], writes=[b_dst])

            E("gpsimd", lambda X: X.memset(Sf[:], 0.0), writes=[b_Sf])
            with ExitStack() as p0a:
                pKV = p0a.enter_context(nc.psum_tensor("pKV", [128, H * DV], F32))
                b_pKV = Buf("pKV")
                pending_kv = [None]

                def emit_pending_kv():
                    if pending_kv[0] is not None:
                        fkv, bufs = pending_kv[0]
                        E("tensor", fkv, reads=bufs, writes=[b_pKV])
                        pending_kv[0] = None

                sched = []
                if cfg.nptau and STAGE >= 0:
                    ratio = max(1, cfg.nptau // max(cfg.ntau, 1))
                    li = 0
                    for pi in range(cfg.nptau):
                        sched.append(("P", pi))
                        if STAGE >= 1 and (pi + 1) % ratio == 0 and li < cfg.ntau:
                            sched.append(("L", li))
                            li += 1
                    while STAGE >= 1 and li < cfg.ntau:
                        sched.append(("L", li))
                        li += 1
                elif STAGE >= 1:
                    sched = [("L", i) for i in range(cfg.ntau)]

                def ptab(g):
                    gb_ = g % 2
                    dma("sync", cosP[gb_][:],
                        ap["cos_pref"][:, g * cfg.pgrp * 32:(g + 1) * cfg.pgrp * 32],
                        [b_cosP[gb_]])
                    dma("sync", sinP[gb_][:],
                        ap["sin_pref"][:, g * cfg.pgrp * 32:(g + 1) * cfg.pgrp * 32],
                        [b_sinP[gb_]])

                if cfg.ngrp:
                    ptab(0)

                sweep, apply_ln = make_sweep(sched)
                LOOK = 24

                def stageA(pos):
                    kind, tau = sched[pos]
                    buf = pos % NB
                    src = ap["x_pref"] if kind == "P" else ap["x_loc"]
                    dma("sync", xt[buf][:], src[tau * 128:(tau + 1) * 128, :], [b_xt[buf]])
                    apply_ln(pos, buf, b_xt[buf], xt[buf][:])

                def stageT(pos):
                    buf = pos % NB
                    transpose8(b_lnxT[buf], lnxT[buf][:], lnx[buf], b_lnx[buf])

                def quarters(buf, w_sb, b_w, n, wstride, evac):
                    for qt in range(n):
                        qb = next_bank()
                        projQ(lnxT[buf], b_lnxT[buf], w_sb, b_w, qt * 512, wstride, qb)
                        evac(qt, qb)

                def ev_scalar(dst_sb, b_dst):
                    def f(qt, qb):
                        E("scalar", lambda X, qt=qt, qb=qb: X.activation(
                            dst_sb[:, qt * 512:(qt + 1) * 512],
                            pP[:, qb * 512:(qb + 1) * 512], AF.Identity),
                          reads=[b_pP[qb]], writes=[b_dst])
                    return f

                def ev_vector(dst_sb, b_dst):
                    def f(qt, qb):
                        E("vector", lambda X, qt=qt, qb=qb: X.tensor_copy(
                            dst_sb[:, qt * 512:(qt + 1) * 512],
                            pP[:, qb * 512:(qb + 1) * 512]),
                          reads=[b_pP[qb]], writes=[b_dst])
                    return f

                def ev_mixed(dst_sb, b_dst):
                    def f(qt, qb):
                        if qt % 2 == 0:
                            ev_vector(dst_sb, b_dst)(qt, qb)
                        else:
                            ev_scalar(dst_sb, b_dst)(qt, qb)
                    return f

                def stageP(pos):
                    kind, tau = sched[pos]
                    buf = pos % NB
                    if kind == "P":
                        g, ti = divmod(tau, cfg.pgrp)
                        gb_ = g % 2
                        quarters(buf, wk, b_wk, 2, D, ev_scalar(kf[buf], b_kf[buf]))
                        quarters(buf, wv, b_wv, 4, 2 * D, ev_mixed(vbf[buf], b_vbf[buf]))
                        emit_pending_kv()
                        rope(buf, kro[buf], b_kro[buf], kf[buf], b_kf[buf],
                             cosP[gb_], b_cosP[gb_], sinP[gb_], b_sinP[gb_], ti * 32)
                        def f(X, buf=buf, ti=ti):
                            w3 = wpref[:, ti * H:(ti + 1) * H].unsqueeze(2).broadcast_to((128, H, 64))
                            k3 = kro[buf][:].rearrange("p (h d) -> p h d", h=H)
                            o3 = kdw[buf][:].rearrange("p (h d) -> p h d", h=H)
                            return X.tensor_tensor(o3, k3, w3, ALU.mult)
                        E("vector", f, reads=[b_kro[buf], b_wpref], writes=[b_kdw[buf]])
                        def fkv(X, buf=buf, ti=ti):
                            for h in range(H):
                                ins = X.matmul(pKV[64:128, h * DV:(h + 1) * DV],
                                               kdw[buf][:, h * 64:(h + 1) * 64],
                                               vbf[buf][:, h * DV:(h + 1) * DV],
                                               start=(ti == 0 and h % 4 == 0),
                                               stop=(ti == cfg.pgrp - 1 and h % 4 == 3),
                                               tile_position=(0, 64))
                            return ins
                        pending_kv[0] = (fkv, [b_kdw[buf], b_vbf[buf]])
                        if ti == cfg.pgrp - 1:
                            emit_pending_kv()
                            def f(X):
                                S3 = Sf[64:128, :].rearrange("p (h v) -> p h v", h=H)
                                l3 = lamgH[64:128, :].unsqueeze(2).broadcast_to((64, H, DV))
                                return X.tensor_tensor(S3, S3, l3, ALU.mult)
                            E("vector", f, reads=[b_Sf, b_lamg], writes=[b_Sf])
                            E("vector", lambda X: X.tensor_add(Sf[64:128, :], Sf[64:128, :],
                                                               pKV[64:128, :]),
                              reads=[b_Sf, b_pKV], writes=[b_Sf])
                            if g + 1 < cfg.ngrp:
                                ptab(g + 1)
                    else:
                        emit_pending_kv()
                        quarters(buf, wq, b_wq, 2, D, ev_scalar(qf[buf], b_qf[buf]))
                        rope(buf, qro[buf], b_qro[buf], qf[buf], b_qf[buf],
                             cosL, b_cosL, sinL, b_sinL, tau * 32)
                        dma("sync", q_s[tau * 128:(tau + 1) * 128, :], qro[buf][:],
                            [b_qs], [b_qro[buf]])
                        quarters(buf, wk, b_wk, 2, D, ev_scalar(kf[buf], b_kf[buf]))
                        rope(buf, kro[buf], b_kro[buf], kf[buf], b_kf[buf],
                             cosL, b_cosL, sinL, b_sinL, tau * 32)
                        dma("sync", k_s[tau * 128:(tau + 1) * 128, :], kro[buf][:],
                            [b_ks], [b_kro[buf]])
                        quarters(buf, wv, b_wv, 4, 2 * D, ev_mixed(vbf[buf], b_vbf[buf]))
                        dma("sync", v_s[tau * 128:(tau + 1) * 128, :], vbf[buf][:],
                            [b_vs], [b_vbf[buf]])
                        quarters(buf, wg, b_wg, 4, 2 * D, ev_scalar(gf[buf], b_gf[buf]))
                        dma("sync", g_s[tau * 128:(tau + 1) * 128, :], gf[buf][:],
                            [b_gs], [b_gf[buf]])

                if sched:
                    for sl in range(min(LOOK, len(sched))):
                        sweep(sl)
                    stageA(0)
                    if len(sched) > 1:
                        stageA(1)
                    stageT(0)
                    for pos in range(len(sched)):
                        if pos + LOOK < len(sched):
                            sweep(pos + LOOK)
                        if pos + 2 < len(sched):
                            stageA(pos + 2)
                        if pos + 1 < len(sched):
                            stageT(pos + 1)
                        stageP(pos)
                emit_pending_kv()
                kb.barrier()

            if DEBUG_OUTS:
                dma("sync", s_dbg, Sf[64:128, :], [Buf()], [b_Sf])
            kb.barrier()

        accS = e(nc.sbuf_tensor("accS", [128, cfg.ntau * D], F32))
        b_accT = [Buf(f"acc{t}") for t in range(cfg.ntau)]
        ln2T = e(nc.sbuf_tensor("ln2TS", [128, 8 * cfg.ntau * 128], BF16))
        b_ln2T = [Buf(f"ln2T{g}") for g in range(cfg.ntg)]
        WeT = None
        b_WeTf = Buf("WeT")
        b_WeT_tau = [Buf(f"We{t}") for t in range(cfg.ntau)]
        punits = 0
        if STAGE >= 2:
            with ExitStack() as p1:
                punits = _retention(nc, p1.enter_context, kb, cfg, ap, q_s, b_qs, k_s, b_ks,
                                    v_s, b_vs, g_s, b_gs, ident, b_ident, Sf, b_Sf, accS, b_accT,
                                    cst, b_cst, ln2T, b_ln2T, We, b_WeT_tau, WeT, b_WeTf)
                kb.barrier()

        if STAGE >= 3:
            with ExitStack() as p23:
                _ffn(nc, p23.enter_context, kb, cfg, ap, accS, b_accT, ident, b_ident,
                     We, b_WeT_tau, WeT, b_WeTf, ln2T, b_ln2T, cst, b_cst, punits)
                kb.barrier()

        b_out = Buf("out")
        if STAGE >= 2:
            if DEBUG_OUTS:
                dma("sync", acc_dbg, accS[:], [Buf()], b_accT)
            dma("sync", out.rearrange("(tau p) d -> p tau d", p=128),
                accS[:].rearrange("p (tau d) -> p tau d", d=D), [b_out], b_accT)
        else:
            E("gpsimd", lambda X: X.memset(accS[:], 0.0), writes=b_accT)
            dma("sync", out.rearrange("(tau p) d -> p tau d", p=128),
                accS[:].rearrange("p (tau d) -> p tau d", d=D), [b_out], b_accT)
        kb._wait("sync", b_out.wr)

        block = e(nc.Block())
        kb.finalize(block)
    return nc



def _router(kb, env, tau, buf):
    E = kb.emit
    TB = env["TB"]
    accS, b_accT = env["accS"], env["b_accT"]
    ln2T, b_ln2T = env["ln2T"], env["b_ln2T"]
    We, b_WeT_tau = env["We"], env["b_WeT_tau"]
    ident, b_ident = env["ident"], env["b_ident"]
    cst, b_cst = env["cst"], env["b_cst"]
    st = env["st"]; b_st = env["b_st"][buf]
    ln2, b_ln2 = env["ln2"][buf], env["b_ln2"][buf]
    dmp, b_dmp = env["dmp"], env["b_dmp"]
    pT, b_pT = env["pT"], env["b_pT"]
    pG, b_pG = env["pG"], env["b_pG"]
    gts, b_gts = env["gts"], env["b_gts"]
    msk, b_msk = env["msk"], env["b_msk"]
    eqs, b_eqs = env["eqs"], env["b_eqs"]
    Webf, b_Webf = env["Webf"], env["b_Webf"]
    WeTs, b_WeTf = env["WeTs"], env["b_WeTf"]
    gwT, b_gwT = env["gwT"], env["b_gwT"]
    gb, b_gb = env["gb"], env["b_gb"]
    onesb, b_ones = env["onesb"], env["b_ones"]
    b2bf, b_b2 = env["b2bf"], env["b_b2"]
    b_m12 = env["b_m12"][buf]
    xa = accS[:, tau * D:(tau + 1) * D]
    sc = 16 + 32 * buf
    sm = st[:, sc:sc + 1]; sq = st[:, sc + 1:sc + 2]
    mean = st[:, sc + 2:sc + 3]; var = st[:, sc + 3:sc + 4]
    rstd = st[:, sc + 4:sc + 5]; nb = st[:, sc + 5:sc + 6]
    tg = tau // 4
    E("scalar", lambda X: (X.activation(dmp[0], xa, AF.Square, accum_out=sq),
                           X.activation(dmp[1], xa, AF.Identity, accum_out=sm))[-1],
      reads=[b_accT[tau]], writes=list(b_dmp) + [b_st])
    E("vector", lambda X: (X.tensor_scalar_mul(mean, sm, 1.0 / D),
                           X.tensor_scalar_mul(var, sq, 1.0 / D))[-1],
      reads=[b_st], writes=[b_st])
    E("vector", lambda X: X.tensor_mul(nb, mean, mean),
      reads=[b_st], writes=[b_st])
    E("vector", lambda X: X.tensor_sub(var, var, nb),
      reads=[b_st], writes=[b_st])
    E("scalar", lambda X: X.activation(rstd, var, AF.Sqrt, bias=cst[:, 0:1]),
      reads=[b_st, b_cst], writes=[b_st])
    E("vector", lambda X: X.reciprocal(rstd, rstd),
      reads=[b_st], writes=[b_st])
    E("vector", lambda X: X.tensor_mul(nb, mean, rstd),
      reads=[b_st], writes=[b_st])
    E("vector", lambda X: X.tensor_scalar_mul(nb, nb, -1.0),
      reads=[b_st], writes=[b_st])
    E("scalar", lambda X: X.activation(ln2[:], xa, AF.Identity, bias=nb, scale=rstd),
      reads=[b_accT[tau], b_st], writes=[b_ln2])
    E("tensor", lambda X: [X.transpose(pT[:, b * 128:(b + 1) * 128],
                                       ln2[:, b * 128:(b + 1) * 128], ident[:])
                           for b in range(8)][-1],
      reads=[b_ln2, b_ident], writes=[b_pT])
    def f(X, tau=tau):
        dst = ln2T[:].rearrange("p (kt t) -> p kt t", kt=8)
        src = pT[:, 0:1024].rearrange("p (kt t) -> p kt t", kt=8)
        return X.activation(dst[:, :, tau * 128:(tau + 1) * 128], src, AF.Identity)
    E("scalar", f, reads=[b_pT], writes=[b_ln2T[tg]])
    def f(X, tau=tau):
        for kt in range(8):
            X.matmul(pG[:, 0:NE], ln2T[:, kt * TB + tau * 128: kt * TB + (tau + 1) * 128],
                     gwT[:, kt * NE:(kt + 1) * NE], start=(kt == 0), stop=False)
        return X.matmul(pG[:, 0:NE], onesb[0:1, :], gb[0:1, :], start=False, stop=True)
    E("tensor", f, reads=[b_ln2T[tg], b_gwT, b_ones, b_gb], writes=[b_pG])
    mc = 22 + 32 * buf
    m1 = st[:, mc:mc + 1]; m2 = st[:, mc + 1:mc + 2]; p1 = st[:, mc + 2:mc + 3]
    p2v = st[:, mc + 3:mc + 4]; dd = st[:, mc + 4:mc + 5]
    eq1 = eqs[:, 0:NE]; eq2 = eqs[:, NE:2 * NE]
    E("vector", lambda X: X.tensor_copy(gts[:], pG[:, 0:NE]),
      reads=[b_pG], writes=[b_gts])
    E("vector", lambda X: X.reduce_max(m1, gts[:], AX.X),
      reads=[b_gts], writes=[b_m12])
    E("vector", lambda X: X.tensor_single_scalar(eq1, gts[:], m1, ALU.is_equal),
      reads=[b_gts, b_m12], writes=[b_eqs])
    E("vector", lambda X: X.scalar_tensor_tensor(msk[:], eq1, -1e30, gts[:],
                                                 op0=ALU.mult, op1=ALU.add),
      reads=[b_eqs, b_gts], writes=[b_msk])
    E("vector", lambda X: X.reduce_max(m2, msk[:], AX.X),
      reads=[b_msk], writes=[b_m12])
    E("vector", lambda X: X.tensor_single_scalar(eq2, msk[:], m2, ALU.is_equal),
      reads=[b_msk, b_m12], writes=[b_eqs])
    E("vector", lambda X: X.tensor_sub(dd, m1, m2),
      reads=[b_m12], writes=[b_m12])
    E("scalar", lambda X: X.activation(p1, dd, AF.Sigmoid),
      reads=[b_m12], writes=[b_m12])
    E("vector", lambda X: X.tensor_scalar(p2v, p1, -1.0, 1.0, op0=ALU.mult, op1=ALU.add),
      reads=[b_m12], writes=[b_m12])
    E("vector", lambda X, tau=tau: X.tensor_single_scalar(
        We[:, tau * NE:(tau + 1) * NE], eq1, p1, ALU.mult),
      reads=[b_eqs, b_m12], writes=[b_WeT_tau[tau]])
    E("vector", lambda X, tau=tau: X.scalar_tensor_tensor(
        We[:, tau * NE:(tau + 1) * NE], eq2, p2v, We[:, tau * NE:(tau + 1) * NE],
        op0=ALU.mult, op1=ALU.add),
      reads=[b_eqs, b_m12, b_WeT_tau[tau]], writes=[b_WeT_tau[tau]])
    E("vector", lambda X, tau=tau: X.tensor_copy(Webf[:], We[:, tau * NE:(tau + 1) * NE]),
      reads=[b_WeT_tau[tau]], writes=[b_Webf])
    E("tensor", lambda X: X.transpose(pT[0:NE, 0:128], Webf[:], ident[:]),
      reads=[b_Webf, b_ident], writes=[b_pT])
    E("scalar", lambda X: X.activation(WeTs[0:NE, :], pT[0:NE, 0:128], AF.Identity),
      reads=[b_pT], writes=[b_WeTf])
    def f(X):
        for nb_ in range(2):
            ins = X.matmul(pG[:, nb_ * 512:(nb_ + 1) * 512],
                           WeTs[0:NE, :],
                           b2bf[0:NE, nb_ * 512:(nb_ + 1) * 512],
                           start=True, stop=True)
        return ins
    E("tensor", f, reads=[b_WeTf, b_b2], writes=[b_pG])
    E("vector", lambda X, tau=tau: X.tensor_add(accS[:, tau * D:(tau + 1) * D],
                                                accS[:, tau * D:(tau + 1) * D], pG[:, :]),
      reads=[b_accT[tau], b_pG], writes=[b_pG, b_accT[tau]])


def _retention(nc, e, kb, cfg, ap, q_s, b_qs, k_s, b_ks, v_s, b_vs, g_s, b_gs,
               ident, b_ident, Sf, b_Sf, accS, b_accT, cst, b_cst,
               ln2T, b_ln2T, We, b_WeT_tau, WeT, b_WeTf):
    E = kb.emit
    NB = 2
    TB = cfg.ntau * 128
    wo = e(nc.sbuf_tensor("woS", [128, 16 * D], BF16)); b_wo = Buf()
    E("gpsimd", lambda X: X.dma_start(out=wo[:].rearrange("p (a n) -> p a n", a=16),
                                      in_=ap["L0_attn_Wo"].rearrange("(a p) n -> p a n", p=128)),
      writes=[b_wo], kind="d")
    dq128 = e(nc.sbuf_tensor("dq128S", [128, H], F32)); b_dq = Buf()
    dkv128 = e(nc.sbuf_tensor("dkv128S", [128, H], F32)); b_dkv = Buf()
    lamH = e(nc.sbuf_tensor("lamHS", [128, H], F32)); b_lam = Buf()
    dmat2 = e(nc.sbuf_tensor("dmat2S", [128, H * 64], BF16)); b_dm = Buf()
    gncol = e(nc.sbuf_tensor("gncolS", [128, 1], F32)); b_gnc = Buf()
    gwT = e(nc.sbuf_tensor("f_gwT", [128, 8 * NE], BF16)); b_gwT = Buf()
    gb = e(nc.sbuf_tensor("f_gb", [1, NE], BF16)); b_gb = Buf()
    onesb = e(nc.sbuf_tensor("f_ones", [1, 128], BF16)); b_ones = Buf()
    b2bf = e(nc.sbuf_tensor("f_b2", [8, D], BF16)); b_b2 = Buf()
    for dst, bf, nm in ((dq128, b_dq, "dq128"), (dkv128, b_dkv, "dkv128"),
                        (lamH, b_lam, "lamH"), (dmat2, b_dm, "dmat2"), (gncol, b_gnc, "gncol")):
        E("sync", lambda X, d=dst, n=nm: X.dma_start(out=d[:], in_=ap[n]),
          writes=[bf], kind="d")
    E("scalar", lambda X: X.activation(wo[:], wo[:], AF.Identity, scale=gncol[:]),
      reads=[b_wo, b_gnc], writes=[b_wo])
    E("gpsimd", lambda X: X.dma_start(out=gwT[:].rearrange("p (a n) -> p a n", a=8),
                                      in_=ap["gate_wT"].rearrange("(a p) n -> p a n", p=128)),
      writes=[b_gwT], kind="d")
    E("gpsimd", lambda X: X.dma_start(out=gb[:], in_=ap["gate_b"]), writes=[b_gb], kind="d")
    E("sync", lambda X: X.dma_start(out=onesb[:], in_=ap["onesb"]), writes=[b_ones], kind="d")
    E("gpsimd", lambda X: X.dma_start(out=b2bf[:], in_=ap["b2"]), writes=[b_b2], kind="d")

    qtk = [e(nc.sbuf_tensor(f"r_q{i}", [128, D], BF16)) for i in range(NB)]
    b_qtk = [Buf() for _ in range(NB)]
    qdq, b_qdq = qtk, b_qtk
    rln2A = e(nc.sbuf_tensor("r_ln2A", [128, D], BF16)); b_rln2A = Buf()
    ktk = [e(nc.sbuf_tensor(f"r_k{i}", [128, D], BF16)) for i in range(2)]
    b_ktk = [Buf() for _ in range(2)]
    kch = e(nc.sbuf_tensor("r_kc", [128, D], BF16)); b_kch = Buf()
    kdc, b_kdc = kch, b_kch
    grc = e(nc.sbuf_tensor("r_g", [128, 2 * D], F32)); b_grc = Buf()
    kTs = e(nc.sbuf_tensor("r_kT", [128, H * 128], BF16)); b_kTs = Buf()
    qTs = e(nc.sbuf_tensor("r_qT", [128, H * 128], BF16)); b_qTs = Buf()
    LT = [e(nc.sbuf_tensor(f"r_LT{i}", [128, H * 64], BF16)) for i in range(2)]
    b_LT = [Buf() for _ in range(2)]
    RT = [e(nc.sbuf_tensor(f"r_RT{i}", [128, H * DV], BF16)) for i in range(2)]
    b_RT = [Buf() for _ in range(2)]
    osb = e(nc.sbuf_tensor("r_o", [128, H * DV], F32)); b_osb = [Buf(), Buf()]
    osq = e(nc.sbuf_tensor("r_osq", [128, H * DV], F32)); b_osq = [Buf(), Buf()]
    o4_one = e(nc.sbuf_tensor("r_o4", [128, H * DV], BF16))
    o4 = [o4_one, o4_one]
    b_o4h = [Buf(), Buf()]
    o3T = e(nc.sbuf_tensor("r_o3T", [128, 16 * 128], BF16))
    b_o3T = [Buf() for _ in range(4)]
    rstt = e(nc.sbuf_tensor("r_rst", [128, 64], F32))
    b_rstt = [[Buf(), Buf()], [Buf(), Buf()]]
    ssq = e(nc.sbuf_tensor("r_ssq", [128, 2 * H], F32))
    ln2, b_ln2 = [rln2A, rln2A], [b_rln2A, b_rln2A]
    stats2 = rstt
    b_stats2 = [Buf() for _ in range(NB)]
    gts = e(nc.sbuf_tensor("f_gt", [128, NE], F32)); b_gts = Buf()
    msk = e(nc.sbuf_tensor("f_msk", [128, NE], F32)); b_msk = Buf()
    b_m12 = Buf()
    eqs = e(nc.sbuf_tensor("f_eq", [128, 2 * NE], F32)); b_eqs = Buf()
    Webf = e(nc.sbuf_tensor("f_Webf", [128, NE], BF16)); b_Webf = Buf()
    WeTs = e(nc.sbuf_tensor("f_WeTs", [8, 128], BF16))

    pTr = e(nc.psum_tensor("pTr", [128, 2048], BF16)); b_pTr = Buf("pTr")
    pA = e(nc.psum_tensor("pA", [128, 1024], F32)); b_pA = Buf("pA")
    pO = e(nc.psum_tensor("pO", [128, 8 * DV], F32))
    b_pOo = Buf("pOo")
    b_pOkv = Buf("pOkv")
    pO3T = e(nc.psum_tensor("pO3T", [128, 2048], BF16))
    b_pO3T = [Buf("pO3T0"), Buf("pO3T1")]

    PUNITS = 0 * min(64, max(0, 2 * (cfg.ntau - 5) - 1) // 2 * 2)
    if PUNITS:
        b1Li = e(nc.sbuf_tensor("i_b1L", [128, NE * 32], F32)); b_b1i = Buf()
        w18 = [e(nc.sbuf_tensor(f"i_w1{i}", [128, 8 * 512], BF16)) for i in range(2)]
        b_w18 = [Buf() for _ in range(2)]
        w28 = [e(nc.sbuf_tensor(f"i_w2{i}", [128, 4 * 1024], BF16)) for i in range(2)]
        b_w28 = [Buf() for _ in range(2)]
        hTi = e(nc.sbuf_tensor("i_hT", [128, 4 * 512], BF16)); b_hTi = Buf()
        pHi = e(nc.psum_tensor("pHi", [128, 512], F32)); b_pHi = Buf("pHi")
        pXi = e(nc.psum_tensor("pXi", [128, 512], F32)); b_pXi = Buf("pXi")
        E("sync", lambda X: X.dma_start(out=b1Li[:], in_=ap["b1L"]), writes=[b_b1i], kind="d")
    TBr = cfg.ntau * 128
    pst = {"load": 0, "comp": 0}

    def emit_load(u):
        wb = u % 2
        ex, h8 = u // 8, u % 8
        E("gpsimd", lambda X, ex=ex, h8=h8, wb=wb: X.dma_start(
            out=w18[wb][:].rearrange("p (a n) -> p a n", a=8),
            in_=ap["w1"][ex, :, h8 * 512:(h8 + 1) * 512]
            .rearrange("(a p) n -> p a n", p=128)),
          writes=[b_w18[wb]], kind="d")
        E("gpsimd", lambda X, ex=ex, h8=h8, wb=wb: X.dma_start(
            out=w28[wb][:].rearrange("p (a n) -> p a n", a=4),
            in_=ap["w2"][ex, h8 * 512:(h8 + 1) * 512, :]
            .rearrange("(a p) n -> p a n", p=128)),
          writes=[b_w28[wb]], kind="d")

    def emit_comp(u):
        wb = u % 2
        ex, h8 = u // 8, u % 8
        for ht in range(4):
            def f(X, wb=wb, ht=ht):
                for kt in range(8):
                    ins = X.matmul(pHi[:, :],
                                   w18[wb][:, kt * 512 + ht * 128: kt * 512 + (ht + 1) * 128],
                                   ln2T[:, kt * TBr: kt * TBr + 512],
                                   start=(kt == 0), stop=(kt == 7))
                return ins
            E("tensor", f, reads=[b_w18[wb], b_ln2T[0]], writes=[b_pHi])
            def f(X, ex=ex, h8=h8, ht=ht):
                bcol = ex * 32 + h8 * 4 + ht
                return X.activation(hTi[:, ht * 512:(ht + 1) * 512], pHi[:, :], AF.Gelu,
                                    bias=b1Li[:, bcol:bcol + 1])
            E("scalar", f, reads=[b_pHi, b_b1i], writes=[b_hTi])
        for m in range(4):
            for nb_ in range(2):
                def f(X, wb=wb, m=m, nb_=nb_):
                    for kt in range(4):
                        ins = X.matmul(pXi[:, :],
                                       hTi[:, kt * 512 + m * 128: kt * 512 + (m + 1) * 128],
                                       w28[wb][:, kt * 1024 + nb_ * 512: kt * 1024 + (nb_ + 1) * 512],
                                       start=(kt == 0), stop=(kt == 3))
                    return ins
                E("tensor", f, reads=[b_hTi, b_w28[wb]], writes=[b_pXi])
                sl = slice(m * D + nb_ * 512, m * D + (nb_ + 1) * 512)
                E("vector", lambda X, sl=sl, m=m, ex=ex: X.scalar_tensor_tensor(
                    accS[:, sl], pXi[:, :], We[:, m * NE + ex: m * NE + ex + 1],
                    accS[:, sl], op0=ALU.mult, op1=ALU.add),
                  reads=[b_pXi, b_WeT_tau[m], b_accT[m]], writes=[b_accT[m]])

    def pump():
        if pst["comp"] < pst["load"] and pst["comp"] < PUNITS:
            emit_comp(pst["comp"]); pst["comp"] += 1
        if pst["load"] < PUNITS and pst["load"] - pst["comp"] < 2:
            emit_load(pst["load"]); pst["load"] += 1


    def ld(dst, src, bw, br=()):
        E("sync", lambda X, d=dst, s=src: X.dma_start(out=d, in_=s),
          reads=list(br), writes=list(bw), kind="d")

    b_m12l = [b_m12, Buf()]
    envA = dict(TB=TB, accS=accS, b_accT=b_accT, ln2T=ln2T, b_ln2T=b_ln2T,
                We=We, b_WeT_tau=b_WeT_tau, ident=ident, b_ident=b_ident,
                cst=cst, b_cst=b_cst, st=rstt, b_st=b_stats2,
                ln2=ln2, b_ln2=b_ln2,
                dmp=[osq[:, 0:1024], osq[:, 1024:2048]], b_dmp=b_osq,
                pT=pTr, b_pT=b_pTr, pG=pA, b_pG=b_pA,
                gts=gts, b_gts=b_gts, msk=msk, b_msk=b_msk, eqs=eqs, b_eqs=b_eqs,
                Webf=Webf, b_Webf=b_Webf, WeTs=WeTs, b_WeTf=b_WeTf,
                gwT=gwT, b_gwT=b_gwT, gb=gb, b_gb=b_gb, onesb=onesb, b_ones=b_ones,
                b2bf=b2bf, b_b2=b_b2, b_m12=b_m12l)

    def p2_router(tau, buf):
        _router(kb, envA, tau, buf)

    def tau_loads(tau):
        buf = tau % NB
        ld(qtk[buf][:], q_s[tau * 128:(tau + 1) * 128, :], [b_qtk[buf]], [b_qs])
        ld(ktk[buf][:], k_s[tau * 128:(tau + 1) * 128, :], [b_ktk[buf]], [b_ks])
        ld(accS[:, tau * D:(tau + 1) * D].rearrange("p (o d) -> p o d", o=1),
           ap["x_loc"][tau * 128:(tau + 1) * 128, :].rearrange("(o t) d -> t o d", o=1),
           [b_accT[tau]])

        def f(X, buf=buf):
            d3 = dq128[:].unsqueeze(2).broadcast_to((128, H, 64))
            q3 = qtk[buf][:].rearrange("p (h d) -> p h d", h=H)
            return X.tensor_tensor(q3, q3, d3, ALU.mult)
        E("vector", f, reads=[b_qtk[buf], b_dq], writes=[b_qdq[buf]])

    tau_loads(0)
    for tau in range(cfg.ntau):
        buf = tau % NB
        if tau + 1 < cfg.ntau:
            tau_loads(tau + 1)

        def fk(X, buf=buf):
            for h in range(H):
                ins = X.transpose(pTr[0:64, h * 128:(h + 1) * 128],
                                  ktk[buf][:, h * 64:(h + 1) * 64], ident[:])
            return ins
        def fq(X, buf=buf):
            for h in range(H):
                ins = X.transpose(pTr[0:64, h * 128:(h + 1) * 128],
                                  qdq[buf][:, h * 64:(h + 1) * 64], ident[:])
            return ins
        def fqup(X, buf=buf):
            for h in range(H):
                ins = X.transpose(pTr[64:128, h * 128:(h + 1) * 128],
                                  qdq[buf][:, h * 64:(h + 1) * 64], ident[:],
                                  tile_position=(0, 64))
            return ins
        E("tensor", lambda X, buf=buf: (fk(X, buf), fqup(X, buf))[-1],
          reads=[b_ktk[buf], b_qdq[buf], b_ident], writes=[b_pTr])
        E("scalar", lambda X: X.activation(kTs[0:64, :], pTr[0:64, 0:H * 128], AF.Identity),
          reads=[b_pTr], writes=[b_kTs])
        for ci_ in range(2):
            def f(X, ci_=ci_):
                src = pTr[64:128, 0:H * 128].rearrange("p (h t) -> p h t", h=H)
                dst = LT[ci_][64:128, :].rearrange("p (h i) -> p h i", h=H)
                return X.activation(dst, src[:, :, ci_ * 64:(ci_ + 1) * 64], AF.Identity)
            E("scalar", f, reads=[b_pTr], writes=[b_LT[ci_]])
        E("tensor", fq, reads=[b_qdq[buf], b_ident], writes=[b_pTr])
        E("scalar", lambda X: X.activation(qTs[0:64, :], pTr[0:64, 0:H * 128], AF.Identity),
          reads=[b_pTr], writes=[b_qTs])

        for ci in range(2):
            ch = 2 * tau + ci
            ld(kch[0:64, :], k_s[ch * 64:(ch + 1) * 64, :], [b_kch], [b_ks])
            ld(RT[ci][0:64, :], v_s[ch * 64:(ch + 1) * 64, :], [b_RT[ci]], [b_vs])
            ld(grc[0:64, :], g_s[ch * 64:(ch + 1) * 64, :], [b_grc], [b_gs])
            E("scalar", lambda X: X.activation(grc[0:64, :], grc[0:64, :], AF.Silu),
              reads=[b_grc], writes=[b_grc])
            E("scalar", lambda X, ci=ci: X.activation(RT[ci][64:128, :], Sf[64:128, :], AF.Identity),
              reads=[b_Sf], writes=[b_RT[ci]])
            def f(X):
                S3 = Sf[64:128, :].rearrange("p (h v) -> p h v", h=H)
                l3 = lamH[64:128, :].unsqueeze(2).broadcast_to((64, H, DV))
                return X.tensor_tensor(S3, S3, l3, ALU.mult)
            E("vector", f, reads=[b_Sf, b_lam], writes=[b_Sf])
            def f(X, ci=ci):
                d3 = dkv128[0:64, :].unsqueeze(2).broadcast_to((64, H, 64))
                k3 = kch[0:64, :].rearrange("p (h d) -> p h d", h=H)
                return X.tensor_tensor(k3, k3, d3, ALU.mult)
            E("vector", f, reads=[b_kch, b_dkv], writes=[b_kdc])
            def f(X, ci=ci):
                for h in range(H):
                    ins = X.matmul(pA[0:64, h * 64:(h + 1) * 64],
                                   kTs[0:64, h * 128 + ci * 64: h * 128 + (ci + 1) * 64],
                                   qTs[0:64, h * 128 + ci * 64: h * 128 + (ci + 1) * 64],
                                   start=(h % 8 == 0), stop=(h % 8 == 7))
                return ins
            E("tensor", f, reads=[b_kTs, b_qTs], writes=[b_pA])
            E("vector", lambda X, ci=ci: X.tensor_tensor(LT[ci][0:64, :], pA[0:64, :],
                                                         dmat2[0:64, :], ALU.mult),
              reads=[b_pA, b_dm], writes=[b_LT[ci]])
            for hh in range(2):
                hs = range(hh * 8, hh * 8 + 8)
                def f(X, ci=ci, hs=hs):
                    for h in hs:
                        ins = X.matmul(pO[0:64, (h % 8) * DV:(h % 8 + 1) * DV],
                                       LT[ci][:, h * 64:(h + 1) * 64],
                                       RT[ci][:, h * DV:(h + 1) * DV],
                                       start=(h % 4 == 0), stop=(h % 4 == 3))
                    return ins
                E("tensor", f, reads=[b_LT[ci], b_RT[ci]], writes=[b_pOo])
                cols = slice(hh * 1024, (hh + 1) * 1024)
                E("scalar", lambda X, cols=cols: X.activation(osb[0:64, cols], pO[0:64, 0:1024],
                                                              AF.Identity),
                  reads=[b_pOo], writes=[b_osb[hh]])
                E("scalar", lambda X, cols=cols: X.activation(osq[0:64, cols], pO[0:64, 0:1024],
                                                              AF.Square),
                  reads=[b_pOo], writes=[b_osq[hh]])
                def f(X, ci=ci, hs=hs):
                    for h in hs:
                        ins = X.matmul(pO[64:128, (h % 8) * DV:(h % 8 + 1) * DV],
                                       kdc[0:64, h * 64:(h + 1) * 64],
                                       RT[ci][0:64, h * DV:(h + 1) * DV],
                                       start=(h % 4 == 0), stop=(h % 4 == 3),
                                       tile_position=(0, 64))
                    return ins
                E("tensor", f, reads=[b_kdc, b_RT[ci]], writes=[b_pOkv])
                E("vector", lambda X, cols=cols: X.tensor_add(Sf[64:128, cols],
                                                              Sf[64:128, cols], pO[64:128, 0:1024]),
                  reads=[b_Sf, b_pOkv], writes=[b_Sf])
            for hh in range(2):
                cols = slice(hh * 1024, (hh + 1) * 1024)
                sq = ssq[:, ci * H + hh * 8: ci * H + (hh + 1) * 8]
                rc = rstt[:, ci * 32 + hh * 8: ci * 32 + (hh + 1) * 8]
                E("vector", lambda X, sq=sq, cols=cols: X.tensor_reduce(
                    sq[0:64, :], osq[0:64, cols].rearrange("p (h v) -> p h v", h=8),
                    AX.X, ALU.add),
                  reads=[b_osq[hh]], writes=[b_rstt[ci][hh]])
                E("scalar", lambda X, sq=sq, rc=rc: X.activation(
                    rc[0:64, :], sq[0:64, :], AF.Sqrt, bias=cst[0:64, 0:1], scale=cst[0:64, 1:2]),
                  reads=[b_rstt[ci][hh], b_cst], writes=[b_rstt[ci][hh]])
                E("vector", lambda X, rc=rc: X.reciprocal(rc[0:64, :], rc[0:64, :]),
                  reads=[b_rstt[ci][hh]], writes=[b_rstt[ci][hh]])
                def f(X, rc=rc, cols=cols):
                    o3 = osb[0:64, cols].rearrange("p (h v) -> p h v", h=8)
                    r3 = rc[0:64, :].unsqueeze(2).broadcast_to((64, 8, DV))
                    return X.tensor_tensor(o3, o3, r3, ALU.mult)
                E("vector", f, reads=[b_osb[hh], b_rstt[ci][hh]], writes=[b_osb[hh]])
                E("vector", lambda X, ci=ci, cols=cols: X.tensor_tensor(
                    o4[ci][0:64, cols], osb[0:64, cols], grc[0:64, cols], ALU.mult),
                  reads=[b_osb[hh], b_grc], writes=[b_o4h[hh]])
                def f(X, ci=ci, hh=hh):
                    for b in range(8):
                        ins = X.transpose(pO3T[:, ci * 1024 + (hh * 8 + b) * 64:
                                               ci * 1024 + (hh * 8 + b + 1) * 64],
                                          o4[ci][0:64, hh * 1024 + b * 128:
                                                 hh * 1024 + (b + 1) * 128],
                                          ident[0:64, 0:64])
                    return ins
                E("tensor", f, reads=[b_o4h[hh], b_ident], writes=[b_pO3T[ci]])
                def f(X, ci=ci, hh=hh):
                    dst = o3T[:, :].rearrange("p (b t) -> p b t", b=16)
                    src = pO3T[:, ci * 1024 + hh * 512: ci * 1024 + (hh + 1) * 512]                         .rearrange("p (b t) -> p b t", b=8)
                    return X.activation(dst[:, hh * 8:(hh + 1) * 8, ci * 64:(ci + 1) * 64],
                                        src, AF.Identity)
                E("scalar", f, reads=[b_pO3T[ci]], writes=[b_o3T[ci * 2 + hh]])
            if tau >= 5 and ci == 0:
                pump()

        def f(X):
            for kt in range(16):
                for nb_ in range(2):
                    ins = X.matmul(pA[:, nb_ * 512:(nb_ + 1) * 512],
                                   o3T[:, kt * 128:(kt + 1) * 128],
                                   wo[:, kt * D + nb_ * 512: kt * D + (nb_ + 1) * 512],
                                   start=(kt == 0), stop=(kt == 15))
            return ins
        E("tensor", f, reads=b_o3T + [b_wo], writes=[b_pA])
        E("vector", lambda X, tau=tau: X.tensor_add(
            accS[:, tau * D:(tau + 1) * D], accS[:, tau * D:(tau + 1) * D], pA[:, :]),
          reads=[b_accT[tau], b_pA], writes=[b_accT[tau]])
        if tau >= 5:
            pump()
        if tau >= 1:
            p2_router(tau - 1, (tau - 1) % NB)
    p2_router(cfg.ntau - 1, (cfg.ntau - 1) % NB)
    while pst["comp"] < PUNITS:
        pump()
    return PUNITS


def _ffn(nc, e, kb, cfg, ap, accS, b_accT, ident, b_ident, We, b_WeT_tau,
         WeT, b_WeTf, ln2T, b_ln2T, cst, b_cst, punits=0):
    E = kb.emit
    TB = cfg.ntau * 128
    b1L = e(nc.sbuf_tensor("f_b1L", [128, NE * 32], F32)); b_b1 = Buf()
    E("sync", lambda X: X.dma_start(out=b1L[:], in_=ap["b1L"]), writes=[b_b1], kind="d")

    w1q = [e(nc.sbuf_tensor(f"f_w1q{i}", [128, 8 * 1024], BF16)) for i in range(2)]
    b_w1q = [Buf() for _ in range(2)]
    w2q = [e(nc.sbuf_tensor(f"f_w2q{i}", [128, 8 * 1024], BF16)) for i in range(2)]
    b_w2q = [Buf() for _ in range(2)]
    hT = [e(nc.sbuf_tensor(f"f_hT{i}", [128, 8 * 512], BF16)) for i in range(2)]
    b_hT = [Buf() for _ in range(2)]
    pH = [e(nc.psum_tensor(f"f_pH{i}", [128, 512], F32)) for i in range(2)]
    b_pH = [Buf() for _ in range(2)]
    pX = [e(nc.psum_tensor(f"f_pX{i}", [128, 512], F32)) for i in range(2)]
    b_pX = [Buf() for _ in range(2)]

    nextr = cfg.ntau
    envB = None
    if False:
        rln2 = [e(nc.sbuf_tensor(f"rb_ln2{i}", [128, D], BF16)) for i in range(2)]
        rst = e(nc.sbuf_tensor("rb_st", [128, 64], F32))
        rdmp = e(nc.sbuf_tensor("rb_dmp", [128, 2048], F32))
        rgts = e(nc.sbuf_tensor("rb_gt", [128, NE], F32))
        rmsk = e(nc.sbuf_tensor("rb_msk", [128, NE], F32))
        reqs = e(nc.sbuf_tensor("rb_eq", [128, 2 * NE], F32))
        rWebf = e(nc.sbuf_tensor("rb_Webf", [128, NE], BF16))
        rWeTs = e(nc.sbuf_tensor("rb_WeTs", [8, 128], BF16))
        rgwT = e(nc.sbuf_tensor("rb_gwT", [128, 8 * NE], BF16)); b_rgwT = Buf()
        rgb = e(nc.sbuf_tensor("rb_gb", [1, NE], BF16)); b_rgb = Buf()
        rones = e(nc.sbuf_tensor("rb_ones", [1, 128], BF16)); b_rones = Buf()
        rb2bf = e(nc.sbuf_tensor("rb_b2", [8, D], BF16)); b_rb2 = Buf()
        pRT = e(nc.psum_tensor("pRT", [128, 1024], BF16))
        pG2 = e(nc.psum_tensor("pG2", [128, 1024], F32))
        E("gpsimd", lambda X: X.dma_start(out=rgwT[:].rearrange("p (a n) -> p a n", a=8),
                                          in_=ap["gate_wT"].rearrange("(a p) n -> p a n", p=128)),
          writes=[b_rgwT], kind="d")
        E("gpsimd", lambda X: X.dma_start(out=rgb[:], in_=ap["gate_b"]), writes=[b_rgb], kind="d")
        E("sync", lambda X: X.dma_start(out=rones[:], in_=ap["onesb"]), writes=[b_rones], kind="d")
        E("gpsimd", lambda X: X.dma_start(out=rb2bf[:], in_=ap["b2"]), writes=[b_rb2], kind="d")
        envB = dict(TB=TB, accS=accS, b_accT=b_accT, ln2T=ln2T, b_ln2T=b_ln2T,
                    We=We, b_WeT_tau=b_WeT_tau, ident=ident, b_ident=b_ident,
                    cst=cst, b_cst=b_cst, st=rst, b_st=[Buf(), Buf()],
                    ln2=rln2, b_ln2=[Buf(), Buf()],
                    dmp=[rdmp[:, 0:1024], rdmp[:, 1024:2048]], b_dmp=[Buf(), Buf()],
                    pT=pRT, b_pT=Buf("pRT"), pG=pG2, b_pG=Buf("pG2"),
                    gts=rgts, b_gts=Buf(), msk=rmsk, b_msk=Buf(), eqs=reqs, b_eqs=Buf(),
                    Webf=rWebf, b_Webf=Buf(), WeTs=rWeTs, b_WeTf=Buf(),
                    gwT=rgwT, b_gwT=b_rgwT, gb=rgb, b_gb=b_rgb, onesb=rones, b_ones=b_rones,
                    b2bf=rb2bf, b_b2=b_rb2, b_m12=[Buf(), Buf()])

    it = 0
    for ex in range(NE):
        for hq in range(4):
            wb = it % 2
            E("gpsimd", lambda X, ex=ex, hq=hq, wb=wb: X.dma_start(
                out=w1q[wb][:].rearrange("p (a n) -> p a n", a=8),
                in_=ap["w1"][ex, :, hq * 1024:(hq + 1) * 1024]
                .rearrange("(a p) n -> p a n", p=128)),
              writes=[b_w1q[wb]], kind="d")
            E("gpsimd", lambda X, ex=ex, hq=hq, wb=wb: X.dma_start(
                out=w2q[wb][:].rearrange("p (a n) -> p a n", a=8),
                in_=ap["w2"][ex, hq * 1024:(hq + 1) * 1024, :]
                .rearrange("(a p) n -> p a n", p=128)),
              writes=[b_w2q[wb]], kind="d")
            covered = 8 * ex + 2 * hq + 1 < punits
            for tg in range(cfg.ntg):
                while nextr < min(4 * (tg + 1), cfg.ntau):
                    _router(kb, envB, nextr, nextr % 2)
                    nextr += 1
                if tg == 0 and covered:
                    continue
                tgb = tg % 2
                for ht in range(8):
                    hb = ht % 2
                    def f(X, wb=wb, tg=tg, ht=ht, hb=hb):
                        for kt in range(8):
                            ins = X.matmul(pH[hb][:, :],
                                           w1q[wb][:, kt * 1024 + ht * 128: kt * 1024 + (ht + 1) * 128],
                                           ln2T[:, kt * TB + tg * 512: kt * TB + (tg + 1) * 512],
                                           start=(kt == 0), stop=(kt == 7))
                        return ins
                    E("tensor", f, reads=[b_w1q[wb], b_ln2T[tg]], writes=[b_pH[hb]])
                    def f(X, ex=ex, hq=hq, ht=ht, hb=hb, tgb=tgb):
                        bcol = ex * 32 + hq * 8 + ht
                        return X.activation(hT[tgb][:, ht * 512:(ht + 1) * 512],
                                            pH[hb][:, :], AF.Gelu,
                                            bias=b1L[:, bcol:bcol + 1])
                    E("scalar", f, reads=[b_pH[hb], b_b1], writes=[b_hT[tgb]])
                for m in range(4):
                    tau = tg * 4 + m
                    for nb_ in range(2):
                        def f(X, wb=wb, m=m, nb_=nb_, tgb=tgb):
                            for kt in range(8):
                                ins = X.matmul(pX[nb_][:, :],
                                               hT[tgb][:, kt * 512 + m * 128: kt * 512 + (m + 1) * 128],
                                               w2q[wb][:, kt * 1024 + nb_ * 512: kt * 1024 + (nb_ + 1) * 512],
                                               start=(kt == 0), stop=(kt == 7))
                            return ins
                        E("tensor", f, reads=[b_hT[tgb], b_w2q[wb]], writes=[b_pX[nb_]])
                        sl = slice(tau * D + nb_ * 512, tau * D + (nb_ + 1) * 512)
                        E("vector", lambda X, sl=sl, tau=tau, ex=ex, nb_=nb_: X.scalar_tensor_tensor(
                            accS[:, sl], pX[nb_][:, :],
                            We[:, tau * NE + ex: tau * NE + ex + 1],
                            accS[:, sl], op0=ALU.mult, op1=ALU.add),
                          reads=[b_pX[nb_], b_WeT_tau[tau], b_accT[tau]], writes=[b_accT[tau]])
            it += 1


def kernel(**inputs):
    cfg = CFG()
    nc = build_nc(cfg)
    in_maps = [_per_core_inputs(c, cfg, inputs) for c in range(NCORE)]
    res = run_bass_kernel_spmd(nc, in_maps, core_ids=list(range(NCORE)))
    outs = [np.asarray(res.results[c]["out"], dtype=np.float32) for c in range(NCORE)]
    return np.concatenate(outs, axis=0).reshape(1, T, D)
